# version 07
# baseline (speedup 1.0000x reference)
import contextlib
import numpy as np

import concourse.bass as bass
import concourse.mybir as mybir
from concourse.bass_utils import run_bass_kernel_spmd

F32 = mybir.dt.float32
BF16 = mybir.dt.bfloat16
I32 = mybir.dt.int32
AF = mybir.ActivationFunctionType
ALU = mybir.AluOpType
AX = mybir.AxisListType

NCORES = 8
RG = [[0, 1, 2, 3, 4, 5, 6, 7]]
TOK = 1024
SEQ = 8192
D = 2048
NDT = D // 128
H = 16
HD = 128
NCH = TOK // 128
INTER = 5504
NOC = INTER // 128
GHR = D // NCORES
DHR = INTER // NCORES
DQKV = 6144
FF = 8192
NFC = FF // 128
NE = 8
RK = 16
EPS = 1e-5
LUTN = 16
INV_SQRT_HD = float(1.0 / np.sqrt(128.0))

ENGS = ("sp", "act", "pe", "dve", "pool")


_NAMES = {}


def _uniq(name):
    n = _NAMES.get(name, 0)
    _NAMES[name] = n + 1
    return name if n == 0 else f"{name}_{n}"


class Scope:
    def __init__(self, k):
        self.k = k
        self.es = contextlib.ExitStack()

    def sbuf(self, name, shape, dt):
        return self.es.enter_context(self.k.nc.sbuf_tensor(_uniq(name), shape, dt))

    def psum(self, name, shape, dt=F32):
        return self.es.enter_context(self.k.nc.psum_tensor(_uniq(name), shape, dt))

    def close(self):
        self.es.close()
        self.k.fence = [(n, s[1]) for n, s in self.k.sems.items()]


class K:

    def __init__(self, nc, es):
        _NAMES.clear()
        self.nc = nc
        self.es = es
        self.fence = []
        self.ops = {e: [] for e in ENGS}
        self.sems = {}
        self.lastwait = {e: {} for e in ENGS}
        self.pending = {e: {} for e in ENGS}

    def scope(self):
        return Scope(self)

    def sem(self, name):
        if name not in self.sems:
            h = self.es.enter_context(self.nc.semaphore(name))
            self.sems[name] = [h, 0]
        return self.sems[name]

    def wait(self, eng, deps):
        p = self.pending[eng]
        for d in deps:
            if d is None:
                continue
            semname, val = d
            if val > p.get(semname, 0):
                p[semname] = val

    def _take_waits(self, eng):
        out = []
        lw = self.lastwait[eng]
        for semname, val in self.pending[eng].items():
            if val <= 0 or lw.get(semname, 0) >= val:
                continue
            lw[semname] = val
            h = self.sem(semname)[0]
            out.append((h.num, h.name, val))
        self.pending[eng] = {}
        return out

    def flush_waits(self, eng):
        for num, name, val in self._take_waits(eng):
            h = self.sems[name][0]
            self.ops[eng].append(lambda q, h=h, v=val: q.wait_ge(h, v))

    def _do(self, eng, semname, inc, deps, fn, attach=True):
        self.wait(eng, list(deps) + self.fence)
        if not attach:
            self.flush_waits(eng)
            waits = []
        else:
            waits = self._take_waits(eng)
            for num, name, v in waits[:-1]:
                h = self.sems[name][0]
                self.ops[eng].append(lambda q, h=h, v=v: q.wait_ge(h, v))
            waits = waits[-1:]
        s = self.sem(semname)
        s[1] += inc
        val = s[1]

        def run(q, fn=fn, h=s[0], kk=inc, waits=waits):
            bi = fn(q)
            if waits:
                si = mybir.ensure_sync_info(bi.ins)
                for num, name, v in waits:
                    si.on_wait.append(mybir.SyncWait(
                        sync_type="semaphore", id=num, ant_name=name,
                        wait_mode="sem-ge-imm", wait_value=v))
            bi.then_inc(h, kk)

        self.ops[eng].append(run)
        return (semname, val)

    def dma(self, eng, out, in_, deps=(), sem=None, **kw):
        if sem is None:
            t = getattr(out, "tensor", None)
            sem = "d_" + (t.name if t is not None else "x")
        return self._do(eng, sem, 16, deps,
                        lambda q: q.dma_start(out=out, in_=in_, **kw), attach=False)

    def op(self, eng, method, *args, deps=(), **kw):
        deps = list(deps)
        if eng in ("dve", "act", "pool"):
            deps.append(self.tok(eng))
        return self._do(eng, "c_" + eng, 1, deps,
                        lambda q: getattr(q, method)(*args, **kw))

    def cc(self, kind, op, ins, outs, deps=()):
        return self._do("pool", "cc", 1, deps,
                        lambda q: q.collective_compute(kind, op, RG, ins=ins, outs=outs),
                        attach=False)

    def tok(self, eng):
        name = "c_" + eng
        return (name, self.sems[name][1] if name in self.sems else 0)

    def dmatok(self, eng):
        name = "dma_" + eng
        return (name, self.sems[name][1] if name in self.sems else 0)

    def barrier_tokens(self):
        return [self.tok(e) for e in ("pe", "dve", "act")] + \
               [self.dmatok(e) for e in ("sp", "act", "pool")] + [self.tok("pool")]

    def emit_all(self, block):
        for eng in ENGS:
            self.flush_waits(eng)
        for eng, attr in (("sp", "sync"), ("act", "scalar"), ("pe", "tensor"),
                          ("dve", "vector"), ("pool", "gpsimd")):
            ops = self.ops[eng]
            if not ops:
                continue

            def body(q, ops=ops):
                for f in ops:
                    f(q)

            getattr(block, attr)(body)



def host_constants():
    import ml_dtypes
    bf16 = ml_dtypes.bfloat16
    c = {}
    c["ident_b"] = np.eye(128).astype(bf16)
    c["ident_f"] = np.eye(128, dtype=np.float32)
    c["ones_col_b"] = np.ones((128, 1), dtype=bf16)
    c["ones_row_f"] = np.ones((1, 128), dtype=np.float32)
    i = np.arange(128)
    diff = (i[None, :] - i[:, None]).astype(np.float64)
    c["diffmaskT"] = np.where(diff >= 0, diff, 1e6).astype(np.float32)
    c["c127"] = np.broadcast_to((127.0 - i)[None, :], (128, 128)).astype(np.float32).copy()
    c["colp1_1024"] = np.broadcast_to((np.arange(1024) + 1.0)[None, :], (128, 1024)).astype(np.float32).copy()
    c["triT_b"] = (i[:, None] <= i[None, :]).astype(bf16)
    c["neg1_8"] = np.full((128, 8), -1.0, dtype=np.float32)
    sel = np.zeros((8, 8 * 128), dtype=np.float64)
    for e in range(8):
        sel[e, e * 128:(e + 1) * 128] = 1.0
    c["sel8_b"] = sel.astype(bf16)
    lm = np.zeros((8, 128, 8, 128), dtype=np.float64)
    tri = (i[:, None] <= i[None, :])
    for kb in range(8):
        for qb in range(8):
            if qb > kb:
                lm[kb, :, qb, :] = 1.0
            elif qb == kb:
                lm[kb, :, qb, :] = tri
    c["lmask_b"] = lm.reshape(1024, 1024).astype(bf16)
    c["fpool"] = np.broadcast_to(np.array(FPOOL_VALS, dtype=np.float32)[None, :],
                                 (128, len(FPOOL_VALS))).copy()
    return c


FPOOL_VALS = [1e-5, -0.5, 1.0 / 128.0, INV_SQRT_HD, 128.0, 1024.0, 0.5, 2.0]

CONST_SHAPES = {
    "ident_b": ([128, 128], BF16), "ident_f": ([128, 128], F32),
    "ones_col_b": ([128, 1], BF16),
    "ones_row_f": ([1, 128], F32), "diffmaskT": ([128, 128], F32),
    "c127": ([128, 128], F32), "colp1_1024": ([128, 1024], F32),
    "triT_b": ([128, 128], BF16), "neg1_8": ([128, 8], F32),
    "sel8_b": ([8, 1024], BF16), "lmask_b": ([1024, 1024], BF16),
    "fpool": ([128, len(FPOOL_VALS)], F32),
}


def declare_constants(nc):
    return {n: nc.dram_tensor("cst_" + n, shp, dt, kind="ExternalInput").ap()
            for n, (shp, dt) in CONST_SHAPES.items()}


def load_constants(k, S, drams):
    sb = {}
    toks = []
    for n, d in drams.items():
        shp, dt = CONST_SHAPES[n]
        if shp[0] > 128:
            nt = shp[0] // 128
            t = S.sbuf("sb_" + n, [128, nt * shp[1]], dt)
            toks.append(k.dma("sp", t[:, :], d.rearrange("(n p) q -> p n q", p=128)))
        else:
            t = S.sbuf("sb_" + n, shp, dt)
            toks.append(k.dma("sp", t[:, :], d))
        sb[n] = t
    fp = sb["fpool"]
    for i, v in enumerate(FPOOL_VALS):
        k.nc.const_aps.register(F32, float(v), fp[:, i:i + 1])
    k.fence = [(n, s[1]) for n, s in k.sems.items()]
    sb["_tok"] = toks
    return sb



def emit_layernorm(k, C, u_sb, x_out_sb, T, tag="ln", dram_out=None):
    S = k.scope()
    ps_s = S.psum(tag + "_s", [1, T])
    ps_q = S.psum(tag + "_q", [1, T])
    ps_b1 = S.psum(tag + "_b1", [128, T])
    ps_b2 = S.psum(tag + "_b2", [128, T])
    usq = S.sbuf(tag + "_usq", [128, T], BF16)
    row = S.sbuf(tag + "_row", [1, 5 * T], F32)
    rbc = S.sbuf(tag + "_rbc", [128, T], F32)
    otile = S.sbuf(tag + "_ot", [128, 2 * T], F32) if dram_out is not None else None

    ones = C["ones_col_b"]
    onesr = C["ones_row_f"]
    for i in range(NDT):
        ut = u_sb[:, i * T:(i + 1) * T]
        for j in range(0, T, 512):
            k.op("pe", "matmul", ps_s[0:1, j:j + 512], ones[:, :],
                 u_sb[:, i * T + j:i * T + j + 512], start=(i == 0), stop=(i == NDT - 1),
                 deps=[k.tok("dve"), k.tok("act")] if (i == 0 and j == 0) else [])
        sq = k.op("act", "activation", usq[:, :], ut, AF.Square,
                  deps=[k.tok("pe")] if i > 0 else [k.tok("dve")])
        for j in range(0, T, 512):
            k.op("pe", "matmul", ps_q[0:1, j:j + 512], ones[:, :], usq[:, j:j + 512],
                 start=(i == 0), stop=(i == NDT - 1), deps=[sq] if j == 0 else [])
    pe_done = k.tok("pe")
    mean = row[:, 0:T]
    var = row[:, T:2 * T]
    rstd = row[:, 2 * T:3 * T]
    mrs = row[:, 3 * T:4 * T]
    scr = row[:, 4 * T:5 * T]
    t1 = k.op("dve", "tensor_scalar", mean, ps_s[:, :], 1.0 / D, None, ALU.mult, deps=[pe_done])
    t2 = k.op("dve", "tensor_scalar", scr, ps_q[:, :], 1.0 / D, None, ALU.mult)
    t3 = k.op("dve", "tensor_mul", var, mean, mean)
    t4 = k.op("dve", "tensor_sub", var, scr, var)
    a1 = k.op("act", "activation", var, var, AF.Ln, bias=EPS, deps=[t4])
    a2 = k.op("act", "activation", rstd, var, AF.Exp, scale=-0.5)
    t5 = k.op("dve", "tensor_mul", mrs, mean, rstd, deps=[a2])
    for j in range(T // 512):
        sl = slice(j * 512, (j + 1) * 512)
        k.op("pe", "matmul", ps_b1[:, sl], onesr[:, :], rstd[0:1, sl], start=True, stop=True,
             deps=[a2])
        k.op("pe", "matmul", ps_b2[:, sl], onesr[:, :], mrs[0:1, sl], start=True, stop=True,
             deps=[t5])
    bc_done = k.tok("pe")
    e1 = k.op("dve", "tensor_copy", rbc[:, :], ps_b1[:, :], deps=[bc_done])
    stoks = []
    dr = fmr(dram_out) if dram_out is not None else None
    for i in range(NDT):
        ut = u_sb[:, i * T:(i + 1) * T]
        m1 = k.op("dve", "tensor_mul", usq[:, :], ut, rbc[:, :], deps=[e1])
        if dram_out is None:
            ot = x_out_sb[:, i * T:(i + 1) * T]
            k.op("dve", "scalar_tensor_tensor", ot, ps_b2[:, :], -1.0, usq[:, :],
                 ALU.mult, ALU.add, deps=[m1])
        else:
            sl = (i % 2) * T
            dep = [stoks[i - 2]] if i >= 2 else []
            s1 = k.op("dve", "scalar_tensor_tensor", otile[:, sl:sl + T],
                      ps_b2[:, :], -1.0, usq[:, :], ALU.mult, ALU.add,
                      deps=[m1] + dep)
            stoks.append(k.dma("sp", dr[:, i, :], otile[:, sl:sl + T], deps=[s1],
                                sem=tag + "_st" + str(i % 2)))
    out = k.tok("dve")
    S.close()
    if dram_out is not None:
        return stoks
    return out


FM = dict(p=128)


def fmr(ap):
    return ap.rearrange("(n p) t -> p n t", p=128)



def phase_ffn0_build(k, C, io):
    SG = k.scope()
    lutcol = SG.sbuf("b_lutcol", [128, 16], F32)
    SL = k.scope()
    lutrow = SL.sbuf("b_lutrow", [1, 16], F32)
    ps_lut = SL.psum("b_pslut", [128, 16])
    tl = k.dma("sp", lutrow[:, :], io["lut"].rearrange("(o l) -> o l", o=1))
    m = k.op("pe", "matmul", ps_lut[:, :], C["ones_row_f"][:, :], lutrow[:, :],
             start=True, stop=True, deps=[tl] + C["_tok"])
    lc = k.op("dve", "tensor_copy", lutcol[:, :], ps_lut[:, :], deps=[m])
    SL.close()

    CW = 512
    bidx = SG.sbuf("b_bidx", [128, CW], I32)
    bidxb = SG.sbuf("b_bidxb", [128, CW], BF16)
    btmp = SG.sbuf("b_btmp", [128, CW], BF16)
    bacc = SG.sbuf("b_bacc", [128, CW], BF16)
    bidx2 = SG.sbuf("b_bidx2", [128, CW], I32)
    bacc2 = SG.sbuf("b_bacc2", [128, CW], BF16)
    bzero = SG.sbuf("b_bzero", [128, CW], BF16)
    zz = k.op("dve", "memset", bzero[:, :], 0.0)

    prev_store = [None, None]
    sl = [0]

    def build_rows(src_ap, dst_ap, nrows, ncols):
        toks = []
        r = 0
        while r < nrows:
            p = min(128, nrows - r)
            c0 = 0
            while c0 < ncols:
                w = min(CW, ncols - c0)
                s = sl[0] % 2
                bidx_s = bidx if s == 0 else bidx2
                bacc_s = bacc if s == 0 else bacc2
                ld = k.dma("sp", bidx_s[0:p, 0:w],
                           in_=src_ap[r:r + p, c0:c0 + w],
                           deps=[prev_store[s]], sem=f"d_bld{s}")
                k.op("dve", "tensor_add", bidxb[0:p, 0:w], bidx_s[0:p, 0:w],
                     bzero[0:p, 0:w], deps=[ld, lc, zz])
                for l in range(LUTN):
                    tgt = bacc_s if l == 0 else btmp
                    k.op("dve", "scalar_tensor_tensor", tgt[0:p, 0:w],
                         bidxb[0:p, 0:w], float(l),
                         lutcol[0:p, l:l + 1].broadcast_to((p, w)),
                         ALU.is_equal, ALU.mult,
                         deps=[prev_store[s]] if l == 0 else [])
                    if l > 0:
                        k.op("dve", "tensor_add", bacc_s[0:p, 0:w], bacc_s[0:p, 0:w],
                             btmp[0:p, 0:w])
                st = k.dma("sp", dst_ap[r:r + p, c0:c0 + w], bacc_s[0:p, 0:w],
                           deps=[k.tok("dve")], sem=f"d_bst{s}")
                prev_store[s] = st
                toks.append(st)
                sl[0] += 1
                c0 += w
            r += p
        return toks

    tg = build_rows(io["gidxT_s"], io["g_in"], GHR, INTER)
    tu = build_rows(io["uidxT_s"], io["u_in"], GHR, INTER)
    td = build_rows(io["didxT_s"], io["d_in"], DHR, D)

    cg = k.cc("AllGather", ALU.bypass, [io["g_in"]], [io["g_out"]], deps=tg)
    cu = k.cc("AllGather", ALU.bypass, [io["u_in"]], [io["u_out"]], deps=tu)
    cd = k.cc("AllGather", ALU.bypass, [io["d_in"]], [io["d_out"]], deps=td)
    io["_cc"] = (cg, cu, cd)
    return SG


def phase_ffn0(k, C, io):
    nc = k.nc
    SB = k.scope()

    x2 = SB.sbuf("b_x2", [128, NDT * TOK], BF16)
    h_sb = SB.sbuf("b_h", [128, NOC * TOK], BF16)
    tx2 = k.dma("sp", x2[:, :], fmr(io["x2"]))
    cg, cu, cd = io["_cc"]

    NB = 2
    SGU = k.scope()
    wg = [SGU.sbuf(f"b_wg{j}", [128, NDT * 256], BF16) for j in range(NB)]
    wu = [SGU.sbuf(f"b_wu{j}", [128, NDT * 256], BF16) for j in range(NB)]
    sg = SGU.sbuf("b_sg", [128, TOK], BF16)
    ps_g = SGU.psum("b_psg", [128, TOK])
    ps_u = SGU.psum("b_psu", [128, TOK])

    ngrp = (NOC + 1) // 2
    ldtok = [None] * ngrp
    usetok = [None] * (NOC + 1)

    def load_grp(g):
        o0 = 2 * g
        no = min(2, NOC - o0)
        b = g % NB
        deps = [cg, cu, tx2]
        if g >= NB:
            deps.append(usetok[2 * (g - NB) + min(2, NOC - 2 * (g - NB)) - 1])
        t1 = k.dma("sp", wg[b][:, 0:NDT * no * 128],
                   io["g_out"][:, o0 * 128:(o0 + no) * 128].rearrange(
                       "(n p) o -> p n o", p=128), deps=deps)
        t2 = k.dma("sp", wu[b][:, 0:NDT * no * 128],
                   io["u_out"][:, o0 * 128:(o0 + no) * 128].rearrange(
                       "(n p) o -> p n o", p=128), deps=deps)
        ldtok[g] = (t1, t2)

    for g in range(min(NB, ngrp)):
        load_grp(g)
    for g in range(ngrp):
        o0 = 2 * g
        no = min(2, NOC - o0)
        b = g % NB
        for oj in range(no):
            oc = o0 + oj
            for kt in range(NDT):
                dep = []
                if kt == 0:
                    dep = [ldtok[g][0], ldtok[g][1], k.tok("act"), k.tok("dve")]
                wsl = slice((kt * no + oj) * 128, (kt * no + oj) * 128 + 128)
                for j in range(0, TOK, 512):
                    k.op("pe", "matmul", ps_g[:, j:j + 512], wg[b][:, wsl],
                         x2[:, kt * TOK + j:kt * TOK + j + 512],
                         start=(kt == 0), stop=(kt == NDT - 1),
                         deps=dep if j == 0 else [])
                    k.op("pe", "matmul", ps_u[:, j:j + 512], wu[b][:, wsl],
                         x2[:, kt * TOK + j:kt * TOK + j + 512],
                         start=(kt == 0), stop=(kt == NDT - 1))
            mm = k.tok("pe")
            a = k.op("act", "activation", sg[:, :], ps_g[:, :], AF.Silu, deps=[mm])
            usetok[oc] = k.op("dve", "tensor_mul", h_sb[:, oc * TOK:(oc + 1) * TOK],
                              sg[:, :], ps_u[:, :], deps=[a, mm])
        if g + NB < ngrp:
            load_grp(g + NB)
    hdone = k.tok("dve")
    SGU.close()

    SD = k.scope()
    wd = [SD.sbuf(f"b_wd{j}", [128, NOC * 128], BF16) for j in range(NB)]
    ps_d = SD.psum("b_psd", [128, TOK])
    dtok = [None] * NDT
    vtok = [None] * NDT

    def load_d(dc):
        b = dc % NB
        deps = [cd, hdone]
        if dc >= NB:
            deps.append(vtok[dc - NB])
        dtok[dc] = k.dma("sp", wd[b][:, :],
                         io["d_out"][:, dc * 128:(dc + 1) * 128].rearrange(
                             "(n p) o -> p n o", p=128), deps=deps)

    for dc in range(min(NB, NDT)):
        load_d(dc)
    for dc in range(NDT):
        b = dc % NB
        for ot in range(NOC):
            dep = [dtok[dc], k.tok("dve")] if ot == 0 else []
            for j in range(0, TOK, 512):
                k.op("pe", "matmul", ps_d[:, j:j + 512], wd[b][:, ot * 128:(ot + 1) * 128],
                     h_sb[:, ot * TOK + j:ot * TOK + j + 512],
                     start=(ot == 0), stop=(ot == NOC - 1), deps=dep if j == 0 else [])
        mm = k.tok("pe")
        vtok[dc] = k.op("dve", "scalar_tensor_tensor", x2[:, dc * TOK:(dc + 1) * TOK],
                        ps_d[:, :], 1.0, x2[:, dc * TOK:(dc + 1) * TOK],
                        ALU.mult, ALU.add, deps=[mm])
        if dc + NB < NDT:
            load_d(dc + NB)
    SD.close()

    emit_layernorm(k, C, x2, x2, TOK, tag="b_ln")
    ot = k.dma("sp", fmr(io["x3"]), x2[:, :], deps=[k.tok("dve")])
    SB.close()
    return [ot]





def phase_l0_attn(k, C, io):
    nc = k.nc
    SB = k.scope()
    x_sb = SB.sbuf("a_x", [128, NDT * TOK], BF16)
    q_sb = SB.sbuf("a_q", [128, NDT * TOK], BF16)
    o_sb = SB.sbuf("a_o", [128, NDT * TOK], BF16)
    S_f = SB.sbuf("a_Sf", [128, H * HD], F32)
    S_b = SB.sbuf("a_Sb", [128, H * HD], BF16)
    ldrow = SB.sbuf("a_ldrow", [1, 16], F32)
    ldcol = SB.sbuf("a_ldcol", [128, 16], F32)
    cdcol = SB.sbuf("a_cdcol", [128, 16], F32)
    dt_t = SB.sbuf("a_dt", [128, H * 128], F32)
    kd_t = SB.sbuf("a_kd", [128, H * 128], F32)
    qd_t = SB.sbuf("a_qd", [128, H * 128], F32)

    tx = k.dma("pool", x_sb[:, :], fmr(io["x0"]))
    tld = k.dma("pool", ldrow[:, :], io["ld"].rearrange("(o l) -> o l", o=1))

    SP0 = k.scope()
    ps_ld = SP0.psum("a_psld", [128, 16])
    m = k.op("pe", "matmul", ps_ld[:, :], C["ones_row_f"][:, :], ldrow[:, :],
             start=True, stop=True, deps=[tld] + C["_tok"])
    lc = k.op("dve", "tensor_copy", ldcol[:, :], ps_ld[:, :], deps=[m])
    SP0.close()
    k.op("act", "activation", cdcol[:, :], ldcol[:, :], AF.Exp, scale=128.0, deps=[lc])
    for h in range(H):
        sc = ldcol[:, h:h + 1]
        k.op("act", "activation", dt_t[:, h * 128:(h + 1) * 128], C["diffmaskT"][:, :],
             AF.Exp, scale=sc, deps=[lc])
        k.op("act", "activation", kd_t[:, h * 128:(h + 1) * 128], C["c127"][:, :],
             AF.Exp, scale=sc)
        k.op("act", "activation", qd_t[:, h * 128:(h + 1) * 128], C["colp1_1024"][:, 0:128],
             AF.Exp, scale=sc)
    zS = k.op("dve", "memset", S_f[:, :], 0.0)
    zSb = k.op("dve", "memset", S_b[:, :], 0.0)

    for hh in range(2):
        t0 = hh * 512
        SH = k.scope()
        k_sb = SH.sbuf("a_k", [128, NDT * 512], BF16)
        v_sb = SH.sbuf("a_v", [128, NDT * 512], BF16)
        wcol = [SH.sbuf(f"a_wcol{j}", [128, NDT * 128], BF16) for j in range(2)]
        ps_p = [SH.psum(f"a_psp{j}", [128, 512]) for j in range(2)]
        seq = [(w, d, ft) for w, d in (("wq", None), ("wk", k_sb), ("wv", v_sb))
               for ft in range(NDT)]
        twl = [None] * len(seq)
        mmdone = [None] * len(seq)
        evtok = [None] * len(seq)

        def load_proj(n):
            wname, _, ft = seq[n]
            twl[n] = k.dma("pool", wcol[n % 2][:, :],
                           io[wname][:, ft * 128:(ft + 1) * 128].rearrange(
                               "(n p) f -> p n f", p=128),
                           deps=[mmdone[n - 2]] if n >= 2 else [k.tok("pe")])

        load_proj(0)
        load_proj(1)
        for n, (wname, dst, ft) in enumerate(seq):
            for kt in range(NDT):
                dep = []
                if kt == 0:
                    dep = [twl[n], tx] + ([evtok[n - 2]] if n >= 2 else [])
                k.op("pe", "matmul", ps_p[n % 2][:, :],
                     wcol[n % 2][:, kt * 128:(kt + 1) * 128],
                     x_sb[:, kt * TOK + t0:kt * TOK + t0 + 512],
                     start=(kt == 0), stop=(kt == NDT - 1), deps=dep)
            mmdone[n] = k.tok("pe")
            if n + 2 < len(seq):
                load_proj(n + 2)
            if dst is None:
                evtok[n] = k.op("act", "activation",
                                q_sb[:, ft * TOK + t0:ft * TOK + t0 + 512],
                                ps_p[n % 2][:, :], AF.Silu, deps=[mmdone[n]])
            else:
                evtok[n] = k.op("act", "activation", dst[:, ft * 512:(ft + 1) * 512],
                                ps_p[n % 2][:, :], AF.Silu, deps=[mmdone[n]])

        SS = k.scope()
        ksc = SS.sbuf("a_ksc", [128, 128], BF16)
        ksctm = SS.sbuf("a_ksctm", [128, 128], BF16)
        vtm = SS.sbuf("a_vtm", [128, 128], BF16)
        dsc = SS.sbuf("a_dsc", [128, 128], BF16)
        qsc = SS.sbuf("a_qsc", [128, 128], BF16)
        ps_tr = SS.psum("a_pstr", [128, 128], BF16)
        ps_sc = SS.psum("a_pssc", [128, 128])
        ps_o = SS.psum("a_pso", [128, 128])
        ps_kv = SS.psum("a_pskv", [128, 128])
        qkv_done = k.tok("act")
        for c in range(4):
            gc = hh * 4 + c
            for h in range(H):
                kfm = k_sb[:, h * 512 + c * 128:h * 512 + c * 128 + 128]
                vfm = v_sb[:, h * 512 + c * 128:h * 512 + c * 128 + 128]
                qfm = q_sb[:, h * TOK + gc * 128:h * TOK + gc * 128 + 128]
                t1 = k.op("dve", "tensor_mul", ksc[:, :], kfm,
                          kd_t[:, h * 128:(h + 1) * 128], deps=[qkv_done, k.tok("pe")])
                tr1 = k.op("pe", "transpose", ps_tr[:, :], ksc[:, :], C["ident_b"][:, :],
                           deps=[t1])
                e1 = k.op("dve", "tensor_copy", ksctm[:, :], ps_tr[:, :], deps=[tr1])
                tr2 = k.op("pe", "transpose", ps_tr[:, :], vfm, C["ident_b"][:, :],
                           deps=[e1])
                e2 = k.op("dve", "tensor_copy", vtm[:, :], ps_tr[:, :], deps=[tr2])
                mm1 = k.op("pe", "matmul", ps_sc[:, :], kfm, qfm, start=True, stop=True,
                           deps=[k.tok("dve")])
                d1 = k.op("dve", "tensor_mul", dsc[:, :], ps_sc[:, :],
                          dt_t[:, h * 128:(h + 1) * 128], deps=[mm1])
                d2 = k.op("dve", "tensor_mul", qsc[:, :], qfm,
                          qd_t[:, h * 128:(h + 1) * 128])
                mo1 = k.op("pe", "matmul", ps_o[:, :], vtm[:, :], dsc[:, :],
                           start=True, stop=False, deps=[d1, e2])
                mo2 = k.op("pe", "matmul", ps_o[:, :], S_b[:, h * 128:(h + 1) * 128],
                           qsc[:, :], start=False, stop=True, deps=[d2])
                eo = k.op("dve", "tensor_copy",
                          o_sb[:, h * TOK + gc * 128:h * TOK + gc * 128 + 128],
                          ps_o[:, :], deps=[mo2])
                mkv = k.op("pe", "matmul", ps_kv[:, :], ksctm[:, :], vtm[:, :],
                           start=True, stop=True, deps=[k.tok("dve")])
                su = k.op("dve", "scalar_tensor_tensor", S_f[:, h * 128:(h + 1) * 128],
                          S_f[:, h * 128:(h + 1) * 128], cdcol[:, h:h + 1],
                          ps_kv[:, :], ALU.mult, ALU.add, deps=[mkv, zS])
                k.op("dve", "tensor_copy", S_b[:, h * 128:(h + 1) * 128],
                     S_f[:, h * 128:(h + 1) * 128], deps=[su, zSb])
        SS.close()
        SH.close()

    tT = k.dma("sp", io["T_in"].rearrange("(n p) e -> p n e", p=128), S_f[:, :],
               deps=[k.tok("dve")])
    cT = k.cc("AllGather", ALU.bypass, [io["T_in"]], [io["T_out"]], deps=[tT])

    S6 = k.scope()
    S0f = S6.sbuf("a_S0f", [128, H * HD], F32)
    S0b = S6.sbuf("a_S0b", [128, H * HD], BF16)
    tr_sb = S6.sbuf("a_tr", [128, H * HD], F32)
    perow = S6.sbuf("a_perow", [1, 16], F32)
    pecol = S6.sbuf("a_pecol", [128, 16], F32)
    w16 = S6.sbuf("a_w16", [128, 16], F32)
    ps_pe = S6.psum("a_pspe", [128, 16])
    tpe = k.dma("pool", perow[:, :], io["pev"].rearrange("(o l) -> o l", o=1))
    mpe = k.op("pe", "matmul", ps_pe[:, :], C["ones_row_f"][:, :], perow[:, :],
               start=True, stop=True, deps=[tpe])
    epe = k.op("dve", "tensor_copy", pecol[:, :], ps_pe[:, :], deps=[mpe])
    z0 = k.op("dve", "memset", S0f[:, :], 0.0)
    ltok = None
    for r in range(NCORES):
        ltok = k.dma("sp", tr_sb[:, :],
                     io["T_out"][r * D:(r + 1) * D, :].rearrange("(n p) e -> p n e", p=128),
                     deps=[cT] + ([k.tok("dve")] if r > 0 else []))
        w1 = k.op("dve", "tensor_scalar", w16[:, :], ldcol[:, :], pecol[:, r:r + 1],
                  1024.0, ALU.mult, ALU.mult, deps=[epe])
        w2 = k.op("act", "activation", w16[:, :], w16[:, :], AF.Exp, deps=[w1])
        w3 = k.op("dve", "tensor_scalar", w16[:, :], w16[:, :], pecol[:, 8 + r:9 + r],
                  None, ALU.mult, deps=[w2])
        for h in range(H):
            k.op("dve", "scalar_tensor_tensor", S0f[:, h * 128:(h + 1) * 128],
                 tr_sb[:, h * 128:(h + 1) * 128], w16[:, h:h + 1],
                 S0f[:, h * 128:(h + 1) * 128], ALU.mult, ALU.add,
                 deps=[w3, ltok, z0])
    cb = k.op("dve", "tensor_copy", S0b[:, :], S0f[:, :])

    qdc = S6.sbuf("a_qdc", [128, TOK], F32)
    qq = S6.sbuf("a_qq", [128, TOK], BF16)
    ps_c = S6.psum("a_psc", [128, 512])
    for h in range(H):
        a1 = k.op("act", "activation", qdc[:, :], C["colp1_1024"][:, :], AF.Exp,
                  scale=ldcol[:, h:h + 1], deps=[k.tok("dve")])
        a2 = k.op("dve", "tensor_mul", qq[:, :], q_sb[:, h * TOK:(h + 1) * TOK],
                  qdc[:, :], deps=[a1])
        for j in (0, 512):
            mc = k.op("pe", "matmul", ps_c[:, :], S0b[:, h * 128:(h + 1) * 128],
                      qq[:, j:j + 512], start=True, stop=True,
                      deps=[a2, cb, k.tok("dve")])
            k.op("dve", "tensor_add", o_sb[:, h * TOK + j:h * TOK + j + 512],
                 o_sb[:, h * TOK + j:h * TOK + j + 512], ps_c[:, :], deps=[mc])
    S6.close()

    S7 = k.scope()
    wg1_sb = S7.sbuf("a_wg1", [128, NDT * 128], BF16)
    wg2_sb = S7.sbuf("a_wg2", [128, NDT * 128], BF16)
    nw_sb = S7.sbuf("a_nw", [128, 16], F32)
    glow = S7.sbuf("a_glow", [128, TOK], BF16)
    sig = S7.sbuf("a_sig", [128, TOK], BF16)
    osq = S7.sbuf("a_osq", [128, TOK], BF16)
    rrow = S7.sbuf("a_rrow", [1, TOK], F32)
    ps_g = S7.psum("a_psg", [128, TOK])
    ps_r = S7.psum("a_psr", [1, TOK])
    ps_b = S7.psum("a_psb", [128, TOK])
    tw1 = k.dma("pool", wg1_sb[:, :], io["wg1"].rearrange("(n p) f -> p n f", p=128))
    tw2 = k.dma("pool", wg2_sb[:, :], io["wg2"].rearrange("p (n f) -> p n f", f=128))
    tnw = k.dma("pool", nw_sb[:, :], io["normw"])
    for j in (0, 512):
        for kt in range(NDT):
            k.op("pe", "matmul", ps_g[:, j:j + 512], wg1_sb[:, kt * 128:(kt + 1) * 128],
                 x_sb[:, kt * TOK + j:kt * TOK + j + 512],
                 start=(kt == 0), stop=(kt == NDT - 1),
                 deps=[tw1, k.tok("dve")] if kt == 0 else [])
    gl = k.op("dve", "tensor_copy", glow[:, :], ps_g[:, :], deps=[k.tok("pe")])
    for h in range(H):
        for j in (0, 512):
            mg = k.op("pe", "matmul", ps_g[:, j:j + 512], wg2_sb[:, h * 128:(h + 1) * 128],
                      glow[:, j:j + 512], start=True, stop=True,
                      deps=[gl, tw2, k.tok("act"), k.tok("dve")] if j == 0 else [])
        sg1 = k.op("act", "activation", sig[:, :], ps_g[:, :], AF.Sigmoid,
                   deps=[k.tok("pe")])
        ho = o_sb[:, h * TOK:(h + 1) * TOK]
        t1 = k.op("dve", "tensor_mul", ho, ho, sig[:, :], deps=[sg1])
        s1 = k.op("act", "activation", osq[:, :], ho, AF.Square, deps=[t1])
        for j in (0, 512):
            k.op("pe", "matmul", ps_r[0:1, j:j + 512], C["ones_col_b"][:, :],
                 osq[:, j:j + 512], start=True, stop=True,
                 deps=[s1, k.tok("act"), k.tok("dve")] if j == 0 else [])
        l1 = k.op("act", "activation", rrow[:, :], ps_r[0:1, :], AF.Ln,
                  scale=1.0 / 128.0, bias=EPS, deps=[k.tok("pe")])
        l2 = k.op("act", "activation", rrow[:, :], rrow[:, :], AF.Exp, scale=-0.5)
        for j in (0, 512):
            k.op("pe", "matmul", ps_b[:, j:j + 512], C["ones_row_f"][:, :],
                 rrow[0:1, j:j + 512], start=True, stop=True,
                 deps=[l2, k.tok("dve")] if j == 0 else [])
        bb = k.tok("pe")
        t2 = k.op("dve", "tensor_mul", ho, ho, ps_b[:, :], deps=[bb])
        k.op("dve", "tensor_scalar", ho, ho, nw_sb[:, h:h + 1], None, ALU.mult,
             deps=[tnw])
    S7.close()

    S8 = k.scope()
    wo_sb = S8.sbuf("a_wo", [128, NDT * 128], BF16)
    ps_w = S8.psum("a_psw", [128, 512])
    for e in range(NDT):
        two = k.dma("pool", wo_sb[:, :],
                    io["wo"][:, e * 128:(e + 1) * 128].rearrange("(n p) f -> p n f", p=128),
                    deps=[k.tok("pe")])
        for j in (0, 512):
            for kt in range(NDT):
                k.op("pe", "matmul", ps_w[:, :], wo_sb[:, kt * 128:(kt + 1) * 128],
                     o_sb[:, kt * TOK + j:kt * TOK + j + 512],
                     start=(kt == 0), stop=(kt == NDT - 1),
                     deps=[two, k.tok("dve")] if kt == 0 else [])
            k.op("dve", "scalar_tensor_tensor",
                 x_sb[:, e * TOK + j:e * TOK + j + 512], ps_w[:, :], 1.0,
                 x_sb[:, e * TOK + j:e * TOK + j + 512], ALU.mult, ALU.add,
                 deps=[k.tok("pe")])
    S8.close()

    emit_layernorm(k, C, x_sb, x_sb, TOK, tag="a_ln")
    ot = k.dma("sp", fmr(io["x2"]), x_sb[:, :], deps=[k.tok("dve")])
    SB.close()
    return [ot]


def declare_l0_io(nc, kin, kout):
    io = {}
    io["x0"] = nc.dram_tensor("x0", [D, TOK], F32, kind=kin).ap()
    io["ld"] = nc.dram_tensor("ld", [H], F32, kind="ExternalInput").ap()
    for w in ("wq", "wk", "wv", "wo"):
        io[w] = nc.dram_tensor(w, [D, D], F32, kind="ExternalInput").ap()
    io["wg1"] = nc.dram_tensor("wg1", [D, 128], F32, kind="ExternalInput").ap()
    io["wg2"] = nc.dram_tensor("wg2", [128, D], F32, kind="ExternalInput").ap()
    io["normw"] = nc.dram_tensor("normw", [128, NDT], F32, kind="ExternalInput").ap()
    io["pev"] = nc.dram_tensor("pev", [16], F32, kind="ExternalInput").ap()
    io["T_in"] = nc.dram_tensor("T_in", [D, HD], F32, kind="Internal").ap()
    io["T_out"] = nc.dram_tensor("T_out", [NCORES * D, HD], F32, kind="Internal",
                                 addr_space=_ccsp("AllGather")).ap()
    io["x2"] = nc.dram_tensor("x2", [D, TOK], BF16, kind=kout).ap()
    return io


def build_l0_program():
    nc = bass.Bass("TRN2", target_bir_lowering=False, debug=False)
    io = declare_l0_io(nc, "ExternalInput", "ExternalOutput")
    cst = declare_constants(nc)
    with contextlib.ExitStack() as es:
        k = K(nc, es)
        with nc.Block() as block:
            S0 = k.scope()
            C = load_constants(k, S0, cst)
            outs = phase_l0_attn(k, C, io)
            k.wait("sp", outs)
            k.emit_all(block)
            S0.close()
    return nc


def stage_l0_inputs(inputs):
    cst = host_constants()
    x = inputs["L0_attn_x"][0]
    maps = []
    for c in range(NCORES):
        m = {("cst_" + n): v for n, v in cst.items()}
        m["x0"] = np.ascontiguousarray(x[c * TOK:(c + 1) * TOK, :].T)
        m["ld"] = inputs["L0_attn_log_decay"]
        m["wq"] = inputs["L0_attn_Wq"]
        m["wk"] = inputs["L0_attn_Wk"]
        m["wv"] = inputs["L0_attn_Wv"]
        m["wo"] = inputs["L0_attn_Wo"]
        m["wg1"] = inputs["L0_attn_Wg1"]
        m["wg2"] = inputs["L0_attn_Wg2"]
        m["normw"] = np.ascontiguousarray(inputs["L0_attn_norm_w"].reshape(16, 128).T)
        pev = np.zeros(16, dtype=np.float32)
        for r in range(NCORES):
            pev[r] = max(c - 1 - r, 0)
            pev[8 + r] = 1.0 if r < c else 0.0
        m["pev"] = pev
        maps.append(m)
    return maps





NKB_FULL = 56


def phase_l1_attn(k, C, io):
    nc = k.nc
    SB = k.scope()
    x_sb = SB.sbuf("c_x", [128, NDT * TOK], BF16)
    q_sb = SB.sbuf("c_q", [128, NDT * TOK], BF16)
    kf_sb = SB.sbuf("c_kf", [128, NDT * TOK], BF16)
    vtl_sb = SB.sbuf("c_vtl", [128, NDT * TOK], BF16)
    y_sb = q_sb
    nm = SB.sbuf("c_nm", [1, 64], F32)
    bat_sb = SB.sbuf("c_bat", [128, 48], F32)
    tx = k.dma("sp", x_sb[:, :], fmr(io["x3"]))
    tb = k.dma("sp", bat_sb[:, :], io["battn_c"])

    SQ = k.scope()
    wcol = [SQ.sbuf(f"c_wcol{j}", [128, NDT * 128], BF16) for j in range(2)]
    vtmp = SQ.sbuf("c_vtmp", [128, TOK], BF16)
    ps_p = [SQ.psum(f"c_psp{j}", [128, TOK]) for j in range(2)]
    ps_tr = SQ.psum("c_pstr", [128, 128], BF16)
    kin_toks = []
    vin_toks = []
    twl = [None] * 48
    mmdone = [None] * 48
    evtok = [None] * 48

    def load_qkv(n):
        twl[n] = k.dma("pool", wcol[n % 2][:, :],
                       io["wattn"][:, n * 128:(n + 1) * 128].rearrange(
                           "(n p) f -> p n f", p=128),
                       deps=[mmdone[n - 2]] if n >= 2 else [k.tok("pe")])

    load_qkv(0)
    load_qkv(1)
    for ft in range(48):
        h = ft % 16
        dst = vtmp if ft >= 32 else (q_sb if ft < 16 else kf_sb)
        for j in (0, 512):
            for kt in range(NDT):
                dep = []
                if kt == 0:
                    dep = [twl[ft], tx] + ([evtok[ft - 2]] if (ft >= 2 and j == 0)
                                           else [k.tok("dve")] if (ft < 2 and j == 0)
                                           else [])
                k.op("pe", "matmul", ps_p[ft % 2][:, j:j + 512],
                     wcol[ft % 2][:, kt * 128:(kt + 1) * 128],
                     x_sb[:, kt * TOK + j:kt * TOK + j + 512],
                     start=(kt == 0), stop=(kt == NDT - 1), deps=dep)
        mmdone[ft] = k.tok("pe")
        if ft + 2 < 48:
            load_qkv(ft + 2)
        for j in (0, 512):
            evtok[ft] = k.op("dve", "tensor_scalar",
                             dst[:, j:j + 512] if ft >= 32 else
                             dst[:, h * TOK + j:h * TOK + j + 512],
                             ps_p[ft % 2][:, j:j + 512], bat_sb[:, ft:ft + 1], None,
                             ALU.add, deps=[mmdone[ft], tb] if j == 0 else [])
        if 16 <= ft < 32:
            kin_toks.append(k.dma("sp", io["K_in"][h * 128:(h + 1) * 128, :],
                                  kf_sb[:, h * TOK:(h + 1) * TOK],
                                  deps=[k.tok("dve")]))
        if ft >= 32:
            for kb in range(8):
                tr = k.op("pe", "transpose", ps_tr[:, :],
                          vtmp[:, kb * 128:kb * 128 + 128], C["ident_b"][:, :],
                          deps=[k.tok("act"), k.tok("dve")])
                k.op("dve", "tensor_copy",
                     vtl_sb[:, h * TOK + kb * 128:h * TOK + kb * 128 + 128],
                     ps_tr[:, :], deps=[tr])
            vin_toks.append(k.dma("sp",
                                  io["V_in"][h * TOK:(h + 1) * TOK, :].rearrange(
                                      "(kb p) d -> p kb d", p=128),
                                  vtl_sb[:, h * TOK:(h + 1) * TOK],
                                  deps=[k.tok("dve")]))
    qsq = SQ.sbuf("c_qsq", [128, TOK], BF16)
    t11 = SQ.sbuf("c_t11", [1, 2], F32)
    for h in range(H):
        s1 = k.op("act", "activation", qsq[:, :], q_sb[:, h * TOK:(h + 1) * TOK],
                  AF.Square, deps=[k.tok("pe")])
        for j in (0, 512):
            m1 = k.op("pe", "matmul", ps_p[0][0:1, 0:512], C["ones_col_b"][:, :],
                      qsq[:, j:j + 512], start=True, stop=True,
                      deps=[s1, k.tok("dve")])
            r1 = k.op("dve", "reduce_max", t11[:, 0:1], ps_p[0][0:1, 0:512], AX.X, deps=[m1])
            if j == 0:
                k.op("dve", "tensor_copy", nm[:, h:h + 1], t11[:, 0:1])
            else:
                k.op("dve", "tensor_max", nm[:, h:h + 1], nm[:, h:h + 1], t11[:, 0:1])
    SQ.close()

    cK = k.cc("AllGather", ALU.bypass, [io["K_in"]], [io["K_out"]], deps=kin_toks)
    cV = k.cc("AllGather", ALU.bypass, [io["V_in"]], [io["V_out"]], deps=vin_toks)

    SA = k.scope()
    Kh = SA.sbuf("c_Kh", [128, 8 * TOK], BF16)
    Vh = SA.sbuf("c_Vh", [128, 64 * 128], BF16)
    ksq = SA.sbuf("c_ksq", [128, 512], BF16)
    probsT = [SA.sbuf(f"c_prt{j}", [128, TOK], BF16) for j in range(2)]
    bigm_bc = SA.sbuf("c_bigm", [128, 64], F32)
    bcol = SA.sbuf("c_bcol", [128, 64], F32)
    negb = SA.sbuf("c_negb", [128, 1], F32)
    rrow = SA.sbuf("c_rrow", [1, TOK], F32)
    rb = SA.sbuf("c_rb", [128, TOK], F32)
    bgr = SA.sbuf("c_bgr", [1, 64], F32)
    ps_sc = [SA.psum(f"c_pssc{j}", [128, TOK]) for j in range(2)]
    ps_sums = SA.psum("c_pssums", [1, TOK])
    ps_o = SA.psum("c_pso", [128, TOK])
    ps_bc = ps_sc[0]

    tbg = k.dma("sp", bgr[:, :], io["bigm"])
    mbg = k.op("pe", "matmul", ps_bc[0:128, 0:64], C["ones_row_f"][:, :], bgr[0:1, :],
               start=True, stop=True, deps=[tbg])
    k.op("dve", "tensor_copy", bigm_bc[:, :], ps_bc[:, 0:64], deps=[mbg])

    for h in range(H):
        hdep = [cK, cV, k.tok("pe"), k.tok("act"), k.tok("dve")]
        tk = [k.dma("sp", Kh[:, r * TOK:(r + 1) * TOK],
                    io["K_out"][r * D + h * 128:r * D + h * 128 + 128, :], deps=hdep)
              for r in range(NCORES)]
        tv = [k.dma("sp", Vh[:, r * 8 * 128:(r + 1) * 8 * 128],
                    io["V_out"][r * (NCORES * D) + h * TOK:
                                r * (NCORES * D) + (h + 1) * TOK, :].rearrange(
                        "(kb p) d -> p kb d", p=128), deps=hdep)
              for r in range(NCORES)]
        for j16 in range(16):
            s1 = k.op("act", "activation", ksq[:, :],
                      Kh[:, j16 * 512:(j16 + 1) * 512], AF.Square,
                      deps=tk + [k.tok("pe")])
            m1 = k.op("pe", "matmul", ps_bc[0:1, 0:512], C["ones_col_b"][:, :],
                      ksq[:, :], start=True, stop=True,
                      deps=[s1, k.tok("dve")])
            k.op("dve", "reduce_max", nm[:, 32:33], ps_bc[0:1, 0:512], AX.X, deps=[m1])
            if j16 == 0:
                k.op("dve", "tensor_copy", nm[:, 16 + h:17 + h], nm[:, 32:33])
            else:
                k.op("dve", "tensor_max", nm[:, 16 + h:17 + h], nm[:, 16 + h:17 + h],
                     nm[:, 32:33])
        k.op("dve", "tensor_mul", nm[:, 33:34], nm[:, h:h + 1], nm[:, 16 + h:17 + h])
        sq = k.op("act", "activation", nm[:, 34:35], nm[:, 33:34], AF.Sqrt,
                  scale=1.0 / 128.0, deps=[k.tok("dve")])
        mB = k.op("pe", "matmul", ps_bc[0:128, 0:1], C["ones_row_f"][:, :],
                  nm[0:1, 34:35], start=True, stop=True, deps=[sq, k.tok("dve")])
        k.op("dve", "tensor_scalar", negb[:, :], ps_bc[:, 0:1], -1.0, None, ALU.mult,
             deps=[mB])
        k.op("dve", "tensor_scalar", bcol[:, :], bigm_bc[:, :], -1.0,
             negb[:, 0:1], ALU.mult, ALU.add)
        exp_tok = [None] * 64
        sc_tok = [None] * 64
        oT_tok = [None] * 64

        def kb_src(kb):
            if kb < NKB_FULL:
                return (Kh[:, kb * 128:kb * 128 + 128],
                        Vh[:, kb * 128:kb * 128 + 128], bcol[:, kb:kb + 1], None)
            lkb = kb - NKB_FULL
            return (kf_sb[:, h * TOK + lkb * 128:h * TOK + lkb * 128 + 128],
                    vtl_sb[:, h * TOK + lkb * 128:h * TOK + lkb * 128 + 128],
                    negb[:, 0:1], C["lmask_b"][:, lkb * TOK:(lkb + 1) * TOK])

        def issue_scores(kb):
            p = kb % 2
            ksrc, _, _, _ = kb_src(kb)
            dep0 = [exp_tok[kb - 2]] if kb >= 2 else [k.tok("act"), k.tok("dve")]
            if kb == 0:
                dep0 += [tk[-1], tv[-1]]
            for j in (0, 512):
                sc_tok[kb] = k.op("pe", "matmul", ps_sc[p][:, j:j + 512], ksrc,
                     q_sb[:, h * TOK + j:h * TOK + j + 512], start=True, stop=True,
                     deps=dep0 if j == 0 else [])

        issue_scores(0)
        for kb in range(64):
            p = kb % 2
            _, vsrc, bc_, msk_ = kb_src(kb)
            dep_e = [sc_tok[kb]] + ([oT_tok[kb - 2]] if kb >= 2 else [])
            e1 = k.op("act", "activation", probsT[p][:, :], ps_sc[p][:, :], AF.Exp,
                      scale=INV_SQRT_HD, bias=bc_, deps=dep_e)
            if msk_ is not None:
                e1 = k.op("dve", "tensor_mul", probsT[p][:, :], probsT[p][:, :],
                          msk_, deps=[e1])
            exp_tok[kb] = e1
            if kb + 1 < 64:
                issue_scores(kb + 1)
            last = (kb == 63)
            st = (kb == 0)
            for j in (0, 512):
                k.op("pe", "matmul", ps_sums[0:1, j:j + 512], C["ones_col_b"][:, :],
                     probsT[p][:, j:j + 512], start=st, stop=last,
                     deps=[e1] if j == 0 else [])
                oT_tok[kb] = k.op("pe", "matmul", ps_o[:, j:j + 512], vsrc,
                     probsT[p][:, j:j + 512], start=st, stop=last)
        mm = k.tok("pe")
        r1 = k.op("dve", "reciprocal", rrow[:, :], ps_sums[0:1, :], deps=[mm])
        for j in (0, 512):
            k.op("pe", "matmul", ps_bc[:, j:j + 512], C["ones_row_f"][:, :],
                 rrow[0:1, j:j + 512], start=True, stop=True,
                 deps=[r1, k.tok("dve")] if j == 0 else [])
        e2 = k.op("dve", "tensor_copy", rb[:, :], ps_bc[:, :], deps=[k.tok("pe")])
        k.op("dve", "tensor_mul", y_sb[:, h * TOK:(h + 1) * TOK], ps_o[:, :],
             rb[:, :], deps=[e2])
    SA.close()

    S8 = k.scope()
    wo_sb = S8.sbuf("c_wo", [128, NDT * 128], BF16)
    bp_sb = S8.sbuf("c_bp", [128, 16], F32)
    ps_w = S8.psum("c_psw", [128, 512])
    tbp = k.dma("sp", bp_sb[:, :], io["bproj_c"])
    for e in range(NDT):
        two = k.dma("pool", wo_sb[:, :],
                    io["wproj"][:, e * 128:(e + 1) * 128].rearrange(
                        "(n p) f -> p n f", p=128),
                    deps=[k.tok("pe")])
        for j in (0, 512):
            for kt in range(NDT):
                k.op("pe", "matmul", ps_w[:, :], wo_sb[:, kt * 128:(kt + 1) * 128],
                     y_sb[:, kt * TOK + j:kt * TOK + j + 512],
                     start=(kt == 0), stop=(kt == NDT - 1),
                     deps=[two, k.tok("dve")] if kt == 0 else [])
            k.op("dve", "scalar_tensor_tensor",
                 x_sb[:, e * TOK + j:e * TOK + j + 512], ps_w[:, :],
                 bp_sb[:, e:e + 1], x_sb[:, e * TOK + j:e * TOK + j + 512],
                 ALU.add, ALU.add, deps=[k.tok("pe"), tbp])
    S8.close()

    emit_layernorm(k, C, x_sb, x_sb, TOK, tag="c_ln")
    ot = k.dma("sp", fmr(io["x4"]), x_sb[:, :], deps=[k.tok("dve")])
    SB.close()
    return [ot]


def declare_l1a_io(nc, kin, kout, x3_ap=None):
    io = {}
    io["x3"] = x3_ap if x3_ap is not None else nc.dram_tensor("x3", [D, TOK], BF16, kind=kin).ap()
    io["wattn"] = nc.dram_tensor("wattn", [D, DQKV], F32, kind="ExternalInput").ap()
    io["battn_c"] = nc.dram_tensor("battn_c", [128, 48], F32, kind="ExternalInput").ap()
    io["wproj"] = nc.dram_tensor("wproj", [D, D], F32, kind="ExternalInput").ap()
    io["bproj_c"] = nc.dram_tensor("bproj_c", [128, 16], F32, kind="ExternalInput").ap()
    io["bigm"] = nc.dram_tensor("bigm", [1, 64], F32, kind="ExternalInput").ap()
    io["K_in"] = nc.dram_tensor("K_in", [D, TOK], BF16, kind="Internal").ap()
    io["V_in"] = nc.dram_tensor("V_in", [NDT * TOK, 128], BF16, kind="Internal").ap()
    sp = _ccsp("AllGather")
    io["K_out"] = nc.dram_tensor("K_out", [NCORES * D, TOK], BF16, kind="Internal",
                                 addr_space=sp).ap()
    io["V_out"] = nc.dram_tensor("V_out", [NCORES * NDT * TOK, 128], BF16,
                                 kind="Internal", addr_space=sp).ap()
    io["x4"] = nc.dram_tensor("x4", [D, TOK], BF16, kind=kout).ap()
    return io


def build_l1a_program():
    nc = bass.Bass("TRN2", target_bir_lowering=False, debug=False)
    io = declare_l1a_io(nc, "ExternalInput", "ExternalOutput")
    cst = declare_constants(nc)
    with contextlib.ExitStack() as es:
        k = K(nc, es)
        with nc.Block() as block:
            S0 = k.scope()
            C = load_constants(k, S0, cst)
            outs = phase_l1_attn(k, C, io)
            k.wait("sp", outs)
            k.emit_all(block)
            S0.close()
    return nc


def stage_l1a_inputs(inputs, x3_percore):
    cst = host_constants()
    maps = []
    for c in range(NCORES):
        m = {("cst_" + n): v for n, v in cst.items()}
        if x3_percore is not None:
            m["x3"] = x3_percore[c]
        m["wattn"] = inputs["L1_attn_W_attn"]
        m["battn_c"] = np.ascontiguousarray(inputs["L1_attn_b_attn"].reshape(48, 128).T)
        m["wproj"] = inputs["L1_attn_W_proj"]
        m["bproj_c"] = np.ascontiguousarray(inputs["L1_attn_b_proj"].reshape(16, 128).T)
        bigm = np.zeros((1, 64), dtype=np.float32)
        bigm[0, c * 8:] = 1e30
        m["bigm"] = bigm
        maps.append(m)
    return maps



def phase_l1_ffn(k, C, io):
    nc = k.nc
    SB = k.scope()
    x_sb = SB.sbuf("d_x", [128, NDT * TOK], BF16)
    cmb_t = SB.sbuf("d_cmb", [128, NE * TOK], BF16)
    t1s = SB.sbuf("d_t1s", [16, NE * TOK], BF16)
    t2s = SB.sbuf("d_t2s", [16, NE * TOK], BF16)
    csr = SB.sbuf("d_csr", [1, TOK], F32)
    a1t_sb = SB.sbuf("d_a1t", [128, NDT * 128], BF16)
    a2t_sb = SB.sbuf("d_a2t", [128, NFC * 128], BF16)
    b2t_sb = SB.sbuf("d_b2t", [16, NE * D], BF16)
    b1c_sb = SB.sbuf("d_b1c", [128, NFC], F32)
    b2r_sb = SB.sbuf("d_b2r", [1, D], F32)
    tx = k.dma("sp", x_sb[:, :], fmr(io["x4"]))
    ta1 = k.dma("pool", a1t_sb[:, :], io["a1t"].rearrange("(n p) c -> p n c", p=128))
    ta2 = k.dma("pool", a2t_sb[:, :], io["a2t"].rearrange("(n p) c -> p n c", p=128))
    tb2 = k.dma("pool", b2t_sb[:, :], io["b2t"])
    tb1c = k.dma("sp", b1c_sb[:, :], io["b1c"])
    tb2r = k.dma("sp", b2r_sb[:, :], io["b2r"])

    SR = k.scope()
    wr_sb = SR.sbuf("d_wr", [128, NDT * 8], BF16)
    brb = SR.sbuf("d_brb", [128, 8], F32)
    lg = SR.sbuf("d_lg", [128, 8], F32)
    prb = SR.sbuf("d_prb", [128, 8], F32)
    p2 = SR.sbuf("d_p2", [128, 8], F32)
    eqm = SR.sbuf("d_eqm", [128, 8], F32)
    eqi = SR.sbuf("d_eqi", [128, 8], I32)
    mrow = SR.sbuf("d_mrow", [128, 4], F32)
    cmb_tm = SR.sbuf("d_cmbtm", [128, 8], F32)
    cs_tm = SR.sbuf("d_cstm", [128, 1], F32)
    cmbT_sb = SR.sbuf("d_cmbr", [8, TOK], BF16)
    ps_r = SR.psum("d_psr", [128, 8])
    ps_t = SR.psum("d_pst", [128, 128])
    ps_cb = SR.psum("d_pscb", [128, TOK])
    twr = k.dma("pool", wr_sb[:, :], io["wr"].rearrange("(n p) e -> p n e", p=128))
    tbr = k.dma("sp", brb[0:1, :], io["br"])
    mbr = k.op("pe", "matmul", ps_t[0:128, 0:8], C["ones_row_f"][:, :], brb[0:1, :],
               start=True, stop=True, deps=[tbr])
    k.op("dve", "tensor_copy", brb[:, :], ps_t[:, 0:8], deps=[mbr])
    for tc in range(8):
        for kt in range(NDT):
            k.op("pe", "matmul", ps_r[:, :],
                 x_sb[:, kt * TOK + tc * 128:kt * TOK + tc * 128 + 128],
                 wr_sb[:, kt * 8:(kt + 1) * 8], start=(kt == 0), stop=(kt == NDT - 1),
                 deps=[twr, tx, k.tok("dve")] if kt == 0 else [])
        k.op("dve", "tensor_add", lg[:, :], ps_r[:, :], brb[:, :], deps=[k.tok("pe")])
        k.op("dve", "reduce_max", mrow[:, 0:1], lg[:, :], AX.X)
        k.op("dve", "tensor_scalar", mrow[:, 1:2], mrow[:, 0:1], -1.0, None, ALU.mult)
        k.op("act", "activation", prb[:, :], lg[:, :], AF.Exp, bias=mrow[:, 1:2],
             accum_out=mrow[:, 2:3], deps=[k.tok("dve")])
        k.op("dve", "reciprocal", mrow[:, 3:4], mrow[:, 2:3], deps=[k.tok("act")])
        k.op("dve", "tensor_scalar", prb[:, :], prb[:, :], mrow[:, 3:4], None, ALU.mult)
        k.op("dve", "reduce_max", mrow[:, 0:1], prb[:, :], AX.X)
        k.op("dve", "tensor_scalar", eqi[:, :], prb[:, :], mrow[:, 0:1], None,
             ALU.is_ge)
        k.op("dve", "tensor_copy", p2[:, :], prb[:, :])
        k.op("dve", "copy_predicated", p2[:, :], eqi[:, :], C["neg1_8"][:, :])
        k.op("dve", "reduce_max", mrow[:, 1:2], p2[:, :], AX.X)
        k.op("dve", "tensor_scalar", eqm[:, :], prb[:, :], mrow[:, 1:2], None,
             ALU.is_ge)
        k.op("dve", "tensor_mul", cmb_tm[:, :], prb[:, :], eqm[:, :])
        k.op("dve", "tensor_add", cs_tm[:, :], mrow[:, 0:1], mrow[:, 1:2])
        tr1 = k.op("pe", "transpose", ps_t[0:8, 0:128], cmb_tm[:, :],
                   C["ident_f"][:, :], deps=[k.tok("dve")])
        k.op("dve", "tensor_copy", cmbT_sb[:, tc * 128:(tc + 1) * 128],
             ps_t[0:8, 0:128], deps=[tr1])
        tr2 = k.op("pe", "transpose", ps_t[0:1, 0:128], cs_tm[:, :],
                   C["ident_f"][:, :], deps=[k.tok("dve")])
        k.op("dve", "tensor_copy", csr[:, tc * 128:(tc + 1) * 128], ps_t[0:1, 0:128],
             deps=[tr2])
    for e in range(NE):
        for j in (0, 512):
            k.op("pe", "matmul", ps_cb[:, j:j + 512],
                 C["sel8_b"][:, e * 128:(e + 1) * 128],
                 cmbT_sb[0:8, j:j + 512], start=True, stop=True,
                 deps=[k.tok("dve")] if j == 0 else [])
        k.op("dve", "tensor_copy", cmb_t[:, e * TOK:(e + 1) * TOK], ps_cb[:, :],
             deps=[k.tok("pe")])
    SR.close()

    ST = k.scope()
    ps_t1 = ST.psum("d_pst1", [128, TOK])
    for i in range(4):
        e0, e1 = 2 * i, 2 * i + 1
        for dt in range(NDT):
            st = (dt == 0)
            sp_ = (dt == NDT - 1)
            for j in (0, 512):
                dep = [ta1, tx, k.tok("dve")] if (dt == 0 and j == 0) else []
                k.op("pe", "matmul", ps_t1[0:16, j:j + 512],
                     a1t_sb[:, dt * 128 + e0 * 16:dt * 128 + e0 * 16 + 16],
                     x_sb[:, dt * TOK + j:dt * TOK + j + 512],
                     start=st, stop=sp_, tile_position=(0, 0), deps=dep)
                k.op("pe", "matmul", ps_t1[64:80, j:j + 512],
                     a1t_sb[:, dt * 128 + e1 * 16:dt * 128 + e1 * 16 + 16],
                     x_sb[:, dt * TOK + j:dt * TOK + j + 512],
                     start=st, stop=sp_, tile_position=(0, 64))
        mm = k.tok("pe")
        k.op("dve", "tensor_scalar", t1s[:, e0 * TOK:(e0 + 1) * TOK],
             ps_t1[0:16, :], 2.0, None, ALU.mult, deps=[mm])
        k.op("dve", "tensor_scalar", t1s[:, e1 * TOK:(e1 + 1) * TOK],
             ps_t1[64:80, :], 2.0, None, ALU.mult)
    ST.close()

    SM = k.scope()
    NBW = 2
    w1c = [SM.sbuf(f"d_w1c{j}", [128, NDT * 128], BF16) for j in range(NBW)]
    b1tc = [SM.sbuf(f"d_b1tc{j}", [16, NE * 128], BF16) for j in range(NBW)]
    base_sb = SM.sbuf("d_base", [128, 512], BF16)
    a_sb = [SM.sbuf(f"d_a{j}", [128, 512], BF16) for j in range(2)]
    ca_sb = SM.sbuf("d_ca", [128, 512], BF16)
    macc = SM.sbuf("d_macc", [128, 512], BF16)
    ps_base = SM.psum("d_psbase", [128, 512])
    ps_e = [SM.psum(f"d_pse{j}", [128, 512]) for j in range(2)]
    t2ps = [SM.psum(f"d_t2ps{i}", [128, 512]) for i in range(4)]
    b1t_r = io["b1t"].rearrange("p (e f) -> p e f", f=FF)

    last_mix = [None]
    for hh in range(2):
        t0 = hh * 512
        wtok = [None] * NFC
        mixtok = [None] * NFC

        def load_fc(fc, hh=hh, wtok=wtok, mixtok=mixtok):
            b = fc % NBW
            deps = ([mixtok[fc - NBW]] if fc >= NBW else []) + [k.tok("pe")]
            t1_ = k.dma("pool", w1c[b][:, :],
                        io["w1"][:, fc * 128:(fc + 1) * 128].rearrange(
                            "(n p) f -> p n f", p=128), deps=deps)
            t2_ = k.dma("pool", b1tc[b][:, :], b1t_r[:, :, fc * 128:(fc + 1) * 128],
                        deps=deps)
            wtok[fc] = (t1_, t2_)

        for fc in range(NBW):
            load_fc(fc)
        for fc in range(NFC):
            b = fc % NBW
            for kt in range(NDT):
                k.op("pe", "matmul", ps_base[:, :], w1c[b][:, kt * 128:(kt + 1) * 128],
                     x_sb[:, kt * TOK + t0:kt * TOK + t0 + 512],
                     start=(kt == 0), stop=(kt == NDT - 1),
                     deps=[wtok[fc][0], k.tok("act"), k.tok("dve")] if kt == 0 else [])
            bs = k.op("act", "activation", base_sb[:, :], ps_base[:, :], AF.Copy,
                      deps=[k.tok("pe")])
            ge_tok = [None] * NE
            ca_tok = [None] * NE
            pre_tok = [None] * NE
            t2_tok = [None] * NE

            def issue_pre(e):
                pp = ps_e[e % 2]
                dep = [ge_tok[e - 2]] if e >= 2 else [k.tok("act"), k.tok("dve")]
                if e == 0:
                    dep += [wtok[fc][1], bs]
                k.op("pe", "matmul", pp[:, :], b1tc[b][:, e * 128:(e + 1) * 128],
                     t1s[0:16, e * TOK + t0:e * TOK + t0 + 512],
                     start=True, stop=False, deps=dep)
                pre_tok[e] = k.op("pe", "matmul", pp[:, :], C["ident_b"][:, :],
                                  base_sb[:, :], start=False, stop=True,
                                  deps=[bs] if e < 2 else [])

            issue_pre(0)
            for e in range(NE):
                i, ro = e // 2, 64 * (e % 2)
                dep_g = [pre_tok[e], tb1c]
                if e >= 2:
                    dep_g += [ca_tok[e - 2], t2_tok[e - 2]]
                ge = k.op("act", "activation", a_sb[e % 2][:, :], ps_e[e % 2][:, :],
                          AF.Gelu_apprx_tanh, bias=b1c_sb[:, fc:fc + 1], deps=dep_g)
                ge_tok[e] = ge
                if e + 1 < NE:
                    issue_pre(e + 1)
                t2_tok[e] = k.op("pe", "matmul", t2ps[i][ro:ro + 16, :],
                     a2t_sb[:, fc * 128 + e * 16:fc * 128 + e * 16 + 16],
                     a_sb[e % 2][:, :], start=(fc == 0), stop=(fc == NFC - 1),
                     tile_position=(0, ro), skip_group_check=True,
                     deps=[ge, ta2])
                ca_tok[e] = k.op("dve", "tensor_mul", ca_sb[:, :], a_sb[e % 2][:, :],
                     cmb_t[:, e * TOK + t0:e * TOK + t0 + 512], deps=[ge])
                if e == 0:
                    k.op("dve", "tensor_copy", macc[:, :], ca_sb[:, :],
                         deps=[mixtok[fc - 1] if fc > 0 else last_mix[0]])
                else:
                    k.op("dve", "tensor_add", macc[:, :], macc[:, :], ca_sb[:, :])
            mixtok[fc] = k.dma("sp", io["mix_d"][fc * 128:(fc + 1) * 128,
                                                 t0:t0 + 512], macc[:, :],
                               deps=[k.tok("dve")])
            if fc + NBW < NFC:
                load_fc(fc + NBW)
        last_mix[0] = mixtok[NFC - 1]
        for e in range(NE):
            i, ro = e // 2, 64 * (e % 2)
            k.op("dve", "scalar_tensor_tensor",
                 t2s[0:16, e * TOK + t0:e * TOK + t0 + 512],
                 t2ps[i][ro:ro + 16, :], 2.0,
                 cmb_t[0:16, e * TOK + t0:e * TOK + t0 + 512],
                 ALU.mult, ALU.mult, deps=[k.tok("pe")])
    SM.close()

    SW = k.scope()
    NBW2 = 2
    NFG = NFC // 4
    mixg = [SW.sbuf(f"d_mixg{j}", [128, 4 * TOK], BF16) for j in range(NBW2)]
    w2g = [SW.sbuf(f"d_w2g{j}", [128, 4 * 512], BF16) for j in range(NBW2)]
    ps_w = [SW.psum(f"d_psw{j}", [128, TOK]) for j in range(4)]
    for dcg in range(4):
        mtok = [None] * NFG
        wtok2 = [None] * NFG

        def load_w2(g, dcg=dcg, mtok=mtok, wtok2=wtok2):
            b = g % NBW2
            deps = [k.tok("pe")] if g >= NBW2 else [k.tok("dve"), k.tok("act")]
            mtok[g] = k.dma("sp", mixg[b][:, :],
                            io["mix_d"][g * 512:(g + 1) * 512, :].rearrange(
                                "(n p) t -> p n t", p=128), deps=deps)
            wtok2[g] = k.dma("pool", w2g[b][:, :],
                             io["w2"][g * 512:(g + 1) * 512,
                                      dcg * 512:(dcg + 1) * 512].rearrange(
                                 "(n p) c -> p n c", p=128), deps=deps)

        for g in range(NBW2):
            load_w2(g)
        for g in range(NFG):
            b = g % NBW2
            for fi in range(4):
                fc = g * 4 + fi
                for dj in range(4):
                    for j in (0, 512):
                        k.op("pe", "matmul", ps_w[dj][:, j:j + 512],
                             w2g[b][:, fi * 512 + dj * 128:fi * 512 + dj * 128 + 128],
                             mixg[b][:, fi * TOK + j:fi * TOK + j + 512],
                             start=(fc == 0), stop=False,
                             deps=[mtok[g], wtok2[g]]
                             if (fi == 0 and dj == 0 and j == 0) else [])
            if g + NBW2 < NFG:
                load_w2(g + NBW2)
        for dj in range(4):
            dc = dcg * 4 + dj
            for e in range(NE):
                for j in (0, 512):
                    k.op("pe", "matmul", ps_w[dj][:, j:j + 512],
                         b2t_sb[:, e * D + dc * 128:e * D + dc * 128 + 128],
                         t2s[0:16, e * TOK + j:e * TOK + j + 512],
                         start=False, stop=False, deps=[tb2, k.tok("dve")] if (e == 0 and j == 0) else [])
            for j in (0, 512):
                k.op("pe", "matmul", ps_w[dj][:, j:j + 512],
                     b2r_sb[0:1, dc * 128:dc * 128 + 128], csr[0:1, j:j + 512],
                     start=False, stop=True, deps=[tb2r] if j == 0 else [])
            for j in (0, 512):
                k.op("dve", "scalar_tensor_tensor",
                     x_sb[:, dc * TOK + j:dc * TOK + j + 512], ps_w[dj][:, j:j + 512],
                     1.0, x_sb[:, dc * TOK + j:dc * TOK + j + 512],
                     ALU.mult, ALU.add, deps=[k.tok("pe")])
    SW.close()

    stoks = emit_layernorm(k, C, x_sb, None, TOK, tag="d_ln", dram_out=io["x5"])
    SB.close()
    return stoks





def _ccsp(kind):
    from concourse.replica_groups import maybe_share_collective_output_space
    return maybe_share_collective_output_space(kind, RG, prefer_local=True)


def declare_ffn0_io(nc, kin, kout, x2_ap=None):
    io = {}
    io["x2"] = x2_ap if x2_ap is not None else nc.dram_tensor("x2", [D, TOK], BF16, kind=kin).ap()
    io["lut"] = nc.dram_tensor("lut", [LUTN], F32, kind="ExternalInput").ap()
    io["gidxT_s"] = nc.dram_tensor("gidxT_s", [GHR, INTER], I32, kind="ExternalInput").ap()
    io["uidxT_s"] = nc.dram_tensor("uidxT_s", [GHR, INTER], I32, kind="ExternalInput").ap()
    io["didxT_s"] = nc.dram_tensor("didxT_s", [DHR, D], I32, kind="ExternalInput").ap()
    io["g_in"] = nc.dram_tensor("g_in", [GHR, INTER], BF16, kind="Internal").ap()
    io["u_in"] = nc.dram_tensor("u_in", [GHR, INTER], BF16, kind="Internal").ap()
    io["d_in"] = nc.dram_tensor("d_in", [DHR, D], BF16, kind="Internal").ap()
    sp = _ccsp("AllGather")
    io["g_out"] = nc.dram_tensor("g_out", [D, INTER], BF16, kind="Internal", addr_space=sp).ap()
    io["u_out"] = nc.dram_tensor("u_out", [D, INTER], BF16, kind="Internal", addr_space=sp).ap()
    io["d_out"] = nc.dram_tensor("d_out", [INTER, D], BF16, kind="Internal", addr_space=sp).ap()
    io["x3"] = nc.dram_tensor("x3", [D, TOK], BF16, kind=kout).ap()
    return io


def build_ffn0_program():
    nc = bass.Bass("TRN2", target_bir_lowering=False, debug=False)
    io = declare_ffn0_io(nc, "ExternalInput", "ExternalOutput")
    cst = declare_constants(nc)
    with contextlib.ExitStack() as es:
        k = K(nc, es)
        with nc.Block() as block:
            S0 = k.scope()
            C = load_constants(k, S0, cst)
            SGb = phase_ffn0_build(k, C, io)
            outs = phase_ffn0(k, C, io)
            SGb.close()
            k.wait("sp", outs)
            k.emit_all(block)
            S0.close()
    return nc







def declare_l1f_io(nc, kin, kout, x4_ap=None):
    io = {}
    io["x4"] = x4_ap if x4_ap is not None else nc.dram_tensor("x4", [D, TOK], BF16, kind=kin).ap()
    io["wr"] = nc.dram_tensor("wr", [D, NE], F32, kind="ExternalInput").ap()
    io["br"] = nc.dram_tensor("br", [1, NE], F32, kind="ExternalInput").ap()
    io["w1"] = nc.dram_tensor("w1", [D, FF], F32, kind="ExternalInput").ap()
    io["b1c"] = nc.dram_tensor("b1c", [128, NFC], F32, kind="ExternalInput").ap()
    io["w2"] = nc.dram_tensor("w2", [FF, D], F32, kind="ExternalInput").ap()
    io["b2r"] = nc.dram_tensor("b2r", [1, D], F32, kind="ExternalInput").ap()
    io["a1t"] = nc.dram_tensor("a1t", [D, 128], F32, kind="ExternalInput").ap()
    io["a2t"] = nc.dram_tensor("a2t", [FF, 128], F32, kind="ExternalInput").ap()
    io["b1t"] = nc.dram_tensor("b1t", [16, NE * FF], F32, kind="ExternalInput").ap()
    io["b2t"] = nc.dram_tensor("b2t", [16, NE * D], F32, kind="ExternalInput").ap()
    io["mix_d"] = nc.dram_tensor("mix_d", [FF, TOK], BF16, kind="Internal").ap()
    io["x5"] = nc.dram_tensor("x5", [D, TOK], F32, kind=kout).ap()
    return io


def build_l1f_program():
    nc = bass.Bass("TRN2", target_bir_lowering=False, debug=False)
    io = declare_l1f_io(nc, "ExternalInput", "ExternalOutput")
    cst = declare_constants(nc)
    with contextlib.ExitStack() as es:
        k = K(nc, es)
        with nc.Block() as block:
            S0 = k.scope()
            C = load_constants(k, S0, cst)
            outs = phase_l1_ffn(k, C, io)
            k.wait("sp", outs)
            k.emit_all(block)
            S0.close()
    return nc


def stage_l1f_inputs(inputs, x4_percore):
    cst = host_constants()
    A1 = inputs["L1_ffn_A1"]; B1 = inputs["L1_ffn_B1"]
    A2 = inputs["L1_ffn_A2"]; B2 = inputs["L1_ffn_B2"]
    maps = []
    for c in range(NCORES):
        m = {("cst_" + n): v for n, v in cst.items()}
        if x4_percore is not None:
            m["x4"] = x4_percore[c]
        m["wr"] = inputs["L1_ffn_Wr"]
        m["br"] = inputs["L1_ffn_br"][None, :]
        m["w1"] = inputs["L1_ffn_W1"]
        m["b1c"] = np.ascontiguousarray(inputs["L1_ffn_b1"].reshape(NFC, 128).T)
        m["w2"] = inputs["L1_ffn_W2"]
        m["b2r"] = inputs["L1_ffn_b2"][None, :]
        m["a1t"] = np.ascontiguousarray(A1.transpose(2, 0, 1).reshape(D, 128))
        m["a2t"] = np.ascontiguousarray(A2.transpose(2, 0, 1).reshape(FF, 128))
        m["b1t"] = np.ascontiguousarray(B1.transpose(2, 0, 1).reshape(16, NE * FF))
        m["b2t"] = np.ascontiguousarray(B2.transpose(2, 0, 1).reshape(16, NE * D))
        maps.append(m)
    return maps


def build_ab_program():
    nc = bass.Bass("TRN2", target_bir_lowering=False, debug=False)
    ioa = declare_l0_io(nc, "ExternalInput", "Internal")
    iob = declare_ffn0_io(nc, "Internal", "ExternalOutput", x2_ap=ioa["x2"])
    cst = declare_constants(nc)
    with contextlib.ExitStack() as es:
        k = K(nc, es)
        with nc.Block() as block:
            S0 = k.scope()
            C = load_constants(k, S0, cst)
            outs_a = phase_l0_attn(k, C, ioa)
            k.fence = [(n, s[1]) for n, s in k.sems.items()]
            outs_b = phase_ffn0(k, C, iob)
            k.wait("sp", outs_a + outs_b)
            k.emit_all(block)
            S0.close()
    return nc


def stage_ffn0_inputs(inputs, x2_percore):
    cst = host_constants()
    gT = np.ascontiguousarray(inputs["L0_ffn_gate_idx"].T)
    uT = np.ascontiguousarray(inputs["L0_ffn_up_idx"].T)
    dT = np.ascontiguousarray(inputs["L0_ffn_down_idx"].T)
    maps = []
    for c in range(NCORES):
        m = {("cst_" + n): v for n, v in cst.items()}
        m["x2"] = x2_percore[c]
        m["lut"] = inputs["L0_ffn_lut"]
        m["gidxT_s"] = np.ascontiguousarray(gT[c * GHR:(c + 1) * GHR, :])
        m["uidxT_s"] = np.ascontiguousarray(uT[c * GHR:(c + 1) * GHR, :])
        m["didxT_s"] = np.ascontiguousarray(dT[c * DHR:(c + 1) * DHR, :])
        maps.append(m)
    return maps


def kernel(**inputs):
    raise NotImplementedError("under construction")


FUSED = True


def build_full_program(expose=False):
    nc = bass.Bass("TRN2", target_bir_lowering=False, debug=False)
    kh = "ExternalOutput" if expose else "Internal"
    ioa = declare_l0_io(nc, "ExternalInput", kh)
    iob = declare_ffn0_io(nc, kh, kh, x2_ap=ioa["x2"])
    ioc = declare_l1a_io(nc, kh, kh, x3_ap=iob["x3"])
    iod = declare_l1f_io(nc, kh, "ExternalOutput", x4_ap=ioc["x4"])
    cst = declare_constants(nc)
    with contextlib.ExitStack() as es:
        k = K(nc, es)
        with nc.Block() as block:
            S0 = k.scope()
            C = load_constants(k, S0, cst)
            SGb = phase_ffn0_build(k, C, iob)
            phase_l0_attn(k, C, ioa)
            k.fence = [(n, s[1]) for n, s in k.sems.items()]
            phase_ffn0(k, C, iob)
            SGb.close()
            k.fence = [(n, s[1]) for n, s in k.sems.items()]
            phase_l1_attn(k, C, ioc)
            k.fence = [(n, s[1]) for n, s in k.sems.items()]
            outs = phase_l1_ffn(k, C, iod)
            k.wait("sp", outs)
            k.emit_all(block)
            S0.close()
    return nc


def stage_all_inputs(inputs):
    ma = stage_l0_inputs(inputs)
    mb_ = stage_ffn0_inputs(inputs, [None] * NCORES)
    mc = stage_l1a_inputs(inputs, None)
    md = stage_l1f_inputs(inputs, None)
    maps = []
    for c in range(NCORES):
        m = dict(ma[c])
        for mm in (mb_[c], mc[c], md[c]):
            mm = dict(mm)
            mm.pop("x2", None)
            m.update(mm)
        maps.append(m)
    return maps


_PROG_CACHE = {}


def kernel(**inputs):
    import ml_dtypes
    if FUSED:
        if "full" not in _PROG_CACHE:
            _PROG_CACHE["full"] = build_full_program()
        nc = _PROG_CACHE["full"]
        maps = stage_all_inputs(inputs)
        res = run_bass_kernel_spmd(nc, maps, core_ids=list(range(NCORES)))
        xs = [np.asarray(res.results[c]["x5"]).astype(np.float32).T for c in range(NCORES)]
    else:
        if "a" not in _PROG_CACHE:
            _PROG_CACHE["a"] = build_l0_program()
            _PROG_CACHE["b"] = build_ffn0_program()
            _PROG_CACHE["c"] = build_l1a_program()
            _PROG_CACHE["d"] = build_l1f_program()
        maps = stage_l0_inputs(inputs)
        res = run_bass_kernel_spmd(_PROG_CACHE["a"], maps, core_ids=list(range(NCORES)))
        x2 = [np.asarray(res.results[c]["x2"]) for c in range(NCORES)]
        maps = stage_ffn0_inputs(inputs, x2)
        res = run_bass_kernel_spmd(_PROG_CACHE["b"], maps, core_ids=list(range(NCORES)))
        x3 = [np.asarray(res.results[c]["x3"]) for c in range(NCORES)]
        maps = stage_l1a_inputs(inputs, x3)
        res = run_bass_kernel_spmd(_PROG_CACHE["c"], maps, core_ids=list(range(NCORES)))
        x4 = [np.asarray(res.results[c]["x4"]) for c in range(NCORES)]
        maps = stage_l1f_inputs(inputs, x4)
        res = run_bass_kernel_spmd(_PROG_CACHE["d"], maps, core_ids=list(range(NCORES)))
        xs = [np.asarray(res.results[c]["x5"]).astype(np.float32).T for c in range(NCORES)]
    out = np.concatenate(xs, axis=0)[None, :, :]
    return np.ascontiguousarray(out)


# version 08
# speedup vs baseline: 1.0320x; 1.0320x over previous
import contextlib
import numpy as np

import concourse.bass as bass
import concourse.mybir as mybir
from concourse.bass_utils import run_bass_kernel_spmd

F32 = mybir.dt.float32
BF16 = mybir.dt.bfloat16
I32 = mybir.dt.int32
AF = mybir.ActivationFunctionType
ALU = mybir.AluOpType
AX = mybir.AxisListType

NCORES = 8
RG = [[0, 1, 2, 3, 4, 5, 6, 7]]
TOK = 1024
SEQ = 8192
D = 2048
NDT = D // 128
H = 16
HD = 128
NCH = TOK // 128
INTER = 5504
NOC = INTER // 128
GHR = D // NCORES
DHR = INTER // NCORES
DQKV = 6144
FF = 8192
NFC = FF // 128
NE = 8
RK = 16
EPS = 1e-5
LUTN = 16
INV_SQRT_HD = float(1.0 / np.sqrt(128.0))

ENGS = ("sp", "act", "pe", "dve", "pool")


_NAMES = {}


def _uniq(name):
    n = _NAMES.get(name, 0)
    _NAMES[name] = n + 1
    return name if n == 0 else f"{name}_{n}"


class Scope:
    def __init__(self, k):
        self.k = k
        self.es = contextlib.ExitStack()

    def sbuf(self, name, shape, dt):
        return self.es.enter_context(self.k.nc.sbuf_tensor(_uniq(name), shape, dt))

    def psum(self, name, shape, dt=F32):
        return self.es.enter_context(self.k.nc.psum_tensor(_uniq(name), shape, dt))

    def close(self):
        self.es.close()
        self.k.fence = [(n, s[1]) for n, s in self.k.sems.items()]


class K:

    def __init__(self, nc, es):
        _NAMES.clear()
        self.nc = nc
        self.es = es
        self.fence = []
        self.ops = {e: [] for e in ENGS}
        self.sems = {}
        self.lastwait = {e: {} for e in ENGS}
        self.pending = {e: {} for e in ENGS}

    def scope(self):
        return Scope(self)

    def sem(self, name):
        if name not in self.sems:
            h = self.es.enter_context(self.nc.semaphore(name))
            self.sems[name] = [h, 0]
        return self.sems[name]

    def wait(self, eng, deps):
        p = self.pending[eng]
        for d in deps:
            if d is None:
                continue
            semname, val = d
            if val > p.get(semname, 0):
                p[semname] = val

    def _take_waits(self, eng):
        out = []
        lw = self.lastwait[eng]
        for semname, val in self.pending[eng].items():
            if val <= 0 or lw.get(semname, 0) >= val:
                continue
            lw[semname] = val
            h = self.sem(semname)[0]
            out.append((h.num, h.name, val))
        self.pending[eng] = {}
        return out

    def flush_waits(self, eng):
        for num, name, val in self._take_waits(eng):
            h = self.sems[name][0]
            self.ops[eng].append(lambda q, h=h, v=val: q.wait_ge(h, v))

    def _do(self, eng, semname, inc, deps, fn, attach=True):
        self.wait(eng, list(deps) + self.fence)
        if not attach:
            self.flush_waits(eng)
            waits = []
        else:
            waits = self._take_waits(eng)
            for num, name, v in waits[:-1]:
                h = self.sems[name][0]
                self.ops[eng].append(lambda q, h=h, v=v: q.wait_ge(h, v))
            waits = waits[-1:]
        s = self.sem(semname)
        s[1] += inc
        val = s[1]

        def run(q, fn=fn, h=s[0], kk=inc, waits=waits):
            bi = fn(q)
            if waits:
                si = mybir.ensure_sync_info(bi.ins)
                for num, name, v in waits:
                    si.on_wait.append(mybir.SyncWait(
                        sync_type="semaphore", id=num, ant_name=name,
                        wait_mode="sem-ge-imm", wait_value=v))
            bi.then_inc(h, kk)

        self.ops[eng].append(run)
        return (semname, val)

    def dma(self, eng, out, in_, deps=(), sem=None, **kw):
        if sem is None:
            t = getattr(out, "tensor", None)
            sem = "d_" + (t.name if t is not None else "x")
        return self._do(eng, sem, 16, deps,
                        lambda q: q.dma_start(out=out, in_=in_, **kw), attach=False)

    def op(self, eng, method, *args, deps=(), **kw):
        deps = list(deps)
        if eng in ("dve", "act", "pool"):
            deps.append(self.tok(eng))
        return self._do(eng, "c_" + eng, 1, deps,
                        lambda q: getattr(q, method)(*args, **kw))

    def cc(self, kind, op, ins, outs, deps=()):
        return self._do("pool", "cc", 1, deps,
                        lambda q: q.collective_compute(kind, op, RG, ins=ins, outs=outs),
                        attach=False)

    def tok(self, eng):
        name = "c_" + eng
        return (name, self.sems[name][1] if name in self.sems else 0)

    def dmatok(self, eng):
        name = "dma_" + eng
        return (name, self.sems[name][1] if name in self.sems else 0)

    def barrier_tokens(self):
        return [self.tok(e) for e in ("pe", "dve", "act")] + \
               [self.dmatok(e) for e in ("sp", "act", "pool")] + [self.tok("pool")]

    def emit_all(self, block):
        for eng in ENGS:
            self.flush_waits(eng)
        for eng, attr in (("sp", "sync"), ("act", "scalar"), ("pe", "tensor"),
                          ("dve", "vector"), ("pool", "gpsimd")):
            ops = self.ops[eng]
            if not ops:
                continue

            def body(q, ops=ops):
                for f in ops:
                    f(q)

            getattr(block, attr)(body)



def host_constants():
    import ml_dtypes
    bf16 = ml_dtypes.bfloat16
    c = {}
    c["ident_b"] = np.eye(128).astype(bf16)
    c["ident_f"] = np.eye(128, dtype=np.float32)
    c["ones_col_b"] = np.ones((128, 1), dtype=bf16)
    c["ones_row_f"] = np.ones((1, 128), dtype=np.float32)
    i = np.arange(128)
    diff = (i[None, :] - i[:, None]).astype(np.float64)
    c["diffmaskT"] = np.where(diff >= 0, diff, 1e6).astype(np.float32)
    c["c127"] = np.broadcast_to((127.0 - i)[None, :], (128, 128)).astype(np.float32).copy()
    c["colp1_1024"] = np.broadcast_to((np.arange(1024) + 1.0)[None, :], (128, 1024)).astype(np.float32).copy()
    c["triT_b"] = (i[:, None] <= i[None, :]).astype(bf16)
    c["neg1_8"] = np.full((128, 8), -1.0, dtype=np.float32)
    sel = np.zeros((8, 8 * 128), dtype=np.float64)
    for e in range(8):
        sel[e, e * 128:(e + 1) * 128] = 1.0
    c["sel8_b"] = sel.astype(bf16)
    lm = np.zeros((8, 128, 8, 128), dtype=np.float64)
    tri = (i[:, None] <= i[None, :])
    for kb in range(8):
        for qb in range(8):
            if qb > kb:
                lm[kb, :, qb, :] = 1.0
            elif qb == kb:
                lm[kb, :, qb, :] = tri
    c["lmask_b"] = lm.reshape(1024, 1024).astype(bf16)
    c["fpool"] = np.broadcast_to(np.array(FPOOL_VALS, dtype=np.float32)[None, :],
                                 (128, len(FPOOL_VALS))).copy()
    return c


FPOOL_VALS = [1e-5, -0.5, 1.0 / 128.0, INV_SQRT_HD, 128.0, 1024.0, 0.5, 2.0]

CONST_SHAPES = {
    "ident_b": ([128, 128], BF16), "ident_f": ([128, 128], F32),
    "ones_col_b": ([128, 1], BF16),
    "ones_row_f": ([1, 128], F32), "diffmaskT": ([128, 128], F32),
    "c127": ([128, 128], F32), "colp1_1024": ([128, 1024], F32),
    "triT_b": ([128, 128], BF16), "neg1_8": ([128, 8], F32),
    "sel8_b": ([8, 1024], BF16), "lmask_b": ([1024, 1024], BF16),
    "fpool": ([128, len(FPOOL_VALS)], F32),
}


def declare_constants(nc):
    return {n: nc.dram_tensor("cst_" + n, shp, dt, kind="ExternalInput").ap()
            for n, (shp, dt) in CONST_SHAPES.items()}


def load_constants(k, S, drams):
    sb = {}
    toks = []
    for n, d in drams.items():
        shp, dt = CONST_SHAPES[n]
        if shp[0] > 128:
            nt = shp[0] // 128
            t = S.sbuf("sb_" + n, [128, nt * shp[1]], dt)
            toks.append(k.dma("sp", t[:, :], d.rearrange("(n p) q -> p n q", p=128)))
        else:
            t = S.sbuf("sb_" + n, shp, dt)
            toks.append(k.dma("sp", t[:, :], d))
        sb[n] = t
    fp = sb["fpool"]
    for i, v in enumerate(FPOOL_VALS):
        k.nc.const_aps.register(F32, float(v), fp[:, i:i + 1])
    k.fence = [(n, s[1]) for n, s in k.sems.items()]
    sb["_tok"] = toks
    return sb



def emit_layernorm(k, C, u_sb, x_out_sb, T, tag="ln", dram_out=None):
    S = k.scope()
    ps_s = S.psum(tag + "_s", [1, T])
    ps_q = S.psum(tag + "_q", [1, T])
    ps_b1 = S.psum(tag + "_b1", [128, T])
    ps_b2 = S.psum(tag + "_b2", [128, T])
    usq = S.sbuf(tag + "_usq", [128, T], BF16)
    row = S.sbuf(tag + "_row", [1, 5 * T], F32)
    rbc = S.sbuf(tag + "_rbc", [128, T], F32)
    otile = S.sbuf(tag + "_ot", [128, 2 * T], F32) if dram_out is not None else None

    ones = C["ones_col_b"]
    onesr = C["ones_row_f"]
    for i in range(NDT):
        ut = u_sb[:, i * T:(i + 1) * T]
        for j in range(0, T, 512):
            k.op("pe", "matmul", ps_s[0:1, j:j + 512], ones[:, :],
                 u_sb[:, i * T + j:i * T + j + 512], start=(i == 0), stop=(i == NDT - 1),
                 deps=[k.tok("dve"), k.tok("act")] if (i == 0 and j == 0) else [])
        sq = k.op("act", "activation", usq[:, :], ut, AF.Square,
                  deps=[k.tok("pe")] if i > 0 else [k.tok("dve")])
        for j in range(0, T, 512):
            k.op("pe", "matmul", ps_q[0:1, j:j + 512], ones[:, :], usq[:, j:j + 512],
                 start=(i == 0), stop=(i == NDT - 1), deps=[sq] if j == 0 else [])
    pe_done = k.tok("pe")
    mean = row[:, 0:T]
    var = row[:, T:2 * T]
    rstd = row[:, 2 * T:3 * T]
    mrs = row[:, 3 * T:4 * T]
    scr = row[:, 4 * T:5 * T]
    t1 = k.op("dve", "tensor_scalar", mean, ps_s[:, :], 1.0 / D, None, ALU.mult, deps=[pe_done])
    t2 = k.op("dve", "tensor_scalar", scr, ps_q[:, :], 1.0 / D, None, ALU.mult)
    t3 = k.op("dve", "tensor_mul", var, mean, mean)
    t4 = k.op("dve", "tensor_sub", var, scr, var)
    a1 = k.op("act", "activation", var, var, AF.Ln, bias=EPS, deps=[t4])
    a2 = k.op("act", "activation", rstd, var, AF.Exp, scale=-0.5)
    t5 = k.op("dve", "tensor_mul", mrs, mean, rstd, deps=[a2])
    for j in range(T // 512):
        sl = slice(j * 512, (j + 1) * 512)
        k.op("pe", "matmul", ps_b1[:, sl], onesr[:, :], rstd[0:1, sl], start=True, stop=True,
             deps=[a2])
        k.op("pe", "matmul", ps_b2[:, sl], onesr[:, :], mrs[0:1, sl], start=True, stop=True,
             deps=[t5])
    bc_done = k.tok("pe")
    e1 = k.op("dve", "tensor_copy", rbc[:, :], ps_b1[:, :], deps=[bc_done])
    stoks = []
    dr = fmr(dram_out) if dram_out is not None else None
    for i in range(NDT):
        ut = u_sb[:, i * T:(i + 1) * T]
        m1 = k.op("dve", "tensor_mul", usq[:, :], ut, rbc[:, :], deps=[e1])
        if dram_out is None:
            ot = x_out_sb[:, i * T:(i + 1) * T]
            k.op("dve", "scalar_tensor_tensor", ot, ps_b2[:, :], -1.0, usq[:, :],
                 ALU.mult, ALU.add, deps=[m1])
        else:
            sl = (i % 2) * T
            dep = [stoks[i - 2]] if i >= 2 else []
            s1 = k.op("dve", "scalar_tensor_tensor", otile[:, sl:sl + T],
                      ps_b2[:, :], -1.0, usq[:, :], ALU.mult, ALU.add,
                      deps=[m1] + dep)
            stoks.append(k.dma("sp", dr[:, i, :], otile[:, sl:sl + T], deps=[s1],
                                sem=tag + "_st" + str(i % 2)))
    out = k.tok("dve")
    S.close()
    if dram_out is not None:
        return stoks
    return out


FM = dict(p=128)


def fmr(ap):
    return ap.rearrange("(n p) t -> p n t", p=128)



def phase_ffn0_build(k, C, io):
    SG = k.scope()
    lutcol = SG.sbuf("b_lutcol", [128, 16], F32)
    SL = k.scope()
    lutrow = SL.sbuf("b_lutrow", [1, 16], F32)
    ps_lut = SL.psum("b_pslut", [128, 16])
    tl = k.dma("sp", lutrow[:, :], io["lut"].rearrange("(o l) -> o l", o=1))
    m = k.op("pe", "matmul", ps_lut[:, :], C["ones_row_f"][:, :], lutrow[:, :],
             start=True, stop=True, deps=[tl] + C["_tok"])
    lc = k.op("dve", "tensor_copy", lutcol[:, :], ps_lut[:, :], deps=[m])
    SL.close()

    CW = 512
    bidx = SG.sbuf("b_bidx", [128, CW], I32)
    bidxb = SG.sbuf("b_bidxb", [128, CW], BF16)
    btmp = SG.sbuf("b_btmp", [128, CW], BF16)
    bacc = SG.sbuf("b_bacc", [128, CW], BF16)
    bidx2 = SG.sbuf("b_bidx2", [128, CW], I32)
    bacc2 = SG.sbuf("b_bacc2", [128, CW], BF16)


    prev_store = [None, None]
    sl = [0]

    def build_rows(src_ap, dst_ap, nrows, ncols):
        toks = []
        r = 0
        while r < nrows:
            p = min(128, nrows - r)
            c0 = 0
            while c0 < ncols:
                w = min(CW, ncols - c0)
                s = sl[0] % 2
                bidx_s = bidx if s == 0 else bidx2
                bacc_s = bacc if s == 0 else bacc2
                ld = k.dma("sp", bidx_s[0:p, 0:w],
                           in_=src_ap[r:r + p, c0:c0 + w],
                           deps=[prev_store[s]], sem=f"d_bld{s}")
                k.op("dve", "tensor_copy", bidxb[0:p, 0:w], bidx_s[0:p, 0:w],
                     deps=[ld, lc])
                for l in range(LUTN):
                    tgt = bacc_s if l == 0 else btmp
                    k.op("dve", "tensor_scalar", tgt[0:p, 0:w], bidxb[0:p, 0:w],
                         float(l), lutcol[0:p, l:l + 1], ALU.is_equal, ALU.mult,
                         deps=[prev_store[s]] if l == 0 else [])
                    if l > 0:
                        k.op("dve", "tensor_add", bacc_s[0:p, 0:w], bacc_s[0:p, 0:w],
                             btmp[0:p, 0:w])
                st = k.dma("sp", dst_ap[r:r + p, c0:c0 + w], bacc_s[0:p, 0:w],
                           deps=[k.tok("dve")], sem=f"d_bst{s}")
                prev_store[s] = st
                toks.append(st)
                sl[0] += 1
                c0 += w
            r += p
        return toks

    tg = build_rows(io["gidxT_s"], io["g_in"], GHR, INTER)
    tu = build_rows(io["uidxT_s"], io["u_in"], GHR, INTER)
    td = build_rows(io["didxT_s"], io["d_in"], DHR, D)

    cg = k.cc("AllGather", ALU.bypass, [io["g_in"]], [io["g_out"]], deps=tg)
    cu = k.cc("AllGather", ALU.bypass, [io["u_in"]], [io["u_out"]], deps=tu)
    cd = k.cc("AllGather", ALU.bypass, [io["d_in"]], [io["d_out"]], deps=td)
    io["_cc"] = (cg, cu, cd)
    return SG


def phase_ffn0(k, C, io):
    nc = k.nc
    SB = k.scope()

    x2 = SB.sbuf("b_x2", [128, NDT * TOK], BF16)
    h_sb = SB.sbuf("b_h", [128, NOC * TOK], BF16)
    tx2 = k.dma("sp", x2[:, :], fmr(io["x2"]))
    cg, cu, cd = io["_cc"]

    NB = 2
    SGU = k.scope()
    wg = [SGU.sbuf(f"b_wg{j}", [128, NDT * 256], BF16) for j in range(NB)]
    wu = [SGU.sbuf(f"b_wu{j}", [128, NDT * 256], BF16) for j in range(NB)]
    sg = SGU.sbuf("b_sg", [128, TOK], BF16)
    ps_g = SGU.psum("b_psg", [128, TOK])
    ps_u = SGU.psum("b_psu", [128, TOK])

    ngrp = (NOC + 1) // 2
    ldtok = [None] * ngrp
    usetok = [None] * (NOC + 1)

    def load_grp(g):
        o0 = 2 * g
        no = min(2, NOC - o0)
        b = g % NB
        deps = [cg, cu, tx2]
        if g >= NB:
            deps.append(usetok[2 * (g - NB) + min(2, NOC - 2 * (g - NB)) - 1])
        t1 = k.dma("sp", wg[b][:, 0:NDT * no * 128],
                   io["g_out"][:, o0 * 128:(o0 + no) * 128].rearrange(
                       "(n p) o -> p n o", p=128), deps=deps)
        t2 = k.dma("sp", wu[b][:, 0:NDT * no * 128],
                   io["u_out"][:, o0 * 128:(o0 + no) * 128].rearrange(
                       "(n p) o -> p n o", p=128), deps=deps)
        ldtok[g] = (t1, t2)

    for g in range(min(NB, ngrp)):
        load_grp(g)
    for g in range(ngrp):
        o0 = 2 * g
        no = min(2, NOC - o0)
        b = g % NB
        for oj in range(no):
            oc = o0 + oj
            for kt in range(NDT):
                dep = []
                if kt == 0:
                    dep = [ldtok[g][0], ldtok[g][1], k.tok("act"), k.tok("dve")]
                wsl = slice((kt * no + oj) * 128, (kt * no + oj) * 128 + 128)
                for j in range(0, TOK, 512):
                    k.op("pe", "matmul", ps_g[:, j:j + 512], wg[b][:, wsl],
                         x2[:, kt * TOK + j:kt * TOK + j + 512],
                         start=(kt == 0), stop=(kt == NDT - 1),
                         deps=dep if j == 0 else [])
                    k.op("pe", "matmul", ps_u[:, j:j + 512], wu[b][:, wsl],
                         x2[:, kt * TOK + j:kt * TOK + j + 512],
                         start=(kt == 0), stop=(kt == NDT - 1))
            mm = k.tok("pe")
            a = k.op("act", "activation", sg[:, :], ps_g[:, :], AF.Silu, deps=[mm])
            usetok[oc] = k.op("dve", "tensor_mul", h_sb[:, oc * TOK:(oc + 1) * TOK],
                              sg[:, :], ps_u[:, :], deps=[a, mm])
        if g + NB < ngrp:
            load_grp(g + NB)
    hdone = k.tok("dve")
    SGU.close()

    SD = k.scope()
    wd = [SD.sbuf(f"b_wd{j}", [128, NOC * 128], BF16) for j in range(NB)]
    ps_d = SD.psum("b_psd", [128, TOK])
    dtok = [None] * NDT
    vtok = [None] * NDT

    def load_d(dc):
        b = dc % NB
        deps = [cd, hdone]
        if dc >= NB:
            deps.append(vtok[dc - NB])
        dtok[dc] = k.dma("sp", wd[b][:, :],
                         io["d_out"][:, dc * 128:(dc + 1) * 128].rearrange(
                             "(n p) o -> p n o", p=128), deps=deps)

    for dc in range(min(NB, NDT)):
        load_d(dc)
    for dc in range(NDT):
        b = dc % NB
        for ot in range(NOC):
            dep = [dtok[dc], k.tok("dve")] if ot == 0 else []
            for j in range(0, TOK, 512):
                k.op("pe", "matmul", ps_d[:, j:j + 512], wd[b][:, ot * 128:(ot + 1) * 128],
                     h_sb[:, ot * TOK + j:ot * TOK + j + 512],
                     start=(ot == 0), stop=(ot == NOC - 1), deps=dep if j == 0 else [])
        mm = k.tok("pe")
        vtok[dc] = k.op("dve", "scalar_tensor_tensor", x2[:, dc * TOK:(dc + 1) * TOK],
                        ps_d[:, :], 1.0, x2[:, dc * TOK:(dc + 1) * TOK],
                        ALU.mult, ALU.add, deps=[mm])
        if dc + NB < NDT:
            load_d(dc + NB)
    SD.close()

    emit_layernorm(k, C, x2, x2, TOK, tag="b_ln")
    ot = k.dma("sp", fmr(io["x3"]), x2[:, :], deps=[k.tok("dve")])
    SB.close()
    return [ot]





def phase_l0_attn(k, C, io):
    nc = k.nc
    SB = k.scope()
    x_sb = SB.sbuf("a_x", [128, NDT * TOK], BF16)
    q_sb = SB.sbuf("a_q", [128, NDT * TOK], BF16)
    o_sb = SB.sbuf("a_o", [128, NDT * TOK], BF16)
    S_f = SB.sbuf("a_Sf", [128, H * HD], F32)
    S_b = SB.sbuf("a_Sb", [128, H * HD], BF16)
    ldrow = SB.sbuf("a_ldrow", [1, 16], F32)
    ldcol = SB.sbuf("a_ldcol", [128, 16], F32)
    cdcol = SB.sbuf("a_cdcol", [128, 16], F32)
    dt_t = SB.sbuf("a_dt", [128, H * 128], F32)
    kd_t = SB.sbuf("a_kd", [128, H * 128], F32)
    qd_t = SB.sbuf("a_qd", [128, H * 128], F32)

    tx = k.dma("pool", x_sb[:, :], fmr(io["x0"]))
    tld = k.dma("pool", ldrow[:, :], io["ld"].rearrange("(o l) -> o l", o=1))

    SP0 = k.scope()
    ps_ld = SP0.psum("a_psld", [128, 16])
    m = k.op("pe", "matmul", ps_ld[:, :], C["ones_row_f"][:, :], ldrow[:, :],
             start=True, stop=True, deps=[tld] + C["_tok"])
    lc = k.op("act", "activation", ldcol[:, :], ps_ld[:, :], AF.Copy, deps=[m])
    SP0.close()
    k.op("act", "activation", cdcol[:, :], ldcol[:, :], AF.Exp, scale=128.0, deps=[lc])
    for h in range(H):
        sc = ldcol[:, h:h + 1]
        k.op("act", "activation", dt_t[:, h * 128:(h + 1) * 128], C["diffmaskT"][:, :],
             AF.Exp, scale=sc, deps=[lc])
        k.op("act", "activation", kd_t[:, h * 128:(h + 1) * 128], C["c127"][:, :],
             AF.Exp, scale=sc)
        k.op("act", "activation", qd_t[:, h * 128:(h + 1) * 128], C["colp1_1024"][:, 0:128],
             AF.Exp, scale=sc)
    zS = k.op("pool", "memset", S_f[:, :], 0.0)
    zSb = k.op("pool", "memset", S_b[:, :], 0.0)

    for hh in range(2):
        t0 = hh * 512
        SH = k.scope()
        k_sb = SH.sbuf("a_k", [128, NDT * 512], BF16)
        v_sb = SH.sbuf("a_v", [128, NDT * 512], BF16)
        wcol = [SH.sbuf(f"a_wcol{j}", [128, NDT * 128], BF16) for j in range(2)]
        ps_p = [SH.psum(f"a_psp{j}", [128, 512]) for j in range(2)]
        seq = [(w, d, ft) for w, d in (("wq", None), ("wk", k_sb), ("wv", v_sb))
               for ft in range(NDT)]
        twl = [None] * len(seq)
        mmdone = [None] * len(seq)
        evtok = [None] * len(seq)

        def load_proj(n):
            wname, _, ft = seq[n]
            twl[n] = k.dma("pool", wcol[n % 2][:, :],
                           io[wname][:, ft * 128:(ft + 1) * 128].rearrange(
                               "(n p) f -> p n f", p=128),
                           deps=[mmdone[n - 2]] if n >= 2 else [k.tok("pe")])

        load_proj(0)
        load_proj(1)
        for n, (wname, dst, ft) in enumerate(seq):
            for kt in range(NDT):
                dep = []
                if kt == 0:
                    dep = [twl[n], tx] + ([evtok[n - 2]] if n >= 2 else [])
                k.op("pe", "matmul", ps_p[n % 2][:, :],
                     wcol[n % 2][:, kt * 128:(kt + 1) * 128],
                     x_sb[:, kt * TOK + t0:kt * TOK + t0 + 512],
                     start=(kt == 0), stop=(kt == NDT - 1), deps=dep)
            mmdone[n] = k.tok("pe")
            if n + 2 < len(seq):
                load_proj(n + 2)
            if dst is None:
                evtok[n] = k.op("act", "activation",
                                q_sb[:, ft * TOK + t0:ft * TOK + t0 + 512],
                                ps_p[n % 2][:, :], AF.Silu, deps=[mmdone[n]])
            else:
                evtok[n] = k.op("act", "activation", dst[:, ft * 512:(ft + 1) * 512],
                                ps_p[n % 2][:, :], AF.Silu, deps=[mmdone[n]])

        SS = k.scope()
        ksc = SS.sbuf("a_ksc", [128, 128], BF16)
        ksctm = SS.sbuf("a_ksctm", [128, 128], BF16)
        vtm = SS.sbuf("a_vtm", [128, 128], BF16)
        dsc = SS.sbuf("a_dsc", [128, 128], BF16)
        qsc = SS.sbuf("a_qsc", [128, 128], BF16)
        ps_tr = SS.psum("a_pstr", [128, 128], BF16)
        ps_sc = SS.psum("a_pssc", [128, 128])
        ps_o = SS.psum("a_pso", [128, 128])
        ps_kv = SS.psum("a_pskv", [128, 128])
        qkv_done = k.tok("act")
        for c in range(4):
            gc = hh * 4 + c
            for h in range(H):
                kfm = k_sb[:, h * 512 + c * 128:h * 512 + c * 128 + 128]
                vfm = v_sb[:, h * 512 + c * 128:h * 512 + c * 128 + 128]
                qfm = q_sb[:, h * TOK + gc * 128:h * TOK + gc * 128 + 128]
                t1 = k.op("dve", "tensor_mul", ksc[:, :], kfm,
                          kd_t[:, h * 128:(h + 1) * 128], deps=[qkv_done, k.tok("pe")])
                tr1 = k.op("pe", "transpose", ps_tr[:, :], ksc[:, :], C["ident_b"][:, :],
                           deps=[t1])
                e1 = k.op("dve", "tensor_copy", ksctm[:, :], ps_tr[:, :], deps=[tr1])
                tr2 = k.op("pe", "transpose", ps_tr[:, :], vfm, C["ident_b"][:, :],
                           deps=[e1])
                e2 = k.op("dve", "tensor_copy", vtm[:, :], ps_tr[:, :], deps=[tr2])
                mm1 = k.op("pe", "matmul", ps_sc[:, :], kfm, qfm, start=True, stop=True,
                           deps=[k.tok("dve")])
                d1 = k.op("dve", "tensor_mul", dsc[:, :], ps_sc[:, :],
                          dt_t[:, h * 128:(h + 1) * 128], deps=[mm1])
                d2 = k.op("dve", "tensor_mul", qsc[:, :], qfm,
                          qd_t[:, h * 128:(h + 1) * 128])
                mo1 = k.op("pe", "matmul", ps_o[:, :], vtm[:, :], dsc[:, :],
                           start=True, stop=False, deps=[d1, e2])
                mo2 = k.op("pe", "matmul", ps_o[:, :], S_b[:, h * 128:(h + 1) * 128],
                           qsc[:, :], start=False, stop=True, deps=[d2])
                eo = k.op("dve", "tensor_copy",
                          o_sb[:, h * TOK + gc * 128:h * TOK + gc * 128 + 128],
                          ps_o[:, :], deps=[mo2])
                mkv = k.op("pe", "matmul", ps_kv[:, :], ksctm[:, :], vtm[:, :],
                           start=True, stop=True, deps=[k.tok("dve")])
                su = k.op("dve", "scalar_tensor_tensor", S_f[:, h * 128:(h + 1) * 128],
                          S_f[:, h * 128:(h + 1) * 128], cdcol[:, h:h + 1],
                          ps_kv[:, :], ALU.mult, ALU.add, deps=[mkv, zS])
                k.op("dve", "tensor_copy", S_b[:, h * 128:(h + 1) * 128],
                     S_f[:, h * 128:(h + 1) * 128], deps=[su, zSb])
        SS.close()
        SH.close()

    tT = k.dma("sp", io["T_in"].rearrange("(n p) e -> p n e", p=128), S_f[:, :],
               deps=[k.tok("dve")])
    cT = k.cc("AllGather", ALU.bypass, [io["T_in"]], [io["T_out"]], deps=[tT])

    S6 = k.scope()
    S0f = S6.sbuf("a_S0f", [128, H * HD], F32)
    S0b = S6.sbuf("a_S0b", [128, H * HD], BF16)
    tr_sb = S6.sbuf("a_tr", [128, H * HD], F32)
    perow = S6.sbuf("a_perow", [1, 16], F32)
    pecol = S6.sbuf("a_pecol", [128, 16], F32)
    w16 = S6.sbuf("a_w16", [128, 16], F32)
    ps_pe = S6.psum("a_pspe", [128, 16])
    tpe = k.dma("pool", perow[:, :], io["pev"].rearrange("(o l) -> o l", o=1))
    mpe = k.op("pe", "matmul", ps_pe[:, :], C["ones_row_f"][:, :], perow[:, :],
               start=True, stop=True, deps=[tpe])
    epe = k.op("dve", "tensor_copy", pecol[:, :], ps_pe[:, :], deps=[mpe])
    z0 = k.op("dve", "memset", S0f[:, :], 0.0)
    ltok = None
    for r in range(NCORES):
        ltok = k.dma("sp", tr_sb[:, :],
                     io["T_out"][r * D:(r + 1) * D, :].rearrange("(n p) e -> p n e", p=128),
                     deps=[cT] + ([k.tok("dve")] if r > 0 else []))
        w1 = k.op("dve", "tensor_scalar", w16[:, :], ldcol[:, :], pecol[:, r:r + 1],
                  1024.0, ALU.mult, ALU.mult, deps=[epe])
        w2 = k.op("act", "activation", w16[:, :], w16[:, :], AF.Exp, deps=[w1])
        w3 = k.op("dve", "tensor_scalar", w16[:, :], w16[:, :], pecol[:, 8 + r:9 + r],
                  None, ALU.mult, deps=[w2])
        for h in range(H):
            k.op("dve", "scalar_tensor_tensor", S0f[:, h * 128:(h + 1) * 128],
                 tr_sb[:, h * 128:(h + 1) * 128], w16[:, h:h + 1],
                 S0f[:, h * 128:(h + 1) * 128], ALU.mult, ALU.add,
                 deps=[w3, ltok, z0])
    cb = k.op("dve", "tensor_copy", S0b[:, :], S0f[:, :])

    qdc = S6.sbuf("a_qdc", [128, TOK], F32)
    qq = S6.sbuf("a_qq", [128, TOK], BF16)
    ps_c = S6.psum("a_psc", [128, 512])
    for h in range(H):
        a1 = k.op("act", "activation", qdc[:, :], C["colp1_1024"][:, :], AF.Exp,
                  scale=ldcol[:, h:h + 1], deps=[k.tok("dve")])
        a2 = k.op("dve", "tensor_mul", qq[:, :], q_sb[:, h * TOK:(h + 1) * TOK],
                  qdc[:, :], deps=[a1])
        for j in (0, 512):
            mc = k.op("pe", "matmul", ps_c[:, :], S0b[:, h * 128:(h + 1) * 128],
                      qq[:, j:j + 512], start=True, stop=True,
                      deps=[a2, cb, k.tok("dve")])
            k.op("dve", "tensor_add", o_sb[:, h * TOK + j:h * TOK + j + 512],
                 o_sb[:, h * TOK + j:h * TOK + j + 512], ps_c[:, :], deps=[mc])
    S6.close()

    S7 = k.scope()
    wg1_sb = S7.sbuf("a_wg1", [128, NDT * 128], BF16)
    wg2_sb = S7.sbuf("a_wg2", [128, NDT * 128], BF16)
    nw_sb = S7.sbuf("a_nw", [128, 16], F32)
    glow = S7.sbuf("a_glow", [128, TOK], BF16)
    sig = S7.sbuf("a_sig", [128, TOK], BF16)
    osq = S7.sbuf("a_osq", [128, TOK], BF16)
    rrow = S7.sbuf("a_rrow", [1, TOK], F32)
    ps_g = S7.psum("a_psg", [128, TOK])
    ps_r = S7.psum("a_psr", [1, TOK])
    ps_b = S7.psum("a_psb", [128, TOK])
    tw1 = k.dma("pool", wg1_sb[:, :], io["wg1"].rearrange("(n p) f -> p n f", p=128))
    tw2 = k.dma("pool", wg2_sb[:, :], io["wg2"].rearrange("p (n f) -> p n f", f=128))
    tnw = k.dma("pool", nw_sb[:, :], io["normw"])
    for j in (0, 512):
        for kt in range(NDT):
            k.op("pe", "matmul", ps_g[:, j:j + 512], wg1_sb[:, kt * 128:(kt + 1) * 128],
                 x_sb[:, kt * TOK + j:kt * TOK + j + 512],
                 start=(kt == 0), stop=(kt == NDT - 1),
                 deps=[tw1, k.tok("dve")] if kt == 0 else [])
    gl = k.op("dve", "tensor_copy", glow[:, :], ps_g[:, :], deps=[k.tok("pe")])
    for h in range(H):
        for j in (0, 512):
            mg = k.op("pe", "matmul", ps_g[:, j:j + 512], wg2_sb[:, h * 128:(h + 1) * 128],
                      glow[:, j:j + 512], start=True, stop=True,
                      deps=[gl, tw2, k.tok("act"), k.tok("dve")] if j == 0 else [])
        sg1 = k.op("act", "activation", sig[:, :], ps_g[:, :], AF.Sigmoid,
                   deps=[k.tok("pe")])
        ho = o_sb[:, h * TOK:(h + 1) * TOK]
        t1 = k.op("dve", "tensor_mul", ho, ho, sig[:, :], deps=[sg1])
        s1 = k.op("act", "activation", osq[:, :], ho, AF.Square, deps=[t1])
        for j in (0, 512):
            k.op("pe", "matmul", ps_r[0:1, j:j + 512], C["ones_col_b"][:, :],
                 osq[:, j:j + 512], start=True, stop=True,
                 deps=[s1, k.tok("act"), k.tok("dve")] if j == 0 else [])
        l1 = k.op("act", "activation", rrow[:, :], ps_r[0:1, :], AF.Ln,
                  scale=1.0 / 128.0, bias=EPS, deps=[k.tok("pe")])
        l2 = k.op("act", "activation", rrow[:, :], rrow[:, :], AF.Exp, scale=-0.5)
        for j in (0, 512):
            k.op("pe", "matmul", ps_b[:, j:j + 512], C["ones_row_f"][:, :],
                 rrow[0:1, j:j + 512], start=True, stop=True,
                 deps=[l2, k.tok("dve")] if j == 0 else [])
        bb = k.tok("pe")
        t2 = k.op("dve", "tensor_mul", ho, ho, ps_b[:, :], deps=[bb])
        k.op("dve", "tensor_scalar", ho, ho, nw_sb[:, h:h + 1], None, ALU.mult,
             deps=[tnw])
    S7.close()

    S8 = k.scope()
    wo_sb = S8.sbuf("a_wo", [128, NDT * 128], BF16)
    ps_w = S8.psum("a_psw", [128, 512])
    for e in range(NDT):
        two = k.dma("pool", wo_sb[:, :],
                    io["wo"][:, e * 128:(e + 1) * 128].rearrange("(n p) f -> p n f", p=128),
                    deps=[k.tok("pe")])
        for j in (0, 512):
            for kt in range(NDT):
                k.op("pe", "matmul", ps_w[:, :], wo_sb[:, kt * 128:(kt + 1) * 128],
                     o_sb[:, kt * TOK + j:kt * TOK + j + 512],
                     start=(kt == 0), stop=(kt == NDT - 1),
                     deps=[two, k.tok("dve")] if kt == 0 else [])
            k.op("dve", "scalar_tensor_tensor",
                 x_sb[:, e * TOK + j:e * TOK + j + 512], ps_w[:, :], 1.0,
                 x_sb[:, e * TOK + j:e * TOK + j + 512], ALU.mult, ALU.add,
                 deps=[k.tok("pe")])
    S8.close()

    emit_layernorm(k, C, x_sb, x_sb, TOK, tag="a_ln")
    ot = k.dma("sp", fmr(io["x2"]), x_sb[:, :], deps=[k.tok("dve")])
    SB.close()
    return [ot]


def declare_l0_io(nc, kin, kout):
    io = {}
    io["x0"] = nc.dram_tensor("x0", [D, TOK], F32, kind=kin).ap()
    io["ld"] = nc.dram_tensor("ld", [H], F32, kind="ExternalInput").ap()
    for w in ("wq", "wk", "wv", "wo"):
        io[w] = nc.dram_tensor(w, [D, D], F32, kind="ExternalInput").ap()
    io["wg1"] = nc.dram_tensor("wg1", [D, 128], F32, kind="ExternalInput").ap()
    io["wg2"] = nc.dram_tensor("wg2", [128, D], F32, kind="ExternalInput").ap()
    io["normw"] = nc.dram_tensor("normw", [128, NDT], F32, kind="ExternalInput").ap()
    io["pev"] = nc.dram_tensor("pev", [16], F32, kind="ExternalInput").ap()
    io["T_in"] = nc.dram_tensor("T_in", [D, HD], F32, kind="Internal").ap()
    io["T_out"] = nc.dram_tensor("T_out", [NCORES * D, HD], F32, kind="Internal",
                                 addr_space=_ccsp("AllGather")).ap()
    io["x2"] = nc.dram_tensor("x2", [D, TOK], BF16, kind=kout).ap()
    return io


def build_l0_program():
    nc = bass.Bass("TRN2", target_bir_lowering=False, debug=False)
    io = declare_l0_io(nc, "ExternalInput", "ExternalOutput")
    cst = declare_constants(nc)
    with contextlib.ExitStack() as es:
        k = K(nc, es)
        with nc.Block() as block:
            S0 = k.scope()
            C = load_constants(k, S0, cst)
            outs = phase_l0_attn(k, C, io)
            k.wait("sp", outs)
            k.emit_all(block)
            S0.close()
    return nc


def stage_l0_inputs(inputs):
    cst = host_constants()
    x = inputs["L0_attn_x"][0]
    maps = []
    for c in range(NCORES):
        m = {("cst_" + n): v for n, v in cst.items()}
        m["x0"] = np.ascontiguousarray(x[c * TOK:(c + 1) * TOK, :].T)
        m["ld"] = inputs["L0_attn_log_decay"]
        m["wq"] = inputs["L0_attn_Wq"]
        m["wk"] = inputs["L0_attn_Wk"]
        m["wv"] = inputs["L0_attn_Wv"]
        m["wo"] = inputs["L0_attn_Wo"]
        m["wg1"] = inputs["L0_attn_Wg1"]
        m["wg2"] = inputs["L0_attn_Wg2"]
        m["normw"] = np.ascontiguousarray(inputs["L0_attn_norm_w"].reshape(16, 128).T)
        pev = np.zeros(16, dtype=np.float32)
        for r in range(NCORES):
            pev[r] = max(c - 1 - r, 0)
            pev[8 + r] = 1.0 if r < c else 0.0
        m["pev"] = pev
        maps.append(m)
    return maps





NKB_FULL = 56


def phase_l1_attn(k, C, io):
    nc = k.nc
    SB = k.scope()
    x_sb = SB.sbuf("c_x", [128, NDT * TOK], BF16)
    q_sb = SB.sbuf("c_q", [128, NDT * TOK], BF16)
    kf_sb = SB.sbuf("c_kf", [128, NDT * TOK], BF16)
    vtl_sb = SB.sbuf("c_vtl", [128, NDT * TOK], BF16)
    y_sb = q_sb
    nm = SB.sbuf("c_nm", [1, 64], F32)
    bat_sb = SB.sbuf("c_bat", [128, 48], F32)
    tx = k.dma("sp", x_sb[:, :], fmr(io["x3"]))
    tb = k.dma("sp", bat_sb[:, :], io["battn_c"])

    SQ = k.scope()
    wcol = [SQ.sbuf(f"c_wcol{j}", [128, NDT * 128], BF16) for j in range(2)]
    vtmp = SQ.sbuf("c_vtmp", [128, TOK], BF16)
    ps_p = [SQ.psum(f"c_psp{j}", [128, TOK]) for j in range(2)]
    ps_tr = SQ.psum("c_pstr", [128, 128], BF16)
    kin_toks = []
    vin_toks = []
    twl = [None] * 48
    mmdone = [None] * 48
    evtok = [None] * 48

    def load_qkv(n):
        twl[n] = k.dma("pool", wcol[n % 2][:, :],
                       io["wattn"][:, n * 128:(n + 1) * 128].rearrange(
                           "(n p) f -> p n f", p=128),
                       deps=[mmdone[n - 2]] if n >= 2 else [k.tok("pe")])

    load_qkv(0)
    load_qkv(1)
    for ft in range(48):
        h = ft % 16
        dst = vtmp if ft >= 32 else (q_sb if ft < 16 else kf_sb)
        for j in (0, 512):
            for kt in range(NDT):
                dep = []
                if kt == 0:
                    dep = [twl[ft], tx] + ([evtok[ft - 2]] if (ft >= 2 and j == 0)
                                           else [k.tok("dve")] if (ft < 2 and j == 0)
                                           else [])
                k.op("pe", "matmul", ps_p[ft % 2][:, j:j + 512],
                     wcol[ft % 2][:, kt * 128:(kt + 1) * 128],
                     x_sb[:, kt * TOK + j:kt * TOK + j + 512],
                     start=(kt == 0), stop=(kt == NDT - 1), deps=dep)
        mmdone[ft] = k.tok("pe")
        if ft + 2 < 48:
            load_qkv(ft + 2)
        for j in (0, 512):
            evtok[ft] = k.op("dve", "tensor_scalar",
                             dst[:, j:j + 512] if ft >= 32 else
                             dst[:, h * TOK + j:h * TOK + j + 512],
                             ps_p[ft % 2][:, j:j + 512], bat_sb[:, ft:ft + 1], None,
                             ALU.add, deps=[mmdone[ft], tb] if j == 0 else [])
        if 16 <= ft < 32:
            kin_toks.append(k.dma("sp", io["K_in"][h * 128:(h + 1) * 128, :],
                                  kf_sb[:, h * TOK:(h + 1) * TOK],
                                  deps=[k.tok("dve")]))
        if ft >= 32:
            for kb in range(8):
                tr = k.op("pe", "transpose", ps_tr[:, :],
                          vtmp[:, kb * 128:kb * 128 + 128], C["ident_b"][:, :],
                          deps=[k.tok("act"), k.tok("dve")])
                k.op("dve", "tensor_copy",
                     vtl_sb[:, h * TOK + kb * 128:h * TOK + kb * 128 + 128],
                     ps_tr[:, :], deps=[tr])
            vin_toks.append(k.dma("sp",
                                  io["V_in"][h * TOK:(h + 1) * TOK, :].rearrange(
                                      "(kb p) d -> p kb d", p=128),
                                  vtl_sb[:, h * TOK:(h + 1) * TOK],
                                  deps=[k.tok("dve")]))
    qsq = SQ.sbuf("c_qsq", [128, TOK], BF16)
    t11 = SQ.sbuf("c_t11", [1, 2], F32)
    for h in range(H):
        s1 = k.op("act", "activation", qsq[:, :], q_sb[:, h * TOK:(h + 1) * TOK],
                  AF.Square, deps=[k.tok("pe")])
        for j in (0, 512):
            m1 = k.op("pe", "matmul", ps_p[0][0:1, 0:512], C["ones_col_b"][:, :],
                      qsq[:, j:j + 512], start=True, stop=True,
                      deps=[s1, k.tok("dve")])
            r1 = k.op("dve", "reduce_max", t11[:, 0:1], ps_p[0][0:1, 0:512], AX.X, deps=[m1])
            if j == 0:
                k.op("dve", "tensor_copy", nm[:, h:h + 1], t11[:, 0:1])
            else:
                k.op("dve", "tensor_max", nm[:, h:h + 1], nm[:, h:h + 1], t11[:, 0:1])
    SQ.close()

    cK = k.cc("AllGather", ALU.bypass, [io["K_in"]], [io["K_out"]], deps=kin_toks)
    cV = k.cc("AllGather", ALU.bypass, [io["V_in"]], [io["V_out"]], deps=vin_toks)

    SA = k.scope()
    Kh = SA.sbuf("c_Kh", [128, 8 * TOK], BF16)
    Vh = SA.sbuf("c_Vh", [128, 64 * 128], BF16)
    ksq = SA.sbuf("c_ksq", [128, 512], BF16)
    probsT = [SA.sbuf(f"c_prt{j}", [128, TOK], BF16) for j in range(2)]
    bigm_bc = SA.sbuf("c_bigm", [128, 64], F32)
    bcol = SA.sbuf("c_bcol", [128, 64], F32)
    negb = SA.sbuf("c_negb", [128, 1], F32)
    rrow = SA.sbuf("c_rrow", [1, TOK], F32)
    rb = SA.sbuf("c_rb", [128, TOK], F32)
    bgr = SA.sbuf("c_bgr", [1, 64], F32)
    ps_sc = [SA.psum(f"c_pssc{j}", [128, TOK]) for j in range(2)]
    ps_sums = SA.psum("c_pssums", [1, TOK])
    ps_o = SA.psum("c_pso", [128, TOK])
    ps_bc = ps_sc[0]

    tbg = k.dma("sp", bgr[:, :], io["bigm"])
    mbg = k.op("pe", "matmul", ps_bc[0:128, 0:64], C["ones_row_f"][:, :], bgr[0:1, :],
               start=True, stop=True, deps=[tbg])
    k.op("dve", "tensor_copy", bigm_bc[:, :], ps_bc[:, 0:64], deps=[mbg])

    for h in range(H):
        hdep = [cK, cV, k.tok("pe"), k.tok("act"), k.tok("dve")]
        tk = [k.dma("sp", Kh[:, r * TOK:(r + 1) * TOK],
                    io["K_out"][r * D + h * 128:r * D + h * 128 + 128, :], deps=hdep)
              for r in range(NCORES)]
        tv = [k.dma("sp", Vh[:, r * 8 * 128:(r + 1) * 8 * 128],
                    io["V_out"][r * (NCORES * D) + h * TOK:
                                r * (NCORES * D) + (h + 1) * TOK, :].rearrange(
                        "(kb p) d -> p kb d", p=128), deps=hdep)
              for r in range(NCORES)]
        for j16 in range(16):
            s1 = k.op("act", "activation", ksq[:, :],
                      Kh[:, j16 * 512:(j16 + 1) * 512], AF.Square,
                      deps=tk + [k.tok("pe")])
            m1 = k.op("pe", "matmul", ps_bc[0:1, 0:512], C["ones_col_b"][:, :],
                      ksq[:, :], start=True, stop=True,
                      deps=[s1, k.tok("dve")])
            k.op("dve", "reduce_max", nm[:, 32:33], ps_bc[0:1, 0:512], AX.X, deps=[m1])
            if j16 == 0:
                k.op("dve", "tensor_copy", nm[:, 16 + h:17 + h], nm[:, 32:33])
            else:
                k.op("dve", "tensor_max", nm[:, 16 + h:17 + h], nm[:, 16 + h:17 + h],
                     nm[:, 32:33])
        k.op("dve", "tensor_mul", nm[:, 33:34], nm[:, h:h + 1], nm[:, 16 + h:17 + h])
        sq = k.op("act", "activation", nm[:, 34:35], nm[:, 33:34], AF.Sqrt,
                  scale=1.0 / 128.0, deps=[k.tok("dve")])
        mB = k.op("pe", "matmul", ps_bc[0:128, 0:1], C["ones_row_f"][:, :],
                  nm[0:1, 34:35], start=True, stop=True, deps=[sq, k.tok("dve")])
        k.op("dve", "tensor_scalar", negb[:, :], ps_bc[:, 0:1], -1.0, None, ALU.mult,
             deps=[mB])
        k.op("dve", "tensor_scalar", bcol[:, :], bigm_bc[:, :], -1.0,
             negb[:, 0:1], ALU.mult, ALU.add)
        exp_tok = [None] * 64
        sc_tok = [None] * 64
        oT_tok = [None] * 64

        def kb_src(kb):
            if kb < NKB_FULL:
                return (Kh[:, kb * 128:kb * 128 + 128],
                        Vh[:, kb * 128:kb * 128 + 128], bcol[:, kb:kb + 1], None)
            lkb = kb - NKB_FULL
            return (kf_sb[:, h * TOK + lkb * 128:h * TOK + lkb * 128 + 128],
                    vtl_sb[:, h * TOK + lkb * 128:h * TOK + lkb * 128 + 128],
                    negb[:, 0:1], C["lmask_b"][:, lkb * TOK:(lkb + 1) * TOK])

        def issue_scores(kb):
            p = kb % 2
            ksrc, _, _, _ = kb_src(kb)
            dep0 = [exp_tok[kb - 2]] if kb >= 2 else [k.tok("act"), k.tok("dve")]
            if kb == 0:
                dep0 += [tk[-1], tv[-1]]
            for j in (0, 512):
                sc_tok[kb] = k.op("pe", "matmul", ps_sc[p][:, j:j + 512], ksrc,
                     q_sb[:, h * TOK + j:h * TOK + j + 512], start=True, stop=True,
                     deps=dep0 if j == 0 else [])

        issue_scores(0)
        for kb in range(64):
            p = kb % 2
            _, vsrc, bc_, msk_ = kb_src(kb)
            dep_e = [sc_tok[kb]] + ([oT_tok[kb - 2]] if kb >= 2 else [])
            e1 = k.op("act", "activation", probsT[p][:, :], ps_sc[p][:, :], AF.Exp,
                      scale=INV_SQRT_HD, bias=bc_, deps=dep_e)
            if msk_ is not None:
                e1 = k.op("dve", "tensor_mul", probsT[p][:, :], probsT[p][:, :],
                          msk_, deps=[e1])
            exp_tok[kb] = e1
            if kb + 1 < 64:
                issue_scores(kb + 1)
            last = (kb == 63)
            st = (kb == 0)
            for j in (0, 512):
                k.op("pe", "matmul", ps_sums[0:1, j:j + 512], C["ones_col_b"][:, :],
                     probsT[p][:, j:j + 512], start=st, stop=last,
                     deps=[e1] if j == 0 else [])
                oT_tok[kb] = k.op("pe", "matmul", ps_o[:, j:j + 512], vsrc,
                     probsT[p][:, j:j + 512], start=st, stop=last)
        mm = k.tok("pe")
        r1 = k.op("dve", "reciprocal", rrow[:, :], ps_sums[0:1, :], deps=[mm])
        for j in (0, 512):
            k.op("pe", "matmul", ps_bc[:, j:j + 512], C["ones_row_f"][:, :],
                 rrow[0:1, j:j + 512], start=True, stop=True,
                 deps=[r1, k.tok("dve")] if j == 0 else [])
        e2 = k.op("dve", "tensor_copy", rb[:, :], ps_bc[:, :], deps=[k.tok("pe")])
        k.op("dve", "tensor_mul", y_sb[:, h * TOK:(h + 1) * TOK], ps_o[:, :],
             rb[:, :], deps=[e2])
    SA.close()

    S8 = k.scope()
    wo_sb = S8.sbuf("c_wo", [128, NDT * 128], BF16)
    bp_sb = S8.sbuf("c_bp", [128, 16], F32)
    ps_w = S8.psum("c_psw", [128, 512])
    tbp = k.dma("sp", bp_sb[:, :], io["bproj_c"])
    for e in range(NDT):
        two = k.dma("pool", wo_sb[:, :],
                    io["wproj"][:, e * 128:(e + 1) * 128].rearrange(
                        "(n p) f -> p n f", p=128),
                    deps=[k.tok("pe")])
        for j in (0, 512):
            for kt in range(NDT):
                k.op("pe", "matmul", ps_w[:, :], wo_sb[:, kt * 128:(kt + 1) * 128],
                     y_sb[:, kt * TOK + j:kt * TOK + j + 512],
                     start=(kt == 0), stop=(kt == NDT - 1),
                     deps=[two, k.tok("dve")] if kt == 0 else [])
            k.op("dve", "scalar_tensor_tensor",
                 x_sb[:, e * TOK + j:e * TOK + j + 512], ps_w[:, :],
                 bp_sb[:, e:e + 1], x_sb[:, e * TOK + j:e * TOK + j + 512],
                 ALU.add, ALU.add, deps=[k.tok("pe"), tbp])
    S8.close()

    emit_layernorm(k, C, x_sb, x_sb, TOK, tag="c_ln")
    ot = k.dma("sp", fmr(io["x4"]), x_sb[:, :], deps=[k.tok("dve")])
    SB.close()
    return [ot]


def declare_l1a_io(nc, kin, kout, x3_ap=None):
    io = {}
    io["x3"] = x3_ap if x3_ap is not None else nc.dram_tensor("x3", [D, TOK], BF16, kind=kin).ap()
    io["wattn"] = nc.dram_tensor("wattn", [D, DQKV], F32, kind="ExternalInput").ap()
    io["battn_c"] = nc.dram_tensor("battn_c", [128, 48], F32, kind="ExternalInput").ap()
    io["wproj"] = nc.dram_tensor("wproj", [D, D], F32, kind="ExternalInput").ap()
    io["bproj_c"] = nc.dram_tensor("bproj_c", [128, 16], F32, kind="ExternalInput").ap()
    io["bigm"] = nc.dram_tensor("bigm", [1, 64], F32, kind="ExternalInput").ap()
    io["K_in"] = nc.dram_tensor("K_in", [D, TOK], BF16, kind="Internal").ap()
    io["V_in"] = nc.dram_tensor("V_in", [NDT * TOK, 128], BF16, kind="Internal").ap()
    sp = _ccsp("AllGather")
    io["K_out"] = nc.dram_tensor("K_out", [NCORES * D, TOK], BF16, kind="Internal",
                                 addr_space=sp).ap()
    io["V_out"] = nc.dram_tensor("V_out", [NCORES * NDT * TOK, 128], BF16,
                                 kind="Internal", addr_space=sp).ap()
    io["x4"] = nc.dram_tensor("x4", [D, TOK], BF16, kind=kout).ap()
    return io


def build_l1a_program():
    nc = bass.Bass("TRN2", target_bir_lowering=False, debug=False)
    io = declare_l1a_io(nc, "ExternalInput", "ExternalOutput")
    cst = declare_constants(nc)
    with contextlib.ExitStack() as es:
        k = K(nc, es)
        with nc.Block() as block:
            S0 = k.scope()
            C = load_constants(k, S0, cst)
            outs = phase_l1_attn(k, C, io)
            k.wait("sp", outs)
            k.emit_all(block)
            S0.close()
    return nc


def stage_l1a_inputs(inputs, x3_percore):
    cst = host_constants()
    maps = []
    for c in range(NCORES):
        m = {("cst_" + n): v for n, v in cst.items()}
        if x3_percore is not None:
            m["x3"] = x3_percore[c]
        m["wattn"] = inputs["L1_attn_W_attn"]
        m["battn_c"] = np.ascontiguousarray(inputs["L1_attn_b_attn"].reshape(48, 128).T)
        m["wproj"] = inputs["L1_attn_W_proj"]
        m["bproj_c"] = np.ascontiguousarray(inputs["L1_attn_b_proj"].reshape(16, 128).T)
        bigm = np.zeros((1, 64), dtype=np.float32)
        bigm[0, c * 8:] = 1e30
        m["bigm"] = bigm
        maps.append(m)
    return maps



def phase_l1_ffn(k, C, io):
    nc = k.nc
    SB = k.scope()
    x_sb = SB.sbuf("d_x", [128, NDT * TOK], BF16)
    cmb_t = SB.sbuf("d_cmb", [128, NE * TOK], BF16)
    t1s = SB.sbuf("d_t1s", [16, NE * TOK], BF16)
    t2s = SB.sbuf("d_t2s", [16, NE * TOK], BF16)
    csr = SB.sbuf("d_csr", [1, TOK], F32)
    a1t_sb = SB.sbuf("d_a1t", [128, NDT * 128], BF16)
    a2t_sb = SB.sbuf("d_a2t", [128, NFC * 128], BF16)
    b2t_sb = SB.sbuf("d_b2t", [16, NE * D], BF16)
    b1c_sb = SB.sbuf("d_b1c", [128, NFC], F32)
    b2r_sb = SB.sbuf("d_b2r", [1, D], F32)
    tx = k.dma("sp", x_sb[:, :], fmr(io["x4"]))
    ta1 = k.dma("pool", a1t_sb[:, :], io["a1t"].rearrange("(n p) c -> p n c", p=128))
    ta2 = k.dma("pool", a2t_sb[:, :], io["a2t"].rearrange("(n p) c -> p n c", p=128))
    tb2 = k.dma("pool", b2t_sb[:, :], io["b2t"])
    tb1c = k.dma("sp", b1c_sb[:, :], io["b1c"])
    tb2r = k.dma("sp", b2r_sb[:, :], io["b2r"])

    SR = k.scope()
    wr_sb = SR.sbuf("d_wr", [128, NDT * 8], BF16)
    brb = SR.sbuf("d_brb", [128, 8], F32)
    lg = SR.sbuf("d_lg", [128, 8], F32)
    prb = SR.sbuf("d_prb", [128, 8], F32)
    p2 = SR.sbuf("d_p2", [128, 8], F32)
    eqm = SR.sbuf("d_eqm", [128, 8], F32)
    eqi = SR.sbuf("d_eqi", [128, 8], I32)
    mrow = SR.sbuf("d_mrow", [128, 4], F32)
    cmb_tm = SR.sbuf("d_cmbtm", [128, 8], F32)
    cs_tm = SR.sbuf("d_cstm", [128, 1], F32)
    cmbT_sb = SR.sbuf("d_cmbr", [8, TOK], BF16)
    ps_r = SR.psum("d_psr", [128, 8])
    ps_t = SR.psum("d_pst", [128, 128])
    ps_cb = SR.psum("d_pscb", [128, TOK])
    twr = k.dma("pool", wr_sb[:, :], io["wr"].rearrange("(n p) e -> p n e", p=128))
    tbr = k.dma("sp", brb[0:1, :], io["br"])
    mbr = k.op("pe", "matmul", ps_t[0:128, 0:8], C["ones_row_f"][:, :], brb[0:1, :],
               start=True, stop=True, deps=[tbr])
    k.op("dve", "tensor_copy", brb[:, :], ps_t[:, 0:8], deps=[mbr])
    for tc in range(8):
        for kt in range(NDT):
            k.op("pe", "matmul", ps_r[:, :],
                 x_sb[:, kt * TOK + tc * 128:kt * TOK + tc * 128 + 128],
                 wr_sb[:, kt * 8:(kt + 1) * 8], start=(kt == 0), stop=(kt == NDT - 1),
                 deps=[twr, tx, k.tok("dve")] if kt == 0 else [])
        k.op("dve", "tensor_add", lg[:, :], ps_r[:, :], brb[:, :], deps=[k.tok("pe")])
        k.op("dve", "reduce_max", mrow[:, 0:1], lg[:, :], AX.X)
        k.op("dve", "tensor_scalar", mrow[:, 1:2], mrow[:, 0:1], -1.0, None, ALU.mult)
        k.op("act", "activation", prb[:, :], lg[:, :], AF.Exp, bias=mrow[:, 1:2],
             accum_out=mrow[:, 2:3], deps=[k.tok("dve")])
        k.op("dve", "reciprocal", mrow[:, 3:4], mrow[:, 2:3], deps=[k.tok("act")])
        k.op("dve", "tensor_scalar", prb[:, :], prb[:, :], mrow[:, 3:4], None, ALU.mult)
        k.op("dve", "reduce_max", mrow[:, 0:1], prb[:, :], AX.X)
        k.op("dve", "tensor_scalar", eqi[:, :], prb[:, :], mrow[:, 0:1], None,
             ALU.is_ge)
        k.op("dve", "tensor_copy", p2[:, :], prb[:, :])
        k.op("dve", "copy_predicated", p2[:, :], eqi[:, :], C["neg1_8"][:, :])
        k.op("dve", "reduce_max", mrow[:, 1:2], p2[:, :], AX.X)
        k.op("dve", "tensor_scalar", eqm[:, :], prb[:, :], mrow[:, 1:2], None,
             ALU.is_ge)
        k.op("dve", "tensor_mul", cmb_tm[:, :], prb[:, :], eqm[:, :])
        k.op("dve", "tensor_add", cs_tm[:, :], mrow[:, 0:1], mrow[:, 1:2])
        tr1 = k.op("pe", "transpose", ps_t[0:8, 0:128], cmb_tm[:, :],
                   C["ident_f"][:, :], deps=[k.tok("dve")])
        k.op("dve", "tensor_copy", cmbT_sb[:, tc * 128:(tc + 1) * 128],
             ps_t[0:8, 0:128], deps=[tr1])
        tr2 = k.op("pe", "transpose", ps_t[0:1, 0:128], cs_tm[:, :],
                   C["ident_f"][:, :], deps=[k.tok("dve")])
        k.op("dve", "tensor_copy", csr[:, tc * 128:(tc + 1) * 128], ps_t[0:1, 0:128],
             deps=[tr2])
    for e in range(NE):
        for j in (0, 512):
            k.op("pe", "matmul", ps_cb[:, j:j + 512],
                 C["sel8_b"][:, e * 128:(e + 1) * 128],
                 cmbT_sb[0:8, j:j + 512], start=True, stop=True,
                 deps=[k.tok("dve")] if j == 0 else [])
        k.op("dve", "tensor_copy", cmb_t[:, e * TOK:(e + 1) * TOK], ps_cb[:, :],
             deps=[k.tok("pe")])
    SR.close()

    ST = k.scope()
    ps_t1 = ST.psum("d_pst1", [128, TOK])
    for i in range(4):
        e0, e1 = 2 * i, 2 * i + 1
        for dt in range(NDT):
            st = (dt == 0)
            sp_ = (dt == NDT - 1)
            for j in (0, 512):
                dep = [ta1, tx, k.tok("dve")] if (dt == 0 and j == 0) else []
                k.op("pe", "matmul", ps_t1[0:16, j:j + 512],
                     a1t_sb[:, dt * 128 + e0 * 16:dt * 128 + e0 * 16 + 16],
                     x_sb[:, dt * TOK + j:dt * TOK + j + 512],
                     start=st, stop=sp_, tile_position=(0, 0), deps=dep)
                k.op("pe", "matmul", ps_t1[64:80, j:j + 512],
                     a1t_sb[:, dt * 128 + e1 * 16:dt * 128 + e1 * 16 + 16],
                     x_sb[:, dt * TOK + j:dt * TOK + j + 512],
                     start=st, stop=sp_, tile_position=(0, 64))
        mm = k.tok("pe")
        k.op("dve", "tensor_scalar", t1s[:, e0 * TOK:(e0 + 1) * TOK],
             ps_t1[0:16, :], 2.0, None, ALU.mult, deps=[mm])
        k.op("dve", "tensor_scalar", t1s[:, e1 * TOK:(e1 + 1) * TOK],
             ps_t1[64:80, :], 2.0, None, ALU.mult)
    ST.close()

    SM = k.scope()
    NBW = 2
    w1c = [SM.sbuf(f"d_w1c{j}", [128, NDT * 128], BF16) for j in range(NBW)]
    b1tc = [SM.sbuf(f"d_b1tc{j}", [16, NE * 128], BF16) for j in range(NBW)]
    base_sb = SM.sbuf("d_base", [128, 512], BF16)
    a_sb = [SM.sbuf(f"d_a{j}", [128, 512], BF16) for j in range(2)]
    ca_sb = SM.sbuf("d_ca", [128, 512], BF16)
    macc = SM.sbuf("d_macc", [128, 512], BF16)
    ps_base = SM.psum("d_psbase", [128, 512])
    ps_e = [SM.psum(f"d_pse{j}", [128, 512]) for j in range(2)]
    t2ps = [SM.psum(f"d_t2ps{i}", [128, 512]) for i in range(4)]
    b1t_r = io["b1t"].rearrange("p (e f) -> p e f", f=FF)

    last_mix = [None]
    for hh in range(2):
        t0 = hh * 512
        wtok = [None] * NFC
        mixtok = [None] * NFC

        def load_fc(fc, hh=hh, wtok=wtok, mixtok=mixtok):
            b = fc % NBW
            deps = ([mixtok[fc - NBW]] if fc >= NBW else []) + [k.tok("pe")]
            t1_ = k.dma("pool", w1c[b][:, :],
                        io["w1"][:, fc * 128:(fc + 1) * 128].rearrange(
                            "(n p) f -> p n f", p=128), deps=deps)
            t2_ = k.dma("pool", b1tc[b][:, :], b1t_r[:, :, fc * 128:(fc + 1) * 128],
                        deps=deps)
            wtok[fc] = (t1_, t2_)

        for fc in range(NBW):
            load_fc(fc)
        for fc in range(NFC):
            b = fc % NBW
            for kt in range(NDT):
                k.op("pe", "matmul", ps_base[:, :], w1c[b][:, kt * 128:(kt + 1) * 128],
                     x_sb[:, kt * TOK + t0:kt * TOK + t0 + 512],
                     start=(kt == 0), stop=(kt == NDT - 1),
                     deps=[wtok[fc][0], k.tok("act"), k.tok("dve")] if kt == 0 else [])
            bs = k.op("act", "activation", base_sb[:, :], ps_base[:, :], AF.Copy,
                      deps=[k.tok("pe")])
            ge_tok = [None] * NE
            ca_tok = [None] * NE
            pre_tok = [None] * NE
            t2_tok = [None] * NE

            def issue_pre(e):
                pp = ps_e[e % 2]
                dep = [ge_tok[e - 2]] if e >= 2 else [k.tok("act"), k.tok("dve")]
                if e == 0:
                    dep += [wtok[fc][1], bs]
                k.op("pe", "matmul", pp[:, :], b1tc[b][:, e * 128:(e + 1) * 128],
                     t1s[0:16, e * TOK + t0:e * TOK + t0 + 512],
                     start=True, stop=False, deps=dep)
                pre_tok[e] = k.op("pe", "matmul", pp[:, :], C["ident_b"][:, :],
                                  base_sb[:, :], start=False, stop=True,
                                  deps=[bs] if e < 2 else [])

            issue_pre(0)
            for e in range(NE):
                i, ro = e // 2, 64 * (e % 2)
                dep_g = [pre_tok[e], tb1c]
                if e >= 2:
                    dep_g += [ca_tok[e - 2], t2_tok[e - 2]]
                ge = k.op("act", "activation", a_sb[e % 2][:, :], ps_e[e % 2][:, :],
                          AF.Gelu_apprx_tanh, bias=b1c_sb[:, fc:fc + 1], deps=dep_g)
                ge_tok[e] = ge
                if e + 1 < NE:
                    issue_pre(e + 1)
                t2_tok[e] = k.op("pe", "matmul", t2ps[i][ro:ro + 16, :],
                     a2t_sb[:, fc * 128 + e * 16:fc * 128 + e * 16 + 16],
                     a_sb[e % 2][:, :], start=(fc == 0), stop=(fc == NFC - 1),
                     tile_position=(0, ro), skip_group_check=True,
                     deps=[ge, ta2])
                ca_tok[e] = k.op("dve", "tensor_mul", ca_sb[:, :], a_sb[e % 2][:, :],
                     cmb_t[:, e * TOK + t0:e * TOK + t0 + 512], deps=[ge])
                if e == 0:
                    k.op("dve", "tensor_copy", macc[:, :], ca_sb[:, :],
                         deps=[mixtok[fc - 1] if fc > 0 else last_mix[0]])
                else:
                    k.op("dve", "tensor_add", macc[:, :], macc[:, :], ca_sb[:, :])
            mixtok[fc] = k.dma("sp", io["mix_d"][fc * 128:(fc + 1) * 128,
                                                 t0:t0 + 512], macc[:, :],
                               deps=[k.tok("dve")])
            if fc + NBW < NFC:
                load_fc(fc + NBW)
        last_mix[0] = mixtok[NFC - 1]
        for e in range(NE):
            i, ro = e // 2, 64 * (e % 2)
            k.op("dve", "scalar_tensor_tensor",
                 t2s[0:16, e * TOK + t0:e * TOK + t0 + 512],
                 t2ps[i][ro:ro + 16, :], 2.0,
                 cmb_t[0:16, e * TOK + t0:e * TOK + t0 + 512],
                 ALU.mult, ALU.mult, deps=[k.tok("pe")])
    SM.close()

    SW = k.scope()
    NBW2 = 2
    NFG = NFC // 4
    mixg = [SW.sbuf(f"d_mixg{j}", [128, 4 * TOK], BF16) for j in range(NBW2)]
    w2g = [SW.sbuf(f"d_w2g{j}", [128, 4 * 512], BF16) for j in range(NBW2)]
    ps_w = [SW.psum(f"d_psw{j}", [128, TOK]) for j in range(4)]
    for dcg in range(4):
        mtok = [None] * NFG
        wtok2 = [None] * NFG

        def load_w2(g, dcg=dcg, mtok=mtok, wtok2=wtok2):
            b = g % NBW2
            deps = [k.tok("pe")] if g >= NBW2 else [k.tok("dve"), k.tok("act")]
            mtok[g] = k.dma("sp", mixg[b][:, :],
                            io["mix_d"][g * 512:(g + 1) * 512, :].rearrange(
                                "(n p) t -> p n t", p=128), deps=deps)
            wtok2[g] = k.dma("pool", w2g[b][:, :],
                             io["w2"][g * 512:(g + 1) * 512,
                                      dcg * 512:(dcg + 1) * 512].rearrange(
                                 "(n p) c -> p n c", p=128), deps=deps)

        for g in range(NBW2):
            load_w2(g)
        for g in range(NFG):
            b = g % NBW2
            for fi in range(4):
                fc = g * 4 + fi
                for dj in range(4):
                    for j in (0, 512):
                        k.op("pe", "matmul", ps_w[dj][:, j:j + 512],
                             w2g[b][:, fi * 512 + dj * 128:fi * 512 + dj * 128 + 128],
                             mixg[b][:, fi * TOK + j:fi * TOK + j + 512],
                             start=(fc == 0), stop=False,
                             deps=[mtok[g], wtok2[g]]
                             if (fi == 0 and dj == 0 and j == 0) else [])
            if g + NBW2 < NFG:
                load_w2(g + NBW2)
        for dj in range(4):
            dc = dcg * 4 + dj
            for e in range(NE):
                for j in (0, 512):
                    k.op("pe", "matmul", ps_w[dj][:, j:j + 512],
                         b2t_sb[:, e * D + dc * 128:e * D + dc * 128 + 128],
                         t2s[0:16, e * TOK + j:e * TOK + j + 512],
                         start=False, stop=False, deps=[tb2, k.tok("dve")] if (e == 0 and j == 0) else [])
            for j in (0, 512):
                k.op("pe", "matmul", ps_w[dj][:, j:j + 512],
                     b2r_sb[0:1, dc * 128:dc * 128 + 128], csr[0:1, j:j + 512],
                     start=False, stop=True, deps=[tb2r] if j == 0 else [])
            for j in (0, 512):
                k.op("dve", "scalar_tensor_tensor",
                     x_sb[:, dc * TOK + j:dc * TOK + j + 512], ps_w[dj][:, j:j + 512],
                     1.0, x_sb[:, dc * TOK + j:dc * TOK + j + 512],
                     ALU.mult, ALU.add, deps=[k.tok("pe")])
    SW.close()

    stoks = emit_layernorm(k, C, x_sb, None, TOK, tag="d_ln", dram_out=io["x5"])
    SB.close()
    return stoks





def _ccsp(kind):
    from concourse.replica_groups import maybe_share_collective_output_space
    return maybe_share_collective_output_space(kind, RG, prefer_local=True)


def declare_ffn0_io(nc, kin, kout, x2_ap=None):
    io = {}
    io["x2"] = x2_ap if x2_ap is not None else nc.dram_tensor("x2", [D, TOK], BF16, kind=kin).ap()
    io["lut"] = nc.dram_tensor("lut", [LUTN], F32, kind="ExternalInput").ap()
    io["gidxT_s"] = nc.dram_tensor("gidxT_s", [GHR, INTER], I32, kind="ExternalInput").ap()
    io["uidxT_s"] = nc.dram_tensor("uidxT_s", [GHR, INTER], I32, kind="ExternalInput").ap()
    io["didxT_s"] = nc.dram_tensor("didxT_s", [DHR, D], I32, kind="ExternalInput").ap()
    io["g_in"] = nc.dram_tensor("g_in", [GHR, INTER], BF16, kind="Internal").ap()
    io["u_in"] = nc.dram_tensor("u_in", [GHR, INTER], BF16, kind="Internal").ap()
    io["d_in"] = nc.dram_tensor("d_in", [DHR, D], BF16, kind="Internal").ap()
    sp = _ccsp("AllGather")
    io["g_out"] = nc.dram_tensor("g_out", [D, INTER], BF16, kind="Internal", addr_space=sp).ap()
    io["u_out"] = nc.dram_tensor("u_out", [D, INTER], BF16, kind="Internal", addr_space=sp).ap()
    io["d_out"] = nc.dram_tensor("d_out", [INTER, D], BF16, kind="Internal", addr_space=sp).ap()
    io["x3"] = nc.dram_tensor("x3", [D, TOK], BF16, kind=kout).ap()
    return io


def build_ffn0_program():
    nc = bass.Bass("TRN2", target_bir_lowering=False, debug=False)
    io = declare_ffn0_io(nc, "ExternalInput", "ExternalOutput")
    cst = declare_constants(nc)
    with contextlib.ExitStack() as es:
        k = K(nc, es)
        with nc.Block() as block:
            S0 = k.scope()
            C = load_constants(k, S0, cst)
            SGb = phase_ffn0_build(k, C, io)
            outs = phase_ffn0(k, C, io)
            SGb.close()
            k.wait("sp", outs)
            k.emit_all(block)
            S0.close()
    return nc







def declare_l1f_io(nc, kin, kout, x4_ap=None):
    io = {}
    io["x4"] = x4_ap if x4_ap is not None else nc.dram_tensor("x4", [D, TOK], BF16, kind=kin).ap()
    io["wr"] = nc.dram_tensor("wr", [D, NE], F32, kind="ExternalInput").ap()
    io["br"] = nc.dram_tensor("br", [1, NE], F32, kind="ExternalInput").ap()
    io["w1"] = nc.dram_tensor("w1", [D, FF], F32, kind="ExternalInput").ap()
    io["b1c"] = nc.dram_tensor("b1c", [128, NFC], F32, kind="ExternalInput").ap()
    io["w2"] = nc.dram_tensor("w2", [FF, D], F32, kind="ExternalInput").ap()
    io["b2r"] = nc.dram_tensor("b2r", [1, D], F32, kind="ExternalInput").ap()
    io["a1t"] = nc.dram_tensor("a1t", [D, 128], F32, kind="ExternalInput").ap()
    io["a2t"] = nc.dram_tensor("a2t", [FF, 128], F32, kind="ExternalInput").ap()
    io["b1t"] = nc.dram_tensor("b1t", [16, NE * FF], F32, kind="ExternalInput").ap()
    io["b2t"] = nc.dram_tensor("b2t", [16, NE * D], F32, kind="ExternalInput").ap()
    io["mix_d"] = nc.dram_tensor("mix_d", [FF, TOK], BF16, kind="Internal").ap()
    io["x5"] = nc.dram_tensor("x5", [D, TOK], F32, kind=kout).ap()
    return io


def build_l1f_program():
    nc = bass.Bass("TRN2", target_bir_lowering=False, debug=False)
    io = declare_l1f_io(nc, "ExternalInput", "ExternalOutput")
    cst = declare_constants(nc)
    with contextlib.ExitStack() as es:
        k = K(nc, es)
        with nc.Block() as block:
            S0 = k.scope()
            C = load_constants(k, S0, cst)
            outs = phase_l1_ffn(k, C, io)
            k.wait("sp", outs)
            k.emit_all(block)
            S0.close()
    return nc


def stage_l1f_inputs(inputs, x4_percore):
    cst = host_constants()
    A1 = inputs["L1_ffn_A1"]; B1 = inputs["L1_ffn_B1"]
    A2 = inputs["L1_ffn_A2"]; B2 = inputs["L1_ffn_B2"]
    maps = []
    for c in range(NCORES):
        m = {("cst_" + n): v for n, v in cst.items()}
        if x4_percore is not None:
            m["x4"] = x4_percore[c]
        m["wr"] = inputs["L1_ffn_Wr"]
        m["br"] = inputs["L1_ffn_br"][None, :]
        m["w1"] = inputs["L1_ffn_W1"]
        m["b1c"] = np.ascontiguousarray(inputs["L1_ffn_b1"].reshape(NFC, 128).T)
        m["w2"] = inputs["L1_ffn_W2"]
        m["b2r"] = inputs["L1_ffn_b2"][None, :]
        m["a1t"] = np.ascontiguousarray(A1.transpose(2, 0, 1).reshape(D, 128))
        m["a2t"] = np.ascontiguousarray(A2.transpose(2, 0, 1).reshape(FF, 128))
        m["b1t"] = np.ascontiguousarray(B1.transpose(2, 0, 1).reshape(16, NE * FF))
        m["b2t"] = np.ascontiguousarray(B2.transpose(2, 0, 1).reshape(16, NE * D))
        maps.append(m)
    return maps


def build_ab_program():
    nc = bass.Bass("TRN2", target_bir_lowering=False, debug=False)
    ioa = declare_l0_io(nc, "ExternalInput", "Internal")
    iob = declare_ffn0_io(nc, "Internal", "ExternalOutput", x2_ap=ioa["x2"])
    cst = declare_constants(nc)
    with contextlib.ExitStack() as es:
        k = K(nc, es)
        with nc.Block() as block:
            S0 = k.scope()
            C = load_constants(k, S0, cst)
            outs_a = phase_l0_attn(k, C, ioa)
            k.fence = [(n, s[1]) for n, s in k.sems.items()]
            outs_b = phase_ffn0(k, C, iob)
            k.wait("sp", outs_a + outs_b)
            k.emit_all(block)
            S0.close()
    return nc


def stage_ffn0_inputs(inputs, x2_percore):
    cst = host_constants()
    gT = np.ascontiguousarray(inputs["L0_ffn_gate_idx"].T)
    uT = np.ascontiguousarray(inputs["L0_ffn_up_idx"].T)
    dT = np.ascontiguousarray(inputs["L0_ffn_down_idx"].T)
    maps = []
    for c in range(NCORES):
        m = {("cst_" + n): v for n, v in cst.items()}
        m["x2"] = x2_percore[c]
        m["lut"] = inputs["L0_ffn_lut"]
        m["gidxT_s"] = np.ascontiguousarray(gT[c * GHR:(c + 1) * GHR, :])
        m["uidxT_s"] = np.ascontiguousarray(uT[c * GHR:(c + 1) * GHR, :])
        m["didxT_s"] = np.ascontiguousarray(dT[c * DHR:(c + 1) * DHR, :])
        maps.append(m)
    return maps


def kernel(**inputs):
    raise NotImplementedError("under construction")


FUSED = True


def build_full_program(expose=False):
    nc = bass.Bass("TRN2", target_bir_lowering=False, debug=False)
    kh = "ExternalOutput" if expose else "Internal"
    ioa = declare_l0_io(nc, "ExternalInput", kh)
    iob = declare_ffn0_io(nc, kh, kh, x2_ap=ioa["x2"])
    ioc = declare_l1a_io(nc, kh, kh, x3_ap=iob["x3"])
    iod = declare_l1f_io(nc, kh, "ExternalOutput", x4_ap=ioc["x4"])
    cst = declare_constants(nc)
    with contextlib.ExitStack() as es:
        k = K(nc, es)
        with nc.Block() as block:
            S0 = k.scope()
            C = load_constants(k, S0, cst)
            SGb = phase_ffn0_build(k, C, iob)
            phase_l0_attn(k, C, ioa)
            k.fence = [(n, s[1]) for n, s in k.sems.items()]
            phase_ffn0(k, C, iob)
            SGb.close()
            k.fence = [(n, s[1]) for n, s in k.sems.items()]
            phase_l1_attn(k, C, ioc)
            k.fence = [(n, s[1]) for n, s in k.sems.items()]
            outs = phase_l1_ffn(k, C, iod)
            k.wait("sp", outs)
            k.emit_all(block)
            S0.close()
    return nc


def stage_all_inputs(inputs):
    ma = stage_l0_inputs(inputs)
    mb_ = stage_ffn0_inputs(inputs, [None] * NCORES)
    mc = stage_l1a_inputs(inputs, None)
    md = stage_l1f_inputs(inputs, None)
    maps = []
    for c in range(NCORES):
        m = dict(ma[c])
        for mm in (mb_[c], mc[c], md[c]):
            mm = dict(mm)
            mm.pop("x2", None)
            m.update(mm)
        maps.append(m)
    return maps


_PROG_CACHE = {}


def kernel(**inputs):
    import ml_dtypes
    if FUSED:
        if "full" not in _PROG_CACHE:
            _PROG_CACHE["full"] = build_full_program()
        nc = _PROG_CACHE["full"]
        maps = stage_all_inputs(inputs)
        res = run_bass_kernel_spmd(nc, maps, core_ids=list(range(NCORES)))
        xs = [np.asarray(res.results[c]["x5"]).astype(np.float32).T for c in range(NCORES)]
    else:
        if "a" not in _PROG_CACHE:
            _PROG_CACHE["a"] = build_l0_program()
            _PROG_CACHE["b"] = build_ffn0_program()
            _PROG_CACHE["c"] = build_l1a_program()
            _PROG_CACHE["d"] = build_l1f_program()
        maps = stage_l0_inputs(inputs)
        res = run_bass_kernel_spmd(_PROG_CACHE["a"], maps, core_ids=list(range(NCORES)))
        x2 = [np.asarray(res.results[c]["x2"]) for c in range(NCORES)]
        maps = stage_ffn0_inputs(inputs, x2)
        res = run_bass_kernel_spmd(_PROG_CACHE["b"], maps, core_ids=list(range(NCORES)))
        x3 = [np.asarray(res.results[c]["x3"]) for c in range(NCORES)]
        maps = stage_l1a_inputs(inputs, x3)
        res = run_bass_kernel_spmd(_PROG_CACHE["c"], maps, core_ids=list(range(NCORES)))
        x4 = [np.asarray(res.results[c]["x4"]) for c in range(NCORES)]
        maps = stage_l1f_inputs(inputs, x4)
        res = run_bass_kernel_spmd(_PROG_CACHE["d"], maps, core_ids=list(range(NCORES)))
        xs = [np.asarray(res.results[c]["x5"]).astype(np.float32).T for c in range(NCORES)]
    out = np.concatenate(xs, axis=0)[None, :, :]
    return np.ascontiguousarray(out)


# version 09 (final)
# speedup vs baseline: 1.0394x; 1.0072x over previous
import contextlib
import numpy as np

import concourse.bass as bass
import concourse.mybir as mybir
from concourse.bass_utils import run_bass_kernel_spmd

F32 = mybir.dt.float32
BF16 = mybir.dt.bfloat16
I32 = mybir.dt.int32
AF = mybir.ActivationFunctionType
ALU = mybir.AluOpType
AX = mybir.AxisListType

NCORES = 8
RG = [[0, 1, 2, 3, 4, 5, 6, 7]]
TOK = 1024
SEQ = 8192
D = 2048
NDT = D // 128
H = 16
HD = 128
NCH = TOK // 128
INTER = 5504
NOC = INTER // 128
GHR = D // NCORES
DHR = INTER // NCORES
DQKV = 6144
FF = 8192
NFC = FF // 128
NE = 8
RK = 16
EPS = 1e-5
LUTN = 16
INV_SQRT_HD = float(1.0 / np.sqrt(128.0))

ENGS = ("sp", "act", "pe", "dve", "pool")


_NAMES = {}


def _uniq(name):
    n = _NAMES.get(name, 0)
    _NAMES[name] = n + 1
    return name if n == 0 else f"{name}_{n}"


class Scope:
    def __init__(self, k):
        self.k = k
        self.es = contextlib.ExitStack()

    def sbuf(self, name, shape, dt):
        return self.es.enter_context(self.k.nc.sbuf_tensor(_uniq(name), shape, dt))

    def psum(self, name, shape, dt=F32):
        return self.es.enter_context(self.k.nc.psum_tensor(_uniq(name), shape, dt))

    def close(self):
        self.es.close()
        self.k.fence = [(n, s[1]) for n, s in self.k.sems.items()]


class K:

    def __init__(self, nc, es):
        _NAMES.clear()
        self.nc = nc
        self.es = es
        self.fence = []
        self.ops = {e: [] for e in ENGS}
        self.sems = {}
        self.lastwait = {e: {} for e in ENGS}
        self.pending = {e: {} for e in ENGS}

    def scope(self):
        return Scope(self)

    def sem(self, name):
        if name not in self.sems:
            h = self.es.enter_context(self.nc.semaphore(name))
            self.sems[name] = [h, 0]
        return self.sems[name]

    def wait(self, eng, deps):
        p = self.pending[eng]
        for d in deps:
            if d is None:
                continue
            semname, val = d
            if val > p.get(semname, 0):
                p[semname] = val

    def _take_waits(self, eng):
        out = []
        lw = self.lastwait[eng]
        for semname, val in self.pending[eng].items():
            if val <= 0 or lw.get(semname, 0) >= val:
                continue
            lw[semname] = val
            h = self.sem(semname)[0]
            out.append((h.num, h.name, val))
        self.pending[eng] = {}
        return out

    def flush_waits(self, eng):
        for num, name, val in self._take_waits(eng):
            h = self.sems[name][0]
            self.ops[eng].append(lambda q, h=h, v=val: q.wait_ge(h, v))

    def _do(self, eng, semname, inc, deps, fn, attach=True):
        self.wait(eng, list(deps) + self.fence)
        if not attach:
            self.flush_waits(eng)
            waits = []
        else:
            waits = self._take_waits(eng)
            for num, name, v in waits[:-1]:
                h = self.sems[name][0]
                self.ops[eng].append(lambda q, h=h, v=v: q.wait_ge(h, v))
            waits = waits[-1:]
        s = self.sem(semname)
        s[1] += inc
        val = s[1]

        def run(q, fn=fn, h=s[0], kk=inc, waits=waits):
            bi = fn(q)
            if waits:
                si = mybir.ensure_sync_info(bi.ins)
                for num, name, v in waits:
                    si.on_wait.append(mybir.SyncWait(
                        sync_type="semaphore", id=num, ant_name=name,
                        wait_mode="sem-ge-imm", wait_value=v))
            bi.then_inc(h, kk)

        self.ops[eng].append(run)
        return (semname, val)

    def dma(self, eng, out, in_, deps=(), sem=None, **kw):
        if sem is None:
            t = getattr(out, "tensor", None)
            sem = "d_" + (t.name if t is not None else "x")
        return self._do(eng, sem, 16, deps,
                        lambda q: q.dma_start(out=out, in_=in_, **kw), attach=False)

    def op(self, eng, method, *args, deps=(), **kw):
        deps = list(deps)
        if eng in ("dve", "act", "pool"):
            deps.append(self.tok(eng))
        return self._do(eng, "c_" + eng, 1, deps,
                        lambda q: getattr(q, method)(*args, **kw))

    def cc(self, kind, op, ins, outs, deps=()):
        return self._do("pool", "cc", 1, deps,
                        lambda q: q.collective_compute(kind, op, RG, ins=ins, outs=outs),
                        attach=False)

    def tok(self, eng):
        name = "c_" + eng
        return (name, self.sems[name][1] if name in self.sems else 0)

    def dmatok(self, eng):
        name = "dma_" + eng
        return (name, self.sems[name][1] if name in self.sems else 0)

    def barrier_tokens(self):
        return [self.tok(e) for e in ("pe", "dve", "act")] + \
               [self.dmatok(e) for e in ("sp", "act", "pool")] + [self.tok("pool")]

    def emit_all(self, block):
        for eng in ENGS:
            self.flush_waits(eng)
        for eng, attr in (("sp", "sync"), ("act", "scalar"), ("pe", "tensor"),
                          ("dve", "vector"), ("pool", "gpsimd")):
            ops = self.ops[eng]
            if not ops:
                continue

            def body(q, ops=ops):
                for f in ops:
                    f(q)

            getattr(block, attr)(body)



def host_constants():
    import ml_dtypes
    bf16 = ml_dtypes.bfloat16
    c = {}
    c["ident_b"] = np.eye(128).astype(bf16)
    c["ident_f"] = np.eye(128, dtype=np.float32)
    c["ones_col_b"] = np.ones((128, 1), dtype=bf16)
    c["ones_row_f"] = np.ones((1, 128), dtype=np.float32)
    i = np.arange(128)
    diff = (i[None, :] - i[:, None]).astype(np.float64)
    c["diffmaskT"] = np.where(diff >= 0, diff, 1e6).astype(np.float32)
    c["c127"] = np.broadcast_to((127.0 - i)[None, :], (128, 128)).astype(np.float32).copy()
    c["colp1_1024"] = np.broadcast_to((np.arange(1024) + 1.0)[None, :], (128, 1024)).astype(np.float32).copy()
    c["triT_b"] = (i[:, None] <= i[None, :]).astype(bf16)
    c["neg1_8"] = np.full((128, 8), -1.0, dtype=np.float32)
    sel = np.zeros((8, 8 * 128), dtype=np.float64)
    for e in range(8):
        sel[e, e * 128:(e + 1) * 128] = 1.0
    c["sel8_b"] = sel.astype(bf16)
    lm = np.zeros((8, 128, 8, 128), dtype=np.float64)
    tri = (i[:, None] <= i[None, :])
    for kb in range(8):
        for qb in range(8):
            if qb > kb:
                lm[kb, :, qb, :] = 1.0
            elif qb == kb:
                lm[kb, :, qb, :] = tri
    c["lmask_b"] = lm.reshape(1024, 1024).astype(bf16)
    c["fpool"] = np.broadcast_to(np.array(FPOOL_VALS, dtype=np.float32)[None, :],
                                 (128, len(FPOOL_VALS))).copy()
    return c


FPOOL_VALS = [1e-5, -0.5, 1.0 / 128.0, INV_SQRT_HD, 128.0, 1024.0, 0.5, 2.0]

CONST_SHAPES = {
    "ident_b": ([128, 128], BF16), "ident_f": ([128, 128], F32),
    "ones_col_b": ([128, 1], BF16),
    "ones_row_f": ([1, 128], F32), "diffmaskT": ([128, 128], F32),
    "c127": ([128, 128], F32), "colp1_1024": ([128, 1024], F32),
    "triT_b": ([128, 128], BF16), "neg1_8": ([128, 8], F32),
    "sel8_b": ([8, 1024], BF16), "lmask_b": ([1024, 1024], BF16),
    "fpool": ([128, len(FPOOL_VALS)], F32),
}


def declare_constants(nc):
    return {n: nc.dram_tensor("cst_" + n, shp, dt, kind="ExternalInput").ap()
            for n, (shp, dt) in CONST_SHAPES.items()}


def load_constants(k, S, drams):
    sb = {}
    toks = []
    for n, d in drams.items():
        shp, dt = CONST_SHAPES[n]
        if shp[0] > 128:
            nt = shp[0] // 128
            t = S.sbuf("sb_" + n, [128, nt * shp[1]], dt)
            toks.append(k.dma("sp", t[:, :], d.rearrange("(n p) q -> p n q", p=128)))
        else:
            t = S.sbuf("sb_" + n, shp, dt)
            toks.append(k.dma("sp", t[:, :], d))
        sb[n] = t
    fp = sb["fpool"]
    for i, v in enumerate(FPOOL_VALS):
        k.nc.const_aps.register(F32, float(v), fp[:, i:i + 1])
    k.fence = [(n, s[1]) for n, s in k.sems.items()]
    sb["_tok"] = toks
    return sb



def emit_layernorm(k, C, u_sb, x_out_sb, T, tag="ln", dram_out=None):
    S = k.scope()
    ps_s = S.psum(tag + "_s", [1, T])
    ps_q = S.psum(tag + "_q", [1, T])
    ps_b1 = S.psum(tag + "_b1", [128, T])
    ps_b2 = S.psum(tag + "_b2", [128, T])
    usq = S.sbuf(tag + "_usq", [128, T], BF16)
    row = S.sbuf(tag + "_row", [1, 5 * T], F32)
    rbc = S.sbuf(tag + "_rbc", [128, T], F32)
    otile = S.sbuf(tag + "_ot", [128, 2 * T], F32) if dram_out is not None else None

    ones = C["ones_col_b"]
    onesr = C["ones_row_f"]
    for i in range(NDT):
        ut = u_sb[:, i * T:(i + 1) * T]
        for j in range(0, T, 512):
            k.op("pe", "matmul", ps_s[0:1, j:j + 512], ones[:, :],
                 u_sb[:, i * T + j:i * T + j + 512], start=(i == 0), stop=(i == NDT - 1),
                 deps=[k.tok("dve"), k.tok("act")] if (i == 0 and j == 0) else [])
        sq = k.op("act", "activation", usq[:, :], ut, AF.Square,
                  deps=[k.tok("pe")] if i > 0 else [k.tok("dve")])
        for j in range(0, T, 512):
            k.op("pe", "matmul", ps_q[0:1, j:j + 512], ones[:, :], usq[:, j:j + 512],
                 start=(i == 0), stop=(i == NDT - 1), deps=[sq] if j == 0 else [])
    pe_done = k.tok("pe")
    mean = row[:, 0:T]
    var = row[:, T:2 * T]
    rstd = row[:, 2 * T:3 * T]
    mrs = row[:, 3 * T:4 * T]
    scr = row[:, 4 * T:5 * T]
    t1 = k.op("dve", "tensor_scalar", mean, ps_s[:, :], 1.0 / D, None, ALU.mult, deps=[pe_done])
    t2 = k.op("dve", "tensor_scalar", scr, ps_q[:, :], 1.0 / D, None, ALU.mult)
    t3 = k.op("dve", "tensor_mul", var, mean, mean)
    t4 = k.op("dve", "tensor_sub", var, scr, var)
    a1 = k.op("act", "activation", var, var, AF.Ln, bias=EPS, deps=[t4])
    a2 = k.op("act", "activation", rstd, var, AF.Exp, scale=-0.5)
    t5 = k.op("dve", "tensor_mul", mrs, mean, rstd, deps=[a2])
    for j in range(T // 512):
        sl = slice(j * 512, (j + 1) * 512)
        k.op("pe", "matmul", ps_b1[:, sl], onesr[:, :], rstd[0:1, sl], start=True, stop=True,
             deps=[a2])
        k.op("pe", "matmul", ps_b2[:, sl], onesr[:, :], mrs[0:1, sl], start=True, stop=True,
             deps=[t5])
    bc_done = k.tok("pe")
    e1 = k.op("dve", "tensor_copy", rbc[:, :], ps_b1[:, :], deps=[bc_done])
    stoks = []
    dr = fmr(dram_out) if dram_out is not None else None
    for i in range(NDT):
        ut = u_sb[:, i * T:(i + 1) * T]
        m1 = k.op("dve", "tensor_mul", usq[:, :], ut, rbc[:, :], deps=[e1])
        if dram_out is None:
            ot = x_out_sb[:, i * T:(i + 1) * T]
            k.op("dve", "scalar_tensor_tensor", ot, ps_b2[:, :], -1.0, usq[:, :],
                 ALU.mult, ALU.add, deps=[m1])
        else:
            sl = (i % 2) * T
            dep = [stoks[i - 2]] if i >= 2 else []
            s1 = k.op("dve", "scalar_tensor_tensor", otile[:, sl:sl + T],
                      ps_b2[:, :], -1.0, usq[:, :], ALU.mult, ALU.add,
                      deps=[m1] + dep)
            stoks.append(k.dma("sp", dr[:, i, :], otile[:, sl:sl + T], deps=[s1],
                                sem=tag + "_st" + str(i % 2)))
    out = k.tok("dve")
    S.close()
    if dram_out is not None:
        return stoks
    return out


FM = dict(p=128)


def fmr(ap):
    return ap.rearrange("(n p) t -> p n t", p=128)



def phase_ffn0_build(k, C, io):
    SG = k.scope()
    lutcol = SG.sbuf("b_lutcol", [128, 16], F32)
    SL = k.scope()
    lutrow = SL.sbuf("b_lutrow", [1, 16], F32)
    ps_lut = SL.psum("b_pslut", [128, 16])
    tl = k.dma("sp", lutrow[:, :], io["lut"].rearrange("(o l) -> o l", o=1))
    m = k.op("pe", "matmul", ps_lut[:, :], C["ones_row_f"][:, :], lutrow[:, :],
             start=True, stop=True, deps=[tl] + C["_tok"])
    lc = k.op("dve", "tensor_copy", lutcol[:, :], ps_lut[:, :], deps=[m])
    SL.close()

    CW = 512
    bidx = SG.sbuf("b_bidx", [128, CW], I32)
    bidxb = SG.sbuf("b_bidxb", [128, CW], BF16)
    btmp = SG.sbuf("b_btmp", [128, CW], BF16)
    bacc = SG.sbuf("b_bacc", [128, CW], BF16)
    bidx2 = SG.sbuf("b_bidx2", [128, CW], I32)
    bacc2 = SG.sbuf("b_bacc2", [128, CW], BF16)


    prev_store = [None, None]
    sl = [0]

    def build_rows(src_ap, dst_ap, nrows, ncols):
        toks = []
        r = 0
        while r < nrows:
            p = min(128, nrows - r)
            c0 = 0
            while c0 < ncols:
                w = min(CW, ncols - c0)
                s = sl[0] % 2
                bidx_s = bidx if s == 0 else bidx2
                bacc_s = bacc if s == 0 else bacc2
                ld = k.dma("sp", bidx_s[0:p, 0:w],
                           in_=src_ap[r:r + p, c0:c0 + w],
                           deps=[prev_store[s]], sem=f"d_bld{s}")
                k.op("dve", "tensor_copy", bidxb[0:p, 0:w], bidx_s[0:p, 0:w],
                     deps=[ld, lc])
                for l in range(LUTN):
                    tgt = bacc_s if l == 0 else btmp
                    k.op("dve", "tensor_scalar", tgt[0:p, 0:w], bidxb[0:p, 0:w],
                         float(l), lutcol[0:p, l:l + 1], ALU.is_equal, ALU.mult,
                         deps=[prev_store[s]] if l == 0 else [])
                    if l > 0:
                        k.op("dve", "tensor_add", bacc_s[0:p, 0:w], bacc_s[0:p, 0:w],
                             btmp[0:p, 0:w])
                st = k.dma("sp", dst_ap[r:r + p, c0:c0 + w], bacc_s[0:p, 0:w],
                           deps=[k.tok("dve")], sem=f"d_bst{s}")
                prev_store[s] = st
                toks.append(st)
                sl[0] += 1
                c0 += w
            r += p
        return toks

    tg = build_rows(io["gidxT_s"], io["g_in"], GHR, INTER)
    tu = build_rows(io["uidxT_s"], io["u_in"], GHR, INTER)
    td = build_rows(io["didxT_s"], io["d_in"], DHR, D)

    cg = k.cc("AllGather", ALU.bypass, [io["g_in"]], [io["g_out"]], deps=tg)
    cu = k.cc("AllGather", ALU.bypass, [io["u_in"]], [io["u_out"]], deps=tu)
    cd = k.cc("AllGather", ALU.bypass, [io["d_in"]], [io["d_out"]], deps=td)
    io["_cc"] = (cg, cu, cd)
    return SG


def phase_ffn0(k, C, io):
    nc = k.nc
    SB = k.scope()

    x2 = SB.sbuf("b_x2", [128, NDT * TOK], BF16)
    h_sb = SB.sbuf("b_h", [128, NOC * TOK], BF16)
    tx2 = k.dma("sp", x2[:, :], fmr(io["x2"]))
    cg, cu, cd = io["_cc"]

    NB = 2
    SGU = k.scope()
    wg = [SGU.sbuf(f"b_wg{j}", [128, NDT * 256], BF16) for j in range(NB)]
    wu = [SGU.sbuf(f"b_wu{j}", [128, NDT * 256], BF16) for j in range(NB)]
    sg = [SGU.sbuf(f"b_sg{j}", [128, TOK], BF16) for j in range(2)]
    ps_g = [SGU.psum(f"b_psg{j}", [128, TOK]) for j in range(2)]
    ps_u = [SGU.psum(f"b_psu{j}", [128, TOK]) for j in range(2)]

    ngrp = (NOC + 1) // 2
    ldtok = [None] * ngrp
    usetok = [None] * (NOC + 1)
    htok = [None] * (NOC + 2)

    def load_grp(g):
        o0 = 2 * g
        no = min(2, NOC - o0)
        b = g % NB
        deps = [cg, cu, tx2]
        if g >= NB:
            deps.append(usetok[2 * (g - NB) + min(2, NOC - 2 * (g - NB)) - 1])
        t1 = k.dma("sp", wg[b][:, 0:NDT * no * 128],
                   io["g_out"][:, o0 * 128:(o0 + no) * 128].rearrange(
                       "(n p) o -> p n o", p=128), deps=deps)
        t2 = k.dma("sp", wu[b][:, 0:NDT * no * 128],
                   io["u_out"][:, o0 * 128:(o0 + no) * 128].rearrange(
                       "(n p) o -> p n o", p=128), deps=deps)
        ldtok[g] = (t1, t2)

    for g in range(min(NB, ngrp)):
        load_grp(g)
    for g in range(ngrp):
        o0 = 2 * g
        no = min(2, NOC - o0)
        b = g % NB
        for oj in range(no):
            oc = o0 + oj
            pp = oc % 2
            for kt in range(NDT):
                dep = []
                if kt == 0:
                    dep = [ldtok[g][0], ldtok[g][1]]
                    dep += [htok[oc - 2]] if oc >= 2 else [k.tok("act"), k.tok("dve")]
                wsl = slice((kt * no + oj) * 128, (kt * no + oj) * 128 + 128)
                for j in range(0, TOK, 512):
                    k.op("pe", "matmul", ps_g[pp][:, j:j + 512], wg[b][:, wsl],
                         x2[:, kt * TOK + j:kt * TOK + j + 512],
                         start=(kt == 0), stop=(kt == NDT - 1),
                         deps=dep if j == 0 else [])
                    k.op("pe", "matmul", ps_u[pp][:, j:j + 512], wu[b][:, wsl],
                         x2[:, kt * TOK + j:kt * TOK + j + 512],
                         start=(kt == 0), stop=(kt == NDT - 1))
            mm = k.tok("pe")
            a = k.op("act", "activation", sg[pp][:, :], ps_g[pp][:, :], AF.Silu,
                     deps=[mm] + ([htok[oc - 2]] if oc >= 2 else []))
            usetok[oc] = k.op("dve", "tensor_mul", h_sb[:, oc * TOK:(oc + 1) * TOK],
                              sg[pp][:, :], ps_u[pp][:, :], deps=[a, mm])
            htok[oc] = usetok[oc]
        if g + NB < ngrp:
            load_grp(g + NB)
    hdone = k.tok("dve")
    SGU.close()

    SD = k.scope()
    wd = [SD.sbuf(f"b_wd{j}", [128, NOC * 128], BF16) for j in range(NB)]
    ps_d = SD.psum("b_psd", [128, TOK])
    dtok = [None] * NDT
    vtok = [None] * NDT

    def load_d(dc):
        b = dc % NB
        deps = [cd, hdone]
        if dc >= NB:
            deps.append(vtok[dc - NB])
        dtok[dc] = k.dma("sp", wd[b][:, :],
                         io["d_out"][:, dc * 128:(dc + 1) * 128].rearrange(
                             "(n p) o -> p n o", p=128), deps=deps)

    for dc in range(min(NB, NDT)):
        load_d(dc)
    for dc in range(NDT):
        b = dc % NB
        for ot in range(NOC):
            dep = [dtok[dc], k.tok("dve")] if ot == 0 else []
            for j in range(0, TOK, 512):
                k.op("pe", "matmul", ps_d[:, j:j + 512], wd[b][:, ot * 128:(ot + 1) * 128],
                     h_sb[:, ot * TOK + j:ot * TOK + j + 512],
                     start=(ot == 0), stop=(ot == NOC - 1), deps=dep if j == 0 else [])
        mm = k.tok("pe")
        vtok[dc] = k.op("dve", "scalar_tensor_tensor", x2[:, dc * TOK:(dc + 1) * TOK],
                        ps_d[:, :], 1.0, x2[:, dc * TOK:(dc + 1) * TOK],
                        ALU.mult, ALU.add, deps=[mm])
        if dc + NB < NDT:
            load_d(dc + NB)
    SD.close()

    emit_layernorm(k, C, x2, x2, TOK, tag="b_ln")
    ot = k.dma("sp", fmr(io["x3"]), x2[:, :], deps=[k.tok("dve")])
    SB.close()
    return [ot]





def phase_l0_attn(k, C, io):
    nc = k.nc
    SB = k.scope()
    x_sb = SB.sbuf("a_x", [128, NDT * TOK], BF16)
    q_sb = SB.sbuf("a_q", [128, NDT * TOK], BF16)
    o_sb = SB.sbuf("a_o", [128, NDT * TOK], BF16)
    S_f = SB.sbuf("a_Sf", [128, H * HD], F32)
    S_b = SB.sbuf("a_Sb", [128, H * HD], BF16)
    ldrow = SB.sbuf("a_ldrow", [1, 16], F32)
    ldcol = SB.sbuf("a_ldcol", [128, 16], F32)
    cdcol = SB.sbuf("a_cdcol", [128, 16], F32)
    dt_t = SB.sbuf("a_dt", [128, H * 128], F32)
    kd_t = SB.sbuf("a_kd", [128, H * 128], F32)
    qd_t = SB.sbuf("a_qd", [128, H * 128], F32)

    tx = k.dma("pool", x_sb[:, :], fmr(io["x0"]))
    tld = k.dma("pool", ldrow[:, :], io["ld"].rearrange("(o l) -> o l", o=1))

    SP0 = k.scope()
    ps_ld = SP0.psum("a_psld", [128, 16])
    m = k.op("pe", "matmul", ps_ld[:, :], C["ones_row_f"][:, :], ldrow[:, :],
             start=True, stop=True, deps=[tld] + C["_tok"])
    lc = k.op("act", "activation", ldcol[:, :], ps_ld[:, :], AF.Copy, deps=[m])
    SP0.close()
    k.op("act", "activation", cdcol[:, :], ldcol[:, :], AF.Exp, scale=128.0, deps=[lc])
    for h in range(H):
        sc = ldcol[:, h:h + 1]
        k.op("act", "activation", dt_t[:, h * 128:(h + 1) * 128], C["diffmaskT"][:, :],
             AF.Exp, scale=sc, deps=[lc])
        k.op("act", "activation", kd_t[:, h * 128:(h + 1) * 128], C["c127"][:, :],
             AF.Exp, scale=sc)
        k.op("act", "activation", qd_t[:, h * 128:(h + 1) * 128], C["colp1_1024"][:, 0:128],
             AF.Exp, scale=sc)
    zS = k.op("pool", "memset", S_f[:, :], 0.0)
    zSb = k.op("pool", "memset", S_b[:, :], 0.0)

    for hh in range(2):
        t0 = hh * 512
        SH = k.scope()
        k_sb = SH.sbuf("a_k", [128, NDT * 512], BF16)
        v_sb = SH.sbuf("a_v", [128, NDT * 512], BF16)
        wcol = [SH.sbuf(f"a_wcol{j}", [128, NDT * 128], BF16) for j in range(2)]
        ps_p = [SH.psum(f"a_psp{j}", [128, 512]) for j in range(2)]
        seq = [(w, d, ft) for w, d in (("wq", None), ("wk", k_sb), ("wv", v_sb))
               for ft in range(NDT)]
        twl = [None] * len(seq)
        mmdone = [None] * len(seq)
        evtok = [None] * len(seq)

        def load_proj(n):
            wname, _, ft = seq[n]
            twl[n] = k.dma("pool", wcol[n % 2][:, :],
                           io[wname][:, ft * 128:(ft + 1) * 128].rearrange(
                               "(n p) f -> p n f", p=128),
                           deps=[mmdone[n - 2]] if n >= 2 else [k.tok("pe")])

        load_proj(0)
        load_proj(1)
        for n, (wname, dst, ft) in enumerate(seq):
            for kt in range(NDT):
                dep = []
                if kt == 0:
                    dep = [twl[n], tx] + ([evtok[n - 2]] if n >= 2 else [])
                k.op("pe", "matmul", ps_p[n % 2][:, :],
                     wcol[n % 2][:, kt * 128:(kt + 1) * 128],
                     x_sb[:, kt * TOK + t0:kt * TOK + t0 + 512],
                     start=(kt == 0), stop=(kt == NDT - 1), deps=dep)
            mmdone[n] = k.tok("pe")
            if n + 2 < len(seq):
                load_proj(n + 2)
            if dst is None:
                evtok[n] = k.op("act", "activation",
                                q_sb[:, ft * TOK + t0:ft * TOK + t0 + 512],
                                ps_p[n % 2][:, :], AF.Silu, deps=[mmdone[n]])
            else:
                evtok[n] = k.op("act", "activation", dst[:, ft * 512:(ft + 1) * 512],
                                ps_p[n % 2][:, :], AF.Silu, deps=[mmdone[n]])

        SS = k.scope()
        ksc = SS.sbuf("a_ksc", [128, 128], BF16)
        ksctm = SS.sbuf("a_ksctm", [128, 128], BF16)
        vtm = SS.sbuf("a_vtm", [128, 128], BF16)
        dsc = SS.sbuf("a_dsc", [128, 128], BF16)
        qsc = SS.sbuf("a_qsc", [128, 128], BF16)
        ps_tr = SS.psum("a_pstr", [128, 128], BF16)
        ps_sc = SS.psum("a_pssc", [128, 128])
        ps_o = SS.psum("a_pso", [128, 128])
        ps_kv = SS.psum("a_pskv", [128, 128])
        qkv_done = k.tok("act")
        for c in range(4):
            gc = hh * 4 + c
            for h in range(H):
                kfm = k_sb[:, h * 512 + c * 128:h * 512 + c * 128 + 128]
                vfm = v_sb[:, h * 512 + c * 128:h * 512 + c * 128 + 128]
                qfm = q_sb[:, h * TOK + gc * 128:h * TOK + gc * 128 + 128]
                t1 = k.op("dve", "tensor_mul", ksc[:, :], kfm,
                          kd_t[:, h * 128:(h + 1) * 128], deps=[qkv_done, k.tok("pe")])
                tr1 = k.op("pe", "transpose", ps_tr[:, :], ksc[:, :], C["ident_b"][:, :],
                           deps=[t1])
                e1 = k.op("dve", "tensor_copy", ksctm[:, :], ps_tr[:, :], deps=[tr1])
                tr2 = k.op("pe", "transpose", ps_tr[:, :], vfm, C["ident_b"][:, :],
                           deps=[e1])
                e2 = k.op("dve", "tensor_copy", vtm[:, :], ps_tr[:, :], deps=[tr2])
                mm1 = k.op("pe", "matmul", ps_sc[:, :], kfm, qfm, start=True, stop=True,
                           deps=[k.tok("dve")])
                d1 = k.op("dve", "tensor_mul", dsc[:, :], ps_sc[:, :],
                          dt_t[:, h * 128:(h + 1) * 128], deps=[mm1])
                d2 = k.op("dve", "tensor_mul", qsc[:, :], qfm,
                          qd_t[:, h * 128:(h + 1) * 128])
                mo1 = k.op("pe", "matmul", ps_o[:, :], vtm[:, :], dsc[:, :],
                           start=True, stop=False, deps=[d1, e2])
                mo2 = k.op("pe", "matmul", ps_o[:, :], S_b[:, h * 128:(h + 1) * 128],
                           qsc[:, :], start=False, stop=True, deps=[d2])
                eo = k.op("dve", "tensor_copy",
                          o_sb[:, h * TOK + gc * 128:h * TOK + gc * 128 + 128],
                          ps_o[:, :], deps=[mo2])
                mkv = k.op("pe", "matmul", ps_kv[:, :], ksctm[:, :], vtm[:, :],
                           start=True, stop=True, deps=[k.tok("dve")])
                su = k.op("dve", "scalar_tensor_tensor", S_f[:, h * 128:(h + 1) * 128],
                          S_f[:, h * 128:(h + 1) * 128], cdcol[:, h:h + 1],
                          ps_kv[:, :], ALU.mult, ALU.add, deps=[mkv, zS])
                k.op("dve", "tensor_copy", S_b[:, h * 128:(h + 1) * 128],
                     S_f[:, h * 128:(h + 1) * 128], deps=[su, zSb])
        SS.close()
        SH.close()

    tT = k.dma("sp", io["T_in"].rearrange("(n p) e -> p n e", p=128), S_f[:, :],
               deps=[k.tok("dve")])
    cT = k.cc("AllGather", ALU.bypass, [io["T_in"]], [io["T_out"]], deps=[tT])

    S6 = k.scope()
    S0f = S6.sbuf("a_S0f", [128, H * HD], F32)
    S0b = S6.sbuf("a_S0b", [128, H * HD], BF16)
    tr_sb = S6.sbuf("a_tr", [128, H * HD], F32)
    perow = S6.sbuf("a_perow", [1, 16], F32)
    pecol = S6.sbuf("a_pecol", [128, 16], F32)
    w16 = S6.sbuf("a_w16", [128, 16], F32)
    ps_pe = S6.psum("a_pspe", [128, 16])
    tpe = k.dma("pool", perow[:, :], io["pev"].rearrange("(o l) -> o l", o=1))
    mpe = k.op("pe", "matmul", ps_pe[:, :], C["ones_row_f"][:, :], perow[:, :],
               start=True, stop=True, deps=[tpe])
    epe = k.op("dve", "tensor_copy", pecol[:, :], ps_pe[:, :], deps=[mpe])
    z0 = k.op("dve", "memset", S0f[:, :], 0.0)
    ltok = None
    for r in range(NCORES):
        ltok = k.dma("sp", tr_sb[:, :],
                     io["T_out"][r * D:(r + 1) * D, :].rearrange("(n p) e -> p n e", p=128),
                     deps=[cT] + ([k.tok("dve")] if r > 0 else []))
        w1 = k.op("dve", "tensor_scalar", w16[:, :], ldcol[:, :], pecol[:, r:r + 1],
                  1024.0, ALU.mult, ALU.mult, deps=[epe])
        w2 = k.op("act", "activation", w16[:, :], w16[:, :], AF.Exp, deps=[w1])
        w3 = k.op("dve", "tensor_scalar", w16[:, :], w16[:, :], pecol[:, 8 + r:9 + r],
                  None, ALU.mult, deps=[w2])
        for h in range(H):
            k.op("dve", "scalar_tensor_tensor", S0f[:, h * 128:(h + 1) * 128],
                 tr_sb[:, h * 128:(h + 1) * 128], w16[:, h:h + 1],
                 S0f[:, h * 128:(h + 1) * 128], ALU.mult, ALU.add,
                 deps=[w3, ltok, z0])
    cb = k.op("dve", "tensor_copy", S0b[:, :], S0f[:, :])

    qdc = S6.sbuf("a_qdc", [128, TOK], F32)
    qq = S6.sbuf("a_qq", [128, TOK], BF16)
    ps_c = S6.psum("a_psc", [128, 512])
    for h in range(H):
        a1 = k.op("act", "activation", qdc[:, :], C["colp1_1024"][:, :], AF.Exp,
                  scale=ldcol[:, h:h + 1], deps=[k.tok("dve")])
        a2 = k.op("dve", "tensor_mul", qq[:, :], q_sb[:, h * TOK:(h + 1) * TOK],
                  qdc[:, :], deps=[a1])
        for j in (0, 512):
            mc = k.op("pe", "matmul", ps_c[:, :], S0b[:, h * 128:(h + 1) * 128],
                      qq[:, j:j + 512], start=True, stop=True,
                      deps=[a2, cb, k.tok("dve")])
            k.op("dve", "tensor_add", o_sb[:, h * TOK + j:h * TOK + j + 512],
                 o_sb[:, h * TOK + j:h * TOK + j + 512], ps_c[:, :], deps=[mc])
    S6.close()

    S7 = k.scope()
    wg1_sb = S7.sbuf("a_wg1", [128, NDT * 128], BF16)
    wg2_sb = S7.sbuf("a_wg2", [128, NDT * 128], BF16)
    nw_sb = S7.sbuf("a_nw", [128, 16], F32)
    glow = S7.sbuf("a_glow", [128, TOK], BF16)
    sig = S7.sbuf("a_sig", [128, TOK], BF16)
    osq = S7.sbuf("a_osq", [128, TOK], BF16)
    rrow = S7.sbuf("a_rrow", [1, TOK], F32)
    ps_g = S7.psum("a_psg", [128, TOK])
    ps_r = S7.psum("a_psr", [1, TOK])
    ps_b = S7.psum("a_psb", [128, TOK])
    tw1 = k.dma("pool", wg1_sb[:, :], io["wg1"].rearrange("(n p) f -> p n f", p=128))
    tw2 = k.dma("pool", wg2_sb[:, :], io["wg2"].rearrange("p (n f) -> p n f", f=128))
    tnw = k.dma("pool", nw_sb[:, :], io["normw"])
    for j in (0, 512):
        for kt in range(NDT):
            k.op("pe", "matmul", ps_g[:, j:j + 512], wg1_sb[:, kt * 128:(kt + 1) * 128],
                 x_sb[:, kt * TOK + j:kt * TOK + j + 512],
                 start=(kt == 0), stop=(kt == NDT - 1),
                 deps=[tw1, k.tok("dve")] if kt == 0 else [])
    gl = k.op("dve", "tensor_copy", glow[:, :], ps_g[:, :], deps=[k.tok("pe")])
    for h in range(H):
        for j in (0, 512):
            mg = k.op("pe", "matmul", ps_g[:, j:j + 512], wg2_sb[:, h * 128:(h + 1) * 128],
                      glow[:, j:j + 512], start=True, stop=True,
                      deps=[gl, tw2, k.tok("act"), k.tok("dve")] if j == 0 else [])
        sg1 = k.op("act", "activation", sig[:, :], ps_g[:, :], AF.Sigmoid,
                   deps=[k.tok("pe")])
        ho = o_sb[:, h * TOK:(h + 1) * TOK]
        t1 = k.op("dve", "tensor_mul", ho, ho, sig[:, :], deps=[sg1])
        s1 = k.op("act", "activation", osq[:, :], ho, AF.Square, deps=[t1])
        for j in (0, 512):
            k.op("pe", "matmul", ps_r[0:1, j:j + 512], C["ones_col_b"][:, :],
                 osq[:, j:j + 512], start=True, stop=True,
                 deps=[s1, k.tok("act"), k.tok("dve")] if j == 0 else [])
        l1 = k.op("act", "activation", rrow[:, :], ps_r[0:1, :], AF.Ln,
                  scale=1.0 / 128.0, bias=EPS, deps=[k.tok("pe")])
        l2 = k.op("act", "activation", rrow[:, :], rrow[:, :], AF.Exp, scale=-0.5)
        for j in (0, 512):
            k.op("pe", "matmul", ps_b[:, j:j + 512], C["ones_row_f"][:, :],
                 rrow[0:1, j:j + 512], start=True, stop=True,
                 deps=[l2, k.tok("dve")] if j == 0 else [])
        bb = k.tok("pe")
        t2 = k.op("dve", "tensor_mul", ho, ho, ps_b[:, :], deps=[bb])
        k.op("dve", "tensor_scalar", ho, ho, nw_sb[:, h:h + 1], None, ALU.mult,
             deps=[tnw])
    S7.close()

    S8 = k.scope()
    wo_sb = S8.sbuf("a_wo", [128, NDT * 128], BF16)
    ps_w = S8.psum("a_psw", [128, 512])
    for e in range(NDT):
        two = k.dma("pool", wo_sb[:, :],
                    io["wo"][:, e * 128:(e + 1) * 128].rearrange("(n p) f -> p n f", p=128),
                    deps=[k.tok("pe")])
        for j in (0, 512):
            for kt in range(NDT):
                k.op("pe", "matmul", ps_w[:, :], wo_sb[:, kt * 128:(kt + 1) * 128],
                     o_sb[:, kt * TOK + j:kt * TOK + j + 512],
                     start=(kt == 0), stop=(kt == NDT - 1),
                     deps=[two, k.tok("dve")] if kt == 0 else [])
            k.op("dve", "scalar_tensor_tensor",
                 x_sb[:, e * TOK + j:e * TOK + j + 512], ps_w[:, :], 1.0,
                 x_sb[:, e * TOK + j:e * TOK + j + 512], ALU.mult, ALU.add,
                 deps=[k.tok("pe")])
    S8.close()

    emit_layernorm(k, C, x_sb, x_sb, TOK, tag="a_ln")
    ot = k.dma("sp", fmr(io["x2"]), x_sb[:, :], deps=[k.tok("dve")])
    SB.close()
    return [ot]


def declare_l0_io(nc, kin, kout):
    io = {}
    io["x0"] = nc.dram_tensor("x0", [D, TOK], F32, kind=kin).ap()
    io["ld"] = nc.dram_tensor("ld", [H], F32, kind="ExternalInput").ap()
    for w in ("wq", "wk", "wv", "wo"):
        io[w] = nc.dram_tensor(w, [D, D], F32, kind="ExternalInput").ap()
    io["wg1"] = nc.dram_tensor("wg1", [D, 128], F32, kind="ExternalInput").ap()
    io["wg2"] = nc.dram_tensor("wg2", [128, D], F32, kind="ExternalInput").ap()
    io["normw"] = nc.dram_tensor("normw", [128, NDT], F32, kind="ExternalInput").ap()
    io["pev"] = nc.dram_tensor("pev", [16], F32, kind="ExternalInput").ap()
    io["T_in"] = nc.dram_tensor("T_in", [D, HD], F32, kind="Internal").ap()
    io["T_out"] = nc.dram_tensor("T_out", [NCORES * D, HD], F32, kind="Internal",
                                 addr_space=_ccsp("AllGather")).ap()
    io["x2"] = nc.dram_tensor("x2", [D, TOK], BF16, kind=kout).ap()
    return io


def build_l0_program():
    nc = bass.Bass("TRN2", target_bir_lowering=False, debug=False)
    io = declare_l0_io(nc, "ExternalInput", "ExternalOutput")
    cst = declare_constants(nc)
    with contextlib.ExitStack() as es:
        k = K(nc, es)
        with nc.Block() as block:
            S0 = k.scope()
            C = load_constants(k, S0, cst)
            outs = phase_l0_attn(k, C, io)
            k.wait("sp", outs)
            k.emit_all(block)
            S0.close()
    return nc


def stage_l0_inputs(inputs):
    cst = host_constants()
    x = inputs["L0_attn_x"][0]
    maps = []
    for c in range(NCORES):
        m = {("cst_" + n): v for n, v in cst.items()}
        m["x0"] = np.ascontiguousarray(x[c * TOK:(c + 1) * TOK, :].T)
        m["ld"] = inputs["L0_attn_log_decay"]
        m["wq"] = inputs["L0_attn_Wq"]
        m["wk"] = inputs["L0_attn_Wk"]
        m["wv"] = inputs["L0_attn_Wv"]
        m["wo"] = inputs["L0_attn_Wo"]
        m["wg1"] = inputs["L0_attn_Wg1"]
        m["wg2"] = inputs["L0_attn_Wg2"]
        m["normw"] = np.ascontiguousarray(inputs["L0_attn_norm_w"].reshape(16, 128).T)
        pev = np.zeros(16, dtype=np.float32)
        for r in range(NCORES):
            pev[r] = max(c - 1 - r, 0)
            pev[8 + r] = 1.0 if r < c else 0.0
        m["pev"] = pev
        maps.append(m)
    return maps





NKB_FULL = 56


def phase_l1_attn(k, C, io):
    nc = k.nc
    SB = k.scope()
    x_sb = SB.sbuf("c_x", [128, NDT * TOK], BF16)
    q_sb = SB.sbuf("c_q", [128, NDT * TOK], BF16)
    kf_sb = SB.sbuf("c_kf", [128, NDT * TOK], BF16)
    vtl_sb = SB.sbuf("c_vtl", [128, NDT * TOK], BF16)
    y_sb = q_sb
    nm = SB.sbuf("c_nm", [1, 64], F32)
    bat_sb = SB.sbuf("c_bat", [128, 48], F32)
    tx = k.dma("sp", x_sb[:, :], fmr(io["x3"]))
    tb = k.dma("sp", bat_sb[:, :], io["battn_c"])

    SQ = k.scope()
    wcol = [SQ.sbuf(f"c_wcol{j}", [128, NDT * 128], BF16) for j in range(2)]
    vtmp = SQ.sbuf("c_vtmp", [128, TOK], BF16)
    ps_p = [SQ.psum(f"c_psp{j}", [128, TOK]) for j in range(2)]
    ps_tr = SQ.psum("c_pstr", [128, 128], BF16)
    kin_toks = []
    vin_toks = []
    twl = [None] * 48
    mmdone = [None] * 48
    evtok = [None] * 48

    def load_qkv(n):
        twl[n] = k.dma("pool", wcol[n % 2][:, :],
                       io["wattn"][:, n * 128:(n + 1) * 128].rearrange(
                           "(n p) f -> p n f", p=128),
                       deps=[mmdone[n - 2]] if n >= 2 else [k.tok("pe")])

    load_qkv(0)
    load_qkv(1)
    for ft in range(48):
        h = ft % 16
        dst = vtmp if ft >= 32 else (q_sb if ft < 16 else kf_sb)
        for j in (0, 512):
            for kt in range(NDT):
                dep = []
                if kt == 0:
                    dep = [twl[ft], tx] + ([evtok[ft - 2]] if (ft >= 2 and j == 0)
                                           else [k.tok("dve")] if (ft < 2 and j == 0)
                                           else [])
                k.op("pe", "matmul", ps_p[ft % 2][:, j:j + 512],
                     wcol[ft % 2][:, kt * 128:(kt + 1) * 128],
                     x_sb[:, kt * TOK + j:kt * TOK + j + 512],
                     start=(kt == 0), stop=(kt == NDT - 1), deps=dep)
        mmdone[ft] = k.tok("pe")
        if ft + 2 < 48:
            load_qkv(ft + 2)
        for j in (0, 512):
            evtok[ft] = k.op("dve", "tensor_scalar",
                             dst[:, j:j + 512] if ft >= 32 else
                             dst[:, h * TOK + j:h * TOK + j + 512],
                             ps_p[ft % 2][:, j:j + 512], bat_sb[:, ft:ft + 1], None,
                             ALU.add, deps=[mmdone[ft], tb] if j == 0 else [])
        if 16 <= ft < 32:
            kin_toks.append(k.dma("sp", io["K_in"][h * 128:(h + 1) * 128, :],
                                  kf_sb[:, h * TOK:(h + 1) * TOK],
                                  deps=[k.tok("dve")]))
        if ft >= 32:
            for kb in range(8):
                tr = k.op("pe", "transpose", ps_tr[:, :],
                          vtmp[:, kb * 128:kb * 128 + 128], C["ident_b"][:, :],
                          deps=[k.tok("act"), k.tok("dve")])
                k.op("dve", "tensor_copy",
                     vtl_sb[:, h * TOK + kb * 128:h * TOK + kb * 128 + 128],
                     ps_tr[:, :], deps=[tr])
            vin_toks.append(k.dma("sp",
                                  io["V_in"][h * TOK:(h + 1) * TOK, :].rearrange(
                                      "(kb p) d -> p kb d", p=128),
                                  vtl_sb[:, h * TOK:(h + 1) * TOK],
                                  deps=[k.tok("dve")]))
    qsq = SQ.sbuf("c_qsq", [128, TOK], BF16)
    t11 = SQ.sbuf("c_t11", [1, 2], F32)
    for h in range(H):
        s1 = k.op("act", "activation", qsq[:, :], q_sb[:, h * TOK:(h + 1) * TOK],
                  AF.Square, deps=[k.tok("pe")])
        for j in (0, 512):
            m1 = k.op("pe", "matmul", ps_p[0][0:1, 0:512], C["ones_col_b"][:, :],
                      qsq[:, j:j + 512], start=True, stop=True,
                      deps=[s1, k.tok("dve")])
            r1 = k.op("dve", "reduce_max", t11[:, 0:1], ps_p[0][0:1, 0:512], AX.X, deps=[m1])
            if j == 0:
                k.op("dve", "tensor_copy", nm[:, h:h + 1], t11[:, 0:1])
            else:
                k.op("dve", "tensor_max", nm[:, h:h + 1], nm[:, h:h + 1], t11[:, 0:1])
    SQ.close()

    cK = k.cc("AllGather", ALU.bypass, [io["K_in"]], [io["K_out"]], deps=kin_toks)
    cV = k.cc("AllGather", ALU.bypass, [io["V_in"]], [io["V_out"]], deps=vin_toks)

    SA = k.scope()
    Kh = SA.sbuf("c_Kh", [128, 8 * TOK], BF16)
    Vh = SA.sbuf("c_Vh", [128, 64 * 128], BF16)
    ksq = SA.sbuf("c_ksq", [128, 512], BF16)
    probsT = [SA.sbuf(f"c_prt{j}", [128, TOK], BF16) for j in range(2)]
    bigm_bc = SA.sbuf("c_bigm", [128, 64], F32)
    bcol = SA.sbuf("c_bcol", [128, 64], F32)
    negb = SA.sbuf("c_negb", [128, 1], F32)
    rrow = SA.sbuf("c_rrow", [1, TOK], F32)
    rb = SA.sbuf("c_rb", [128, TOK], F32)
    bgr = SA.sbuf("c_bgr", [1, 64], F32)
    ps_sc = [SA.psum(f"c_pssc{j}", [128, TOK]) for j in range(2)]
    ps_sums = SA.psum("c_pssums", [1, TOK])
    ps_o = SA.psum("c_pso", [128, TOK])
    ps_bc = ps_sc[0]

    tbg = k.dma("sp", bgr[:, :], io["bigm"])
    mbg = k.op("pe", "matmul", ps_bc[0:128, 0:64], C["ones_row_f"][:, :], bgr[0:1, :],
               start=True, stop=True, deps=[tbg])
    k.op("dve", "tensor_copy", bigm_bc[:, :], ps_bc[:, 0:64], deps=[mbg])

    for h in range(H):
        hdep = [cK, cV, k.tok("pe"), k.tok("act"), k.tok("dve")]
        tk = [k.dma("sp", Kh[:, r * TOK:(r + 1) * TOK],
                    io["K_out"][r * D + h * 128:r * D + h * 128 + 128, :], deps=hdep)
              for r in range(NCORES)]
        tv = [k.dma("sp", Vh[:, r * 8 * 128:(r + 1) * 8 * 128],
                    io["V_out"][r * (NCORES * D) + h * TOK:
                                r * (NCORES * D) + (h + 1) * TOK, :].rearrange(
                        "(kb p) d -> p kb d", p=128), deps=hdep)
              for r in range(NCORES)]
        for j16 in range(16):
            s1 = k.op("act", "activation", ksq[:, :],
                      Kh[:, j16 * 512:(j16 + 1) * 512], AF.Square,
                      deps=tk + [k.tok("pe")])
            m1 = k.op("pe", "matmul", ps_bc[0:1, 0:512], C["ones_col_b"][:, :],
                      ksq[:, :], start=True, stop=True,
                      deps=[s1, k.tok("dve")])
            k.op("dve", "reduce_max", nm[:, 32:33], ps_bc[0:1, 0:512], AX.X, deps=[m1])
            if j16 == 0:
                k.op("dve", "tensor_copy", nm[:, 16 + h:17 + h], nm[:, 32:33])
            else:
                k.op("dve", "tensor_max", nm[:, 16 + h:17 + h], nm[:, 16 + h:17 + h],
                     nm[:, 32:33])
        k.op("dve", "tensor_mul", nm[:, 33:34], nm[:, h:h + 1], nm[:, 16 + h:17 + h])
        sq = k.op("act", "activation", nm[:, 34:35], nm[:, 33:34], AF.Sqrt,
                  scale=1.0 / 128.0, deps=[k.tok("dve")])
        mB = k.op("pe", "matmul", ps_bc[0:128, 0:1], C["ones_row_f"][:, :],
                  nm[0:1, 34:35], start=True, stop=True, deps=[sq, k.tok("dve")])
        k.op("dve", "tensor_scalar", negb[:, :], ps_bc[:, 0:1], -1.0, None, ALU.mult,
             deps=[mB])
        k.op("dve", "tensor_scalar", bcol[:, :], bigm_bc[:, :], -1.0,
             negb[:, 0:1], ALU.mult, ALU.add)
        exp_tok = [None] * 64
        sc_tok = [None] * 64
        oT_tok = [None] * 64

        def kb_src(kb):
            if kb < NKB_FULL:
                return (Kh[:, kb * 128:kb * 128 + 128],
                        Vh[:, kb * 128:kb * 128 + 128], bcol[:, kb:kb + 1], None)
            lkb = kb - NKB_FULL
            return (kf_sb[:, h * TOK + lkb * 128:h * TOK + lkb * 128 + 128],
                    vtl_sb[:, h * TOK + lkb * 128:h * TOK + lkb * 128 + 128],
                    negb[:, 0:1], C["lmask_b"][:, lkb * TOK:(lkb + 1) * TOK])

        def issue_scores(kb):
            p = kb % 2
            ksrc, _, _, _ = kb_src(kb)
            dep0 = [exp_tok[kb - 2]] if kb >= 2 else [k.tok("act"), k.tok("dve")]
            if kb == 0:
                dep0 += [tk[-1], tv[-1]]
            for j in (0, 512):
                sc_tok[kb] = k.op("pe", "matmul", ps_sc[p][:, j:j + 512], ksrc,
                     q_sb[:, h * TOK + j:h * TOK + j + 512], start=True, stop=True,
                     deps=dep0 if j == 0 else [])

        issue_scores(0)
        for kb in range(64):
            p = kb % 2
            _, vsrc, bc_, msk_ = kb_src(kb)
            dep_e = [sc_tok[kb]] + ([oT_tok[kb - 2]] if kb >= 2 else [])
            e1 = k.op("act", "activation", probsT[p][:, :], ps_sc[p][:, :], AF.Exp,
                      scale=INV_SQRT_HD, bias=bc_, deps=dep_e)
            if msk_ is not None:
                e1 = k.op("dve", "tensor_mul", probsT[p][:, :], probsT[p][:, :],
                          msk_, deps=[e1])
            exp_tok[kb] = e1
            if kb + 1 < 64:
                issue_scores(kb + 1)
            last = (kb == 63)
            st = (kb == 0)
            for j in (0, 512):
                k.op("pe", "matmul", ps_sums[0:1, j:j + 512], C["ones_col_b"][:, :],
                     probsT[p][:, j:j + 512], start=st, stop=last,
                     deps=[e1] if j == 0 else [])
                oT_tok[kb] = k.op("pe", "matmul", ps_o[:, j:j + 512], vsrc,
                     probsT[p][:, j:j + 512], start=st, stop=last)
        mm = k.tok("pe")
        r1 = k.op("dve", "reciprocal", rrow[:, :], ps_sums[0:1, :], deps=[mm])
        for j in (0, 512):
            k.op("pe", "matmul", ps_bc[:, j:j + 512], C["ones_row_f"][:, :],
                 rrow[0:1, j:j + 512], start=True, stop=True,
                 deps=[r1, k.tok("dve")] if j == 0 else [])
        e2 = k.op("dve", "tensor_copy", rb[:, :], ps_bc[:, :], deps=[k.tok("pe")])
        k.op("dve", "tensor_mul", y_sb[:, h * TOK:(h + 1) * TOK], ps_o[:, :],
             rb[:, :], deps=[e2])
    SA.close()

    S8 = k.scope()
    wo_sb = S8.sbuf("c_wo", [128, NDT * 128], BF16)
    bp_sb = S8.sbuf("c_bp", [128, 16], F32)
    ps_w = S8.psum("c_psw", [128, 512])
    tbp = k.dma("sp", bp_sb[:, :], io["bproj_c"])
    for e in range(NDT):
        two = k.dma("pool", wo_sb[:, :],
                    io["wproj"][:, e * 128:(e + 1) * 128].rearrange(
                        "(n p) f -> p n f", p=128),
                    deps=[k.tok("pe")])
        for j in (0, 512):
            for kt in range(NDT):
                k.op("pe", "matmul", ps_w[:, :], wo_sb[:, kt * 128:(kt + 1) * 128],
                     y_sb[:, kt * TOK + j:kt * TOK + j + 512],
                     start=(kt == 0), stop=(kt == NDT - 1),
                     deps=[two, k.tok("dve")] if kt == 0 else [])
            k.op("dve", "scalar_tensor_tensor",
                 x_sb[:, e * TOK + j:e * TOK + j + 512], ps_w[:, :],
                 bp_sb[:, e:e + 1], x_sb[:, e * TOK + j:e * TOK + j + 512],
                 ALU.add, ALU.add, deps=[k.tok("pe"), tbp])
    S8.close()

    emit_layernorm(k, C, x_sb, x_sb, TOK, tag="c_ln")
    ot = k.dma("sp", fmr(io["x4"]), x_sb[:, :], deps=[k.tok("dve")])
    SB.close()
    return [ot]


def declare_l1a_io(nc, kin, kout, x3_ap=None):
    io = {}
    io["x3"] = x3_ap if x3_ap is not None else nc.dram_tensor("x3", [D, TOK], BF16, kind=kin).ap()
    io["wattn"] = nc.dram_tensor("wattn", [D, DQKV], F32, kind="ExternalInput").ap()
    io["battn_c"] = nc.dram_tensor("battn_c", [128, 48], F32, kind="ExternalInput").ap()
    io["wproj"] = nc.dram_tensor("wproj", [D, D], F32, kind="ExternalInput").ap()
    io["bproj_c"] = nc.dram_tensor("bproj_c", [128, 16], F32, kind="ExternalInput").ap()
    io["bigm"] = nc.dram_tensor("bigm", [1, 64], F32, kind="ExternalInput").ap()
    io["K_in"] = nc.dram_tensor("K_in", [D, TOK], BF16, kind="Internal").ap()
    io["V_in"] = nc.dram_tensor("V_in", [NDT * TOK, 128], BF16, kind="Internal").ap()
    sp = _ccsp("AllGather")
    io["K_out"] = nc.dram_tensor("K_out", [NCORES * D, TOK], BF16, kind="Internal",
                                 addr_space=sp).ap()
    io["V_out"] = nc.dram_tensor("V_out", [NCORES * NDT * TOK, 128], BF16,
                                 kind="Internal", addr_space=sp).ap()
    io["x4"] = nc.dram_tensor("x4", [D, TOK], BF16, kind=kout).ap()
    return io


def build_l1a_program():
    nc = bass.Bass("TRN2", target_bir_lowering=False, debug=False)
    io = declare_l1a_io(nc, "ExternalInput", "ExternalOutput")
    cst = declare_constants(nc)
    with contextlib.ExitStack() as es:
        k = K(nc, es)
        with nc.Block() as block:
            S0 = k.scope()
            C = load_constants(k, S0, cst)
            outs = phase_l1_attn(k, C, io)
            k.wait("sp", outs)
            k.emit_all(block)
            S0.close()
    return nc


def stage_l1a_inputs(inputs, x3_percore):
    cst = host_constants()
    maps = []
    for c in range(NCORES):
        m = {("cst_" + n): v for n, v in cst.items()}
        if x3_percore is not None:
            m["x3"] = x3_percore[c]
        m["wattn"] = inputs["L1_attn_W_attn"]
        m["battn_c"] = np.ascontiguousarray(inputs["L1_attn_b_attn"].reshape(48, 128).T)
        m["wproj"] = inputs["L1_attn_W_proj"]
        m["bproj_c"] = np.ascontiguousarray(inputs["L1_attn_b_proj"].reshape(16, 128).T)
        bigm = np.zeros((1, 64), dtype=np.float32)
        bigm[0, c * 8:] = 1e30
        m["bigm"] = bigm
        maps.append(m)
    return maps



def phase_l1_ffn(k, C, io):
    nc = k.nc
    SB = k.scope()
    x_sb = SB.sbuf("d_x", [128, NDT * TOK], BF16)
    cmb_t = SB.sbuf("d_cmb", [128, NE * TOK], BF16)
    t1s = SB.sbuf("d_t1s", [16, NE * TOK], BF16)
    t2s = SB.sbuf("d_t2s", [16, NE * TOK], BF16)
    csr = SB.sbuf("d_csr", [1, TOK], F32)
    a1t_sb = SB.sbuf("d_a1t", [128, NDT * 128], BF16)
    a2t_sb = SB.sbuf("d_a2t", [128, NFC * 128], BF16)
    b2t_sb = SB.sbuf("d_b2t", [16, NE * D], BF16)
    b1c_sb = SB.sbuf("d_b1c", [128, NFC], F32)
    b2r_sb = SB.sbuf("d_b2r", [1, D], F32)
    tx = k.dma("sp", x_sb[:, :], fmr(io["x4"]))
    ta1 = k.dma("pool", a1t_sb[:, :], io["a1t"].rearrange("(n p) c -> p n c", p=128))
    ta2 = k.dma("pool", a2t_sb[:, :], io["a2t"].rearrange("(n p) c -> p n c", p=128))
    tb2 = k.dma("pool", b2t_sb[:, :], io["b2t"])
    tb1c = k.dma("sp", b1c_sb[:, :], io["b1c"])
    tb2r = k.dma("sp", b2r_sb[:, :], io["b2r"])

    SR = k.scope()
    wr_sb = SR.sbuf("d_wr", [128, NDT * 8], BF16)
    brb = SR.sbuf("d_brb", [128, 8], F32)
    lg = SR.sbuf("d_lg", [128, 8], F32)
    prb = SR.sbuf("d_prb", [128, 8], F32)
    p2 = SR.sbuf("d_p2", [128, 8], F32)
    eqm = SR.sbuf("d_eqm", [128, 8], F32)
    eqi = SR.sbuf("d_eqi", [128, 8], I32)
    mrow = SR.sbuf("d_mrow", [128, 4], F32)
    cmb_tm = SR.sbuf("d_cmbtm", [128, 8], F32)
    cs_tm = SR.sbuf("d_cstm", [128, 1], F32)
    cmbT_sb = SR.sbuf("d_cmbr", [8, TOK], BF16)
    ps_r = SR.psum("d_psr", [128, 8])
    ps_t = SR.psum("d_pst", [128, 128])
    ps_cb = SR.psum("d_pscb", [128, TOK])
    twr = k.dma("pool", wr_sb[:, :], io["wr"].rearrange("(n p) e -> p n e", p=128))
    tbr = k.dma("sp", brb[0:1, :], io["br"])
    mbr = k.op("pe", "matmul", ps_t[0:128, 0:8], C["ones_row_f"][:, :], brb[0:1, :],
               start=True, stop=True, deps=[tbr])
    k.op("dve", "tensor_copy", brb[:, :], ps_t[:, 0:8], deps=[mbr])
    for tc in range(8):
        for kt in range(NDT):
            k.op("pe", "matmul", ps_r[:, :],
                 x_sb[:, kt * TOK + tc * 128:kt * TOK + tc * 128 + 128],
                 wr_sb[:, kt * 8:(kt + 1) * 8], start=(kt == 0), stop=(kt == NDT - 1),
                 deps=[twr, tx, k.tok("dve")] if kt == 0 else [])
        k.op("dve", "tensor_add", lg[:, :], ps_r[:, :], brb[:, :], deps=[k.tok("pe")])
        k.op("dve", "reduce_max", mrow[:, 0:1], lg[:, :], AX.X)
        k.op("dve", "tensor_scalar", mrow[:, 1:2], mrow[:, 0:1], -1.0, None, ALU.mult)
        k.op("act", "activation", prb[:, :], lg[:, :], AF.Exp, bias=mrow[:, 1:2],
             accum_out=mrow[:, 2:3], deps=[k.tok("dve")])
        k.op("dve", "reciprocal", mrow[:, 3:4], mrow[:, 2:3], deps=[k.tok("act")])
        k.op("dve", "tensor_scalar", prb[:, :], prb[:, :], mrow[:, 3:4], None, ALU.mult)
        k.op("dve", "reduce_max", mrow[:, 0:1], prb[:, :], AX.X)
        k.op("dve", "tensor_scalar", eqi[:, :], prb[:, :], mrow[:, 0:1], None,
             ALU.is_ge)
        k.op("dve", "tensor_copy", p2[:, :], prb[:, :])
        k.op("dve", "copy_predicated", p2[:, :], eqi[:, :], C["neg1_8"][:, :])
        k.op("dve", "reduce_max", mrow[:, 1:2], p2[:, :], AX.X)
        k.op("dve", "tensor_scalar", eqm[:, :], prb[:, :], mrow[:, 1:2], None,
             ALU.is_ge)
        k.op("dve", "tensor_mul", cmb_tm[:, :], prb[:, :], eqm[:, :])
        k.op("dve", "tensor_add", cs_tm[:, :], mrow[:, 0:1], mrow[:, 1:2])
        tr1 = k.op("pe", "transpose", ps_t[0:8, 0:128], cmb_tm[:, :],
                   C["ident_f"][:, :], deps=[k.tok("dve")])
        k.op("dve", "tensor_copy", cmbT_sb[:, tc * 128:(tc + 1) * 128],
             ps_t[0:8, 0:128], deps=[tr1])
        tr2 = k.op("pe", "transpose", ps_t[0:1, 0:128], cs_tm[:, :],
                   C["ident_f"][:, :], deps=[k.tok("dve")])
        k.op("dve", "tensor_copy", csr[:, tc * 128:(tc + 1) * 128], ps_t[0:1, 0:128],
             deps=[tr2])
    for e in range(NE):
        for j in (0, 512):
            k.op("pe", "matmul", ps_cb[:, j:j + 512],
                 C["sel8_b"][:, e * 128:(e + 1) * 128],
                 cmbT_sb[0:8, j:j + 512], start=True, stop=True,
                 deps=[k.tok("dve")] if j == 0 else [])
        k.op("dve", "tensor_copy", cmb_t[:, e * TOK:(e + 1) * TOK], ps_cb[:, :],
             deps=[k.tok("pe")])
    SR.close()

    ST = k.scope()
    ps_t1 = ST.psum("d_pst1", [128, TOK])
    for i in range(4):
        e0, e1 = 2 * i, 2 * i + 1
        for dt in range(NDT):
            st = (dt == 0)
            sp_ = (dt == NDT - 1)
            for j in (0, 512):
                dep = [ta1, tx, k.tok("dve")] if (dt == 0 and j == 0) else []
                k.op("pe", "matmul", ps_t1[0:16, j:j + 512],
                     a1t_sb[:, dt * 128 + e0 * 16:dt * 128 + e0 * 16 + 16],
                     x_sb[:, dt * TOK + j:dt * TOK + j + 512],
                     start=st, stop=sp_, tile_position=(0, 0), deps=dep)
                k.op("pe", "matmul", ps_t1[64:80, j:j + 512],
                     a1t_sb[:, dt * 128 + e1 * 16:dt * 128 + e1 * 16 + 16],
                     x_sb[:, dt * TOK + j:dt * TOK + j + 512],
                     start=st, stop=sp_, tile_position=(0, 64))
        mm = k.tok("pe")
        k.op("dve", "tensor_scalar", t1s[:, e0 * TOK:(e0 + 1) * TOK],
             ps_t1[0:16, :], 2.0, None, ALU.mult, deps=[mm])
        k.op("dve", "tensor_scalar", t1s[:, e1 * TOK:(e1 + 1) * TOK],
             ps_t1[64:80, :], 2.0, None, ALU.mult)
    ST.close()

    SM = k.scope()
    NBW = 2
    w1c = [SM.sbuf(f"d_w1c{j}", [128, NDT * 128], BF16) for j in range(NBW)]
    b1tc = [SM.sbuf(f"d_b1tc{j}", [16, NE * 128], BF16) for j in range(NBW)]
    base_sb = SM.sbuf("d_base", [128, 512], BF16)
    a_sb = [SM.sbuf(f"d_a{j}", [128, 512], BF16) for j in range(2)]
    ca_sb = SM.sbuf("d_ca", [128, 512], BF16)
    macc = SM.sbuf("d_macc", [128, 512], BF16)
    ps_base = SM.psum("d_psbase", [128, 512])
    ps_e = [SM.psum(f"d_pse{j}", [128, 512]) for j in range(2)]
    t2ps = [SM.psum(f"d_t2ps{i}", [128, 512]) for i in range(4)]
    b1t_r = io["b1t"].rearrange("p (e f) -> p e f", f=FF)

    last_mix = [None]
    for hh in range(2):
        t0 = hh * 512
        wtok = [None] * NFC
        mixtok = [None] * NFC

        def load_fc(fc, hh=hh, wtok=wtok, mixtok=mixtok):
            b = fc % NBW
            deps = ([mixtok[fc - NBW]] if fc >= NBW else []) + [k.tok("pe")]
            t1_ = k.dma("pool", w1c[b][:, :],
                        io["w1"][:, fc * 128:(fc + 1) * 128].rearrange(
                            "(n p) f -> p n f", p=128), deps=deps)
            t2_ = k.dma("pool", b1tc[b][:, :], b1t_r[:, :, fc * 128:(fc + 1) * 128],
                        deps=deps)
            wtok[fc] = (t1_, t2_)

        for fc in range(NBW):
            load_fc(fc)
        for fc in range(NFC):
            b = fc % NBW
            for kt in range(NDT):
                k.op("pe", "matmul", ps_base[:, :], w1c[b][:, kt * 128:(kt + 1) * 128],
                     x_sb[:, kt * TOK + t0:kt * TOK + t0 + 512],
                     start=(kt == 0), stop=(kt == NDT - 1),
                     deps=[wtok[fc][0], k.tok("act"), k.tok("dve")] if kt == 0 else [])
            bs = k.op("act", "activation", base_sb[:, :], ps_base[:, :], AF.Copy,
                      deps=[k.tok("pe")])
            ge_tok = [None] * NE
            ca_tok = [None] * NE
            pre_tok = [None] * NE
            t2_tok = [None] * NE

            def issue_pre(e):
                pp = ps_e[e % 2]
                dep = [ge_tok[e - 2]] if e >= 2 else [k.tok("act"), k.tok("dve")]
                if e == 0:
                    dep += [wtok[fc][1], bs]
                k.op("pe", "matmul", pp[:, :], b1tc[b][:, e * 128:(e + 1) * 128],
                     t1s[0:16, e * TOK + t0:e * TOK + t0 + 512],
                     start=True, stop=False, deps=dep)
                pre_tok[e] = k.op("pe", "matmul", pp[:, :], C["ident_b"][:, :],
                                  base_sb[:, :], start=False, stop=True,
                                  deps=[bs] if e < 2 else [])

            issue_pre(0)
            for e in range(NE):
                i, ro = e // 2, 64 * (e % 2)
                dep_g = [pre_tok[e], tb1c]
                if e >= 2:
                    dep_g += [ca_tok[e - 2], t2_tok[e - 2]]
                ge = k.op("act", "activation", a_sb[e % 2][:, :], ps_e[e % 2][:, :],
                          AF.Gelu_apprx_tanh, bias=b1c_sb[:, fc:fc + 1], deps=dep_g)
                ge_tok[e] = ge
                if e + 1 < NE:
                    issue_pre(e + 1)
                t2_tok[e] = k.op("pe", "matmul", t2ps[i][ro:ro + 16, :],
                     a2t_sb[:, fc * 128 + e * 16:fc * 128 + e * 16 + 16],
                     a_sb[e % 2][:, :], start=(fc == 0), stop=(fc == NFC - 1),
                     tile_position=(0, ro), skip_group_check=True,
                     deps=[ge, ta2])
                ca_tok[e] = k.op("dve", "tensor_mul", ca_sb[:, :], a_sb[e % 2][:, :],
                     cmb_t[:, e * TOK + t0:e * TOK + t0 + 512], deps=[ge])
                if e == 0:
                    k.op("dve", "tensor_copy", macc[:, :], ca_sb[:, :],
                         deps=[mixtok[fc - 1] if fc > 0 else last_mix[0]])
                else:
                    k.op("dve", "tensor_add", macc[:, :], macc[:, :], ca_sb[:, :])
            mixtok[fc] = k.dma("sp", io["mix_d"][fc * 128:(fc + 1) * 128,
                                                 t0:t0 + 512], macc[:, :],
                               deps=[k.tok("dve")])
            if fc + NBW < NFC:
                load_fc(fc + NBW)
        last_mix[0] = mixtok[NFC - 1]
        for e in range(NE):
            i, ro = e // 2, 64 * (e % 2)
            k.op("dve", "scalar_tensor_tensor",
                 t2s[0:16, e * TOK + t0:e * TOK + t0 + 512],
                 t2ps[i][ro:ro + 16, :], 2.0,
                 cmb_t[0:16, e * TOK + t0:e * TOK + t0 + 512],
                 ALU.mult, ALU.mult, deps=[k.tok("pe")])
    SM.close()

    SW = k.scope()
    NBW2 = 2
    NFG = NFC // 4
    mixg = [SW.sbuf(f"d_mixg{j}", [128, 4 * TOK], BF16) for j in range(NBW2)]
    w2g = [SW.sbuf(f"d_w2g{j}", [128, 4 * 512], BF16) for j in range(NBW2)]
    ps_w = [SW.psum(f"d_psw{j}", [128, TOK]) for j in range(4)]
    for dcg in range(4):
        mtok = [None] * NFG
        wtok2 = [None] * NFG

        def load_w2(g, dcg=dcg, mtok=mtok, wtok2=wtok2):
            b = g % NBW2
            deps = [k.tok("pe")] if g >= NBW2 else [k.tok("dve"), k.tok("act")]
            mtok[g] = k.dma("sp", mixg[b][:, :],
                            io["mix_d"][g * 512:(g + 1) * 512, :].rearrange(
                                "(n p) t -> p n t", p=128), deps=deps)
            wtok2[g] = k.dma("pool", w2g[b][:, :],
                             io["w2"][g * 512:(g + 1) * 512,
                                      dcg * 512:(dcg + 1) * 512].rearrange(
                                 "(n p) c -> p n c", p=128), deps=deps)

        for g in range(NBW2):
            load_w2(g)
        for g in range(NFG):
            b = g % NBW2
            for fi in range(4):
                fc = g * 4 + fi
                for dj in range(4):
                    for j in (0, 512):
                        k.op("pe", "matmul", ps_w[dj][:, j:j + 512],
                             w2g[b][:, fi * 512 + dj * 128:fi * 512 + dj * 128 + 128],
                             mixg[b][:, fi * TOK + j:fi * TOK + j + 512],
                             start=(fc == 0), stop=False,
                             deps=[mtok[g], wtok2[g]]
                             if (fi == 0 and dj == 0 and j == 0) else [])
            if g + NBW2 < NFG:
                load_w2(g + NBW2)
        for dj in range(4):
            dc = dcg * 4 + dj
            for e in range(NE):
                for j in (0, 512):
                    k.op("pe", "matmul", ps_w[dj][:, j:j + 512],
                         b2t_sb[:, e * D + dc * 128:e * D + dc * 128 + 128],
                         t2s[0:16, e * TOK + j:e * TOK + j + 512],
                         start=False, stop=False, deps=[tb2, k.tok("dve")] if (e == 0 and j == 0) else [])
            for j in (0, 512):
                k.op("pe", "matmul", ps_w[dj][:, j:j + 512],
                     b2r_sb[0:1, dc * 128:dc * 128 + 128], csr[0:1, j:j + 512],
                     start=False, stop=True, deps=[tb2r] if j == 0 else [])
            for j in (0, 512):
                k.op("dve", "scalar_tensor_tensor",
                     x_sb[:, dc * TOK + j:dc * TOK + j + 512], ps_w[dj][:, j:j + 512],
                     1.0, x_sb[:, dc * TOK + j:dc * TOK + j + 512],
                     ALU.mult, ALU.add, deps=[k.tok("pe")])
    SW.close()

    stoks = emit_layernorm(k, C, x_sb, None, TOK, tag="d_ln", dram_out=io["x5"])
    SB.close()
    return stoks





def _ccsp(kind):
    from concourse.replica_groups import maybe_share_collective_output_space
    return maybe_share_collective_output_space(kind, RG, prefer_local=True)


def declare_ffn0_io(nc, kin, kout, x2_ap=None):
    io = {}
    io["x2"] = x2_ap if x2_ap is not None else nc.dram_tensor("x2", [D, TOK], BF16, kind=kin).ap()
    io["lut"] = nc.dram_tensor("lut", [LUTN], F32, kind="ExternalInput").ap()
    io["gidxT_s"] = nc.dram_tensor("gidxT_s", [GHR, INTER], I32, kind="ExternalInput").ap()
    io["uidxT_s"] = nc.dram_tensor("uidxT_s", [GHR, INTER], I32, kind="ExternalInput").ap()
    io["didxT_s"] = nc.dram_tensor("didxT_s", [DHR, D], I32, kind="ExternalInput").ap()
    io["g_in"] = nc.dram_tensor("g_in", [GHR, INTER], BF16, kind="Internal").ap()
    io["u_in"] = nc.dram_tensor("u_in", [GHR, INTER], BF16, kind="Internal").ap()
    io["d_in"] = nc.dram_tensor("d_in", [DHR, D], BF16, kind="Internal").ap()
    sp = _ccsp("AllGather")
    io["g_out"] = nc.dram_tensor("g_out", [D, INTER], BF16, kind="Internal", addr_space=sp).ap()
    io["u_out"] = nc.dram_tensor("u_out", [D, INTER], BF16, kind="Internal", addr_space=sp).ap()
    io["d_out"] = nc.dram_tensor("d_out", [INTER, D], BF16, kind="Internal", addr_space=sp).ap()
    io["x3"] = nc.dram_tensor("x3", [D, TOK], BF16, kind=kout).ap()
    return io


def build_ffn0_program():
    nc = bass.Bass("TRN2", target_bir_lowering=False, debug=False)
    io = declare_ffn0_io(nc, "ExternalInput", "ExternalOutput")
    cst = declare_constants(nc)
    with contextlib.ExitStack() as es:
        k = K(nc, es)
        with nc.Block() as block:
            S0 = k.scope()
            C = load_constants(k, S0, cst)
            SGb = phase_ffn0_build(k, C, io)
            outs = phase_ffn0(k, C, io)
            SGb.close()
            k.wait("sp", outs)
            k.emit_all(block)
            S0.close()
    return nc







def declare_l1f_io(nc, kin, kout, x4_ap=None):
    io = {}
    io["x4"] = x4_ap if x4_ap is not None else nc.dram_tensor("x4", [D, TOK], BF16, kind=kin).ap()
    io["wr"] = nc.dram_tensor("wr", [D, NE], F32, kind="ExternalInput").ap()
    io["br"] = nc.dram_tensor("br", [1, NE], F32, kind="ExternalInput").ap()
    io["w1"] = nc.dram_tensor("w1", [D, FF], F32, kind="ExternalInput").ap()
    io["b1c"] = nc.dram_tensor("b1c", [128, NFC], F32, kind="ExternalInput").ap()
    io["w2"] = nc.dram_tensor("w2", [FF, D], F32, kind="ExternalInput").ap()
    io["b2r"] = nc.dram_tensor("b2r", [1, D], F32, kind="ExternalInput").ap()
    io["a1t"] = nc.dram_tensor("a1t", [D, 128], F32, kind="ExternalInput").ap()
    io["a2t"] = nc.dram_tensor("a2t", [FF, 128], F32, kind="ExternalInput").ap()
    io["b1t"] = nc.dram_tensor("b1t", [16, NE * FF], F32, kind="ExternalInput").ap()
    io["b2t"] = nc.dram_tensor("b2t", [16, NE * D], F32, kind="ExternalInput").ap()
    io["mix_d"] = nc.dram_tensor("mix_d", [FF, TOK], BF16, kind="Internal").ap()
    io["x5"] = nc.dram_tensor("x5", [D, TOK], F32, kind=kout).ap()
    return io


def build_l1f_program():
    nc = bass.Bass("TRN2", target_bir_lowering=False, debug=False)
    io = declare_l1f_io(nc, "ExternalInput", "ExternalOutput")
    cst = declare_constants(nc)
    with contextlib.ExitStack() as es:
        k = K(nc, es)
        with nc.Block() as block:
            S0 = k.scope()
            C = load_constants(k, S0, cst)
            outs = phase_l1_ffn(k, C, io)
            k.wait("sp", outs)
            k.emit_all(block)
            S0.close()
    return nc


def stage_l1f_inputs(inputs, x4_percore):
    cst = host_constants()
    A1 = inputs["L1_ffn_A1"]; B1 = inputs["L1_ffn_B1"]
    A2 = inputs["L1_ffn_A2"]; B2 = inputs["L1_ffn_B2"]
    maps = []
    for c in range(NCORES):
        m = {("cst_" + n): v for n, v in cst.items()}
        if x4_percore is not None:
            m["x4"] = x4_percore[c]
        m["wr"] = inputs["L1_ffn_Wr"]
        m["br"] = inputs["L1_ffn_br"][None, :]
        m["w1"] = inputs["L1_ffn_W1"]
        m["b1c"] = np.ascontiguousarray(inputs["L1_ffn_b1"].reshape(NFC, 128).T)
        m["w2"] = inputs["L1_ffn_W2"]
        m["b2r"] = inputs["L1_ffn_b2"][None, :]
        m["a1t"] = np.ascontiguousarray(A1.transpose(2, 0, 1).reshape(D, 128))
        m["a2t"] = np.ascontiguousarray(A2.transpose(2, 0, 1).reshape(FF, 128))
        m["b1t"] = np.ascontiguousarray(B1.transpose(2, 0, 1).reshape(16, NE * FF))
        m["b2t"] = np.ascontiguousarray(B2.transpose(2, 0, 1).reshape(16, NE * D))
        maps.append(m)
    return maps


def build_ab_program():
    nc = bass.Bass("TRN2", target_bir_lowering=False, debug=False)
    ioa = declare_l0_io(nc, "ExternalInput", "Internal")
    iob = declare_ffn0_io(nc, "Internal", "ExternalOutput", x2_ap=ioa["x2"])
    cst = declare_constants(nc)
    with contextlib.ExitStack() as es:
        k = K(nc, es)
        with nc.Block() as block:
            S0 = k.scope()
            C = load_constants(k, S0, cst)
            outs_a = phase_l0_attn(k, C, ioa)
            k.fence = [(n, s[1]) for n, s in k.sems.items()]
            outs_b = phase_ffn0(k, C, iob)
            k.wait("sp", outs_a + outs_b)
            k.emit_all(block)
            S0.close()
    return nc


def stage_ffn0_inputs(inputs, x2_percore):
    cst = host_constants()
    gT = np.ascontiguousarray(inputs["L0_ffn_gate_idx"].T)
    uT = np.ascontiguousarray(inputs["L0_ffn_up_idx"].T)
    dT = np.ascontiguousarray(inputs["L0_ffn_down_idx"].T)
    maps = []
    for c in range(NCORES):
        m = {("cst_" + n): v for n, v in cst.items()}
        m["x2"] = x2_percore[c]
        m["lut"] = inputs["L0_ffn_lut"]
        m["gidxT_s"] = np.ascontiguousarray(gT[c * GHR:(c + 1) * GHR, :])
        m["uidxT_s"] = np.ascontiguousarray(uT[c * GHR:(c + 1) * GHR, :])
        m["didxT_s"] = np.ascontiguousarray(dT[c * DHR:(c + 1) * DHR, :])
        maps.append(m)
    return maps


def kernel(**inputs):
    raise NotImplementedError("under construction")


FUSED = True


def build_full_program(expose=False):
    nc = bass.Bass("TRN2", target_bir_lowering=False, debug=False)
    kh = "ExternalOutput" if expose else "Internal"
    ioa = declare_l0_io(nc, "ExternalInput", kh)
    iob = declare_ffn0_io(nc, kh, kh, x2_ap=ioa["x2"])
    ioc = declare_l1a_io(nc, kh, kh, x3_ap=iob["x3"])
    iod = declare_l1f_io(nc, kh, "ExternalOutput", x4_ap=ioc["x4"])
    cst = declare_constants(nc)
    with contextlib.ExitStack() as es:
        k = K(nc, es)
        with nc.Block() as block:
            S0 = k.scope()
            C = load_constants(k, S0, cst)
            SGb = phase_ffn0_build(k, C, iob)
            phase_l0_attn(k, C, ioa)
            k.fence = [(n, s[1]) for n, s in k.sems.items()]
            phase_ffn0(k, C, iob)
            SGb.close()
            k.fence = [(n, s[1]) for n, s in k.sems.items()]
            phase_l1_attn(k, C, ioc)
            k.fence = [(n, s[1]) for n, s in k.sems.items()]
            outs = phase_l1_ffn(k, C, iod)
            k.wait("sp", outs)
            k.emit_all(block)
            S0.close()
    return nc


def stage_all_inputs(inputs):
    ma = stage_l0_inputs(inputs)
    mb_ = stage_ffn0_inputs(inputs, [None] * NCORES)
    mc = stage_l1a_inputs(inputs, None)
    md = stage_l1f_inputs(inputs, None)
    maps = []
    for c in range(NCORES):
        m = dict(ma[c])
        for mm in (mb_[c], mc[c], md[c]):
            mm = dict(mm)
            mm.pop("x2", None)
            m.update(mm)
        maps.append(m)
    return maps


_PROG_CACHE = {}


def kernel(**inputs):
    import ml_dtypes
    if FUSED:
        if "full" not in _PROG_CACHE:
            _PROG_CACHE["full"] = build_full_program()
        nc = _PROG_CACHE["full"]
        maps = stage_all_inputs(inputs)
        res = run_bass_kernel_spmd(nc, maps, core_ids=list(range(NCORES)))
        xs = [np.asarray(res.results[c]["x5"]).astype(np.float32).T for c in range(NCORES)]
    else:
        if "a" not in _PROG_CACHE:
            _PROG_CACHE["a"] = build_l0_program()
            _PROG_CACHE["b"] = build_ffn0_program()
            _PROG_CACHE["c"] = build_l1a_program()
            _PROG_CACHE["d"] = build_l1f_program()
        maps = stage_l0_inputs(inputs)
        res = run_bass_kernel_spmd(_PROG_CACHE["a"], maps, core_ids=list(range(NCORES)))
        x2 = [np.asarray(res.results[c]["x2"]) for c in range(NCORES)]
        maps = stage_ffn0_inputs(inputs, x2)
        res = run_bass_kernel_spmd(_PROG_CACHE["b"], maps, core_ids=list(range(NCORES)))
        x3 = [np.asarray(res.results[c]["x3"]) for c in range(NCORES)]
        maps = stage_l1a_inputs(inputs, x3)
        res = run_bass_kernel_spmd(_PROG_CACHE["c"], maps, core_ids=list(range(NCORES)))
        x4 = [np.asarray(res.results[c]["x4"]) for c in range(NCORES)]
        maps = stage_l1f_inputs(inputs, x4)
        res = run_bass_kernel_spmd(_PROG_CACHE["d"], maps, core_ids=list(range(NCORES)))
        xs = [np.asarray(res.results[c]["x5"]).astype(np.float32).T for c in range(NCORES)]
    out = np.concatenate(xs, axis=0)[None, :, :]
    return np.ascontiguousarray(out)
